# version 15
# baseline (speedup 1.0000x reference)
import os

import numpy as np
import concourse.bass as bass
import concourse.mybir as mybir
from concourse.bass_utils import run_bass_kernel_spmd

F32 = mybir.dt.float32
BF16 = mybir.dt.bfloat16
U32 = mybir.dt.uint32
AF = mybir.ActivationFunctionType
AL = mybir.AluOpType
AX = mybir.AxisListType

DTSZ = {mybir.dt.float32: 4, mybir.dt.bfloat16: 2, mybir.dt.uint32: 4,
        mybir.dt.int32: 4, mybir.dt.uint8: 1, mybir.dt.float16: 2}
PAGES_SBUF = (224 * 1024) // 256
NPAGES = PAGES_SBUF + 2048

T = 2048
NT = T // 128
D = 1024
H = 4
DH = 256
CO = 128
CI = 32
NCO = T // CO
E = 8
KL = 64
FW = 128
HOP = 64
NB = T // HOP

DMA = 16



def _fft_mats():
    t = np.arange(FW)
    k_re = np.arange(65)
    k_im = np.arange(1, 64)
    Fre = np.cos(2 * np.pi * np.outer(k_re, t) / FW)
    Fim = -np.sin(2 * np.pi * np.outer(k_im, t) / FW)
    Fp = np.concatenate([Fre, Fim], 0)
    tt = np.arange(HOP, FW)
    c = np.where((k_re == 0) | (k_re == 64), 1.0, 2.0) / FW
    Gre = c[None, :] * np.cos(2 * np.pi * np.outer(tt, k_re) / FW)
    Gim = (-2.0 / FW) * np.sin(2 * np.pi * np.outer(tt, k_im) / FW)
    G1 = np.concatenate([Gre, Gim], 1)
    G2 = np.concatenate([np.zeros((HOP, 1)), Gim, np.zeros((HOP, 1)), -Gre[:, 1:64]], 1)
    FA = np.concatenate([Fre, Fre[1:64]], 0)
    FB = np.concatenate([np.zeros((1, FW)), Fim, np.zeros((1, FW)), Fim], 0)
    return (np.ascontiguousarray(Fp.T, np.float32),
            np.ascontiguousarray(G1.T, np.float32),
            np.ascontiguousarray(G2.T, np.float32),
            np.ascontiguousarray(FA.T, np.float32),
            np.ascontiguousarray(FB.T, np.float32))


def _host_consts():
    c = {}
    c["ident"] = np.eye(128, dtype=np.float32)
    p = np.arange(128)
    f = np.arange(128)
    c["m_stril"] = (p[:, None] > f[None, :]).astype(np.float32)
    c["m_triu"] = (p[:, None] <= f[None, :]).astype(np.float32)
    fpT, g1T, g2T, faT, fbT = _fft_mats()
    c["fft_fT"] = fpT
    c["fft_g1T"] = g1T
    c["fft_g2T"] = g2T
    c["fft_faT"] = faT
    c["fft_fbT"] = fbT
    c["iota_row"] = np.broadcast_to(np.arange(D, dtype=np.float32), (128, D)).copy()
    sr = np.zeros((16, 16 * 128), np.float32)
    for r in range(16):
        sr[r, r * 128:(r + 1) * 128] = 1.0
    c["selrows"] = sr
    return c


def _w1s4(w1s):
    out = np.zeros((128, D), np.float32)
    for h in range(4):
        for r in range(16):
            kind, src = divmod(r, 4)
            out[32 * h + r] = w1s[:, src * 4 + kind]
    return np.ascontiguousarray(out)


def host_prep(inputs):
    g = lambda k: np.asarray(inputs[k])
    x = np.ascontiguousarray(g("L0_attn_hidden_states"), np.float32)
    tr = lambda a: np.ascontiguousarray(np.asarray(a, np.float32).T)
    shared = {
        "WqT": tr(g("L0_attn_q_proj_w")),
        "WkT": tr(g("L0_attn_k_proj_w")),
        "WvT": tr(g("L0_attn_v_proj_w")),
        "bprojT": tr(g("L0_attn_b_proj_w")),
        "qcw": np.ascontiguousarray(g("L0_attn_q_conv_w"), np.float32),
        "kcw": np.ascontiguousarray(g("L0_attn_k_conv_w"), np.float32),
        "vcw": np.ascontiguousarray(g("L0_attn_v_conv_w"), np.float32),
        "fir_s": np.ascontiguousarray(
            g("L0_attn_fir_short_w").reshape(D, 5), np.float32),
        "fir_lT": np.ascontiguousarray(
            g("L0_attn_fir_long_w").reshape(D, KL)[:, ::-1].T, np.float32),
        "fir_l": np.ascontiguousarray(g("L0_attn_fir_long_w").reshape(D, KL), np.float32),
        "oprojT": tr(g("L0_attn_o_proj_w")),
        "W1xT": np.ascontiguousarray(np.asarray(g("L0_attn_fuse_w1"), np.float32)[:, :D].T),
        "W1sT4": _w1s4(np.asarray(g("L0_attn_fuse_w1"), np.float32)[:, D:]),
        "b1": np.ascontiguousarray(np.asarray(g("L0_attn_fuse_b1"), np.float32).reshape(8, 128).T),
        "b2row": np.ascontiguousarray(g("L0_attn_fuse_b2").reshape(1, 4), np.float32),
        "W2T": tr(g("L0_attn_fuse_w2")),
        "ltemp": np.broadcast_to(np.asarray(g("L0_attn_logit_temp"), np.float32).reshape(1, 1), (128, 1)).copy(),
        "ress": np.broadcast_to(np.asarray(g("L0_attn_res_scale_short"), np.float32).reshape(1, 1), (128, 1)).copy(),
        "resl": np.broadcast_to(np.asarray(g("L0_attn_res_scale_long"), np.float32).reshape(1, 1), (128, 1)).copy(),
        "onorm": np.ascontiguousarray(
            np.tile(np.asarray(g("L0_attn_o_norm_w"), np.float32), H).reshape(8, 128).T),
        "Wg": np.ascontiguousarray(g("L0_ffn_Wg"), np.float32),
        "bgrow": np.ascontiguousarray(g("L0_ffn_bg").reshape(1, D), np.float32),
        "We": np.ascontiguousarray(g("L0_ffn_We"), np.float32),
        "be": np.ascontiguousarray(g("L0_ffn_be"), np.float32),
    }
    shared.update(_host_consts())
    in_maps = []
    for c_ in range(8):
        b, tau = c_ // 2, c_ % 2
        m = dict(shared)
        m["x_own"] = np.ascontiguousarray(x[b, tau * T:(tau + 1) * T])
        m["x_pre"] = np.ascontiguousarray(x[b, 0:T])
        m["gate"] = np.full((128, 1), float(tau), np.float32)
        in_maps.append(m)
    return in_maps



class Prog:
    def __init__(self, debug_names=()):
        self.nc = bass.Bass("TRN2", target_bir_lowering=False, detect_race_conditions=bool(os.environ.get("RACECHK")))
        nc = self.nc
        self.debug_names = list(debug_names)
        self.dbg_out = {}
        self.sems = {}
        self.semv = {}
        self.sbuf_off = {}
        self.din = {}
        for name, shape in [
            ("x_own", (T, D)), ("x_pre", (T, D)), ("gate", (128, 1)),
            ("WqT", (D, D)), ("WkT", (D, D)), ("WvT", (D, D)), ("bprojT", (D, 4)),
            ("qcw", (D, 4)), ("kcw", (D, 4)), ("vcw", (D, 4)),
            ("fir_s", (D, 5)), ("fir_lT", (KL, D)), ("fir_l", (D, KL)),
            ("oprojT", (D, D)), ("W1xT", (D, D)), ("W1sT4", (128, D)),
            ("b1", (128, 8)), ("b2row", (1, 4)), ("W2T", (D, 4)),
            ("ltemp", (128, 1)), ("ress", (128, 1)), ("resl", (128, 1)),
            ("onorm", (128, 8)),
            ("Wg", (D, D)), ("bgrow", (1, D)), ("We", (E, D, D)), ("be", (E, D)),
            ("ident", (128, 128)), ("m_stril", (128, 128)), ("m_triu", (128, 128)),
            ("fft_fT", (128, 128)), ("fft_g1T", (128, 64)), ("fft_g2T", (128, 64)),
            ("fft_faT", (128, 128)), ("fft_fbT", (128, 128)),
            ("iota_row", (128, D)), ("selrows", (16, 16 * 128)),
        ]:
            self.din[name] = nc.dram_tensor(name, list(shape), F32, kind="ExternalInput").ap()
        self.y = nc.dram_tensor("y", [T, D], F32, kind="ExternalOutput").ap()
        self.d_xch = nc.dram_tensor("d_xch", [128, 8 * T], BF16, kind="Internal").ap()
        self.d_x1 = nc.dram_tensor("d_x1", [T, D], F32, kind="Internal").ap()
        self.st = {}
        self.wait_hw = {}
        self.sb_base = {}
        self._pseudo = {}
        self._pseudo_next = PAGES_SBUF
        self._wrap_engine(nc.vector, "dv")
        self._wrap_engine(nc.scalar, "ac")
        self._wrap_engine(nc.gpsimd, "pl")

    _WRAP_OPS = ("tensor_copy", "tensor_tensor", "tensor_add", "tensor_sub",
                 "tensor_mul", "tensor_max", "tensor_scalar", "tensor_single_scalar",
                 "tensor_scalar_add", "tensor_scalar_sub", "tensor_scalar_mul",
                 "tensor_scalar_max", "tensor_scalar_min", "tensor_relu",
                 "tensor_reduce", "reduce_max", "reduce_sum", "scalar_tensor_tensor",
                 "tensor_tensor_scan", "memset", "memzero", "reciprocal", "select",
                 "copy_predicated", "transpose", "max", "max_index", "pool",
                 "pool_avg", "pool_max", "bn_stats", "bn_aggr", "activation",
                 "copy", "add", "mul", "square", "sqrt", "sign", "affine_select",
                 "iota")

    def _ap_pages(self, x):
        try:
            pat = x.ap
            off = x.offset
            sp = str(x.space)
            name = x.name
        except (AttributeError, TypeError):
            return None
        if "SBUF" not in sp and "PSUM" not in sp:
            return None
        esz = DTSZ.get(x.dtype, 4)
        ext = 0
        neg = False
        for stq, n in pat[1:]:
            if stq < 0:
                neg = True
            ext += (n - 1) * abs(stq)
        lo = off * esz
        hi = (off + ext + 1) * esz
        if "PSUM" in sp or name not in self.sb_base:
            if name not in self._pseudo:
                self._pseudo[name] = self._pseudo_next
                self._pseudo_next += 16
            b = self._pseudo[name]
            return range(b, b + 16)
        b0, b1 = self.sb_base[name]
        if neg:
            lo, hi = 0, b1 - b0
        alo = b0 + lo
        ahi = min(b0 + hi, b1 + 256)
        return range(alo >> 8, ((ahi - 1) >> 8) + 1)

    def _wrap_engine(self, eng, sem_name):
        semh = self.sem(sem_name)
        lastw = [-1] * NPAGES
        lastr = [-1] * NPAGES
        state = {"waited": 0}

        def dep_sync(a, k):
            reads = []
            writes = []
            for i, x in enumerate(a):
                pg = self._ap_pages(x)
                if pg is not None:
                    (writes if i == 0 else reads).append(pg)
            for kw, x in k.items():
                pg = self._ap_pages(x)
                if pg is not None:
                    (writes if kw in ("accum_out", "out") else reads).append(pg)
            need = -1
            for pgs in reads:
                for pg in pgs:
                    if lastw[pg] > need:
                        need = lastw[pg]
            for pgs in writes:
                for pg in pgs:
                    if lastw[pg] > need:
                        need = lastw[pg]
                    if lastr[pg] > need:
                        need = lastr[pg]
            if need >= state["waited"]:
                eng.wait_ge(semh, need + 1)
                state["waited"] = need + 1
            seq = self.semv[sem_name]
            for pgs in writes:
                for pg in pgs:
                    lastw[pg] = seq
            for pgs in reads:
                for pg in pgs:
                    if lastr[pg] < seq:
                        lastr[pg] = seq

        for name in self._WRAP_OPS:
            if not hasattr(eng, name):
                continue
            orig = getattr(eng, name)

            def mk(orig):
                def f(*a, **k):
                    dep_sync(a, k)
                    instr = orig(*a, **k)
                    instr.then_inc(semh, 1)
                    self.semv[sem_name] += 1
                    return instr
                return f
            setattr(eng, name, mk(orig))

    def sem(self, name):
        if name not in self.sems:
            self.sems[name] = self.nc.alloc_semaphore(name=name)
            self.semv[name] = 0
        return self.sems[name]

    def inc(self, instr, name, n=1):
        if name in ("dv", "ac", "pl"):
            return instr
        instr.then_inc(self.sem(name), n)
        self.semv[name] += n
        return instr

    def wait(self, eng, name):
        self.wait_val(eng, name, self.semv[name])

    def wait_val(self, eng, name, val):
        hw = self.wait_hw.setdefault(id(eng), {})
        if hw.get(name, -1) >= val:
            return
        hw[name] = val
        eng.wait_ge(self.sem(name), val)

    def val(self, name):
        self.sem(name)
        return self.semv[name]

    def sb(self, name, shape, dtype, offset=None):
        if name in self.st:
            return self.st[name]
        if offset is None:
            b0 = self.nc.sbuf_base
            t = self.nc.alloc_sbuf_tensor(name, list(shape), dtype)
            self.sb_base[name] = (b0, self.nc.sbuf_base)
        else:
            t = self.nc.alloc_sbuf_tensor_at(name, list(shape), dtype, offset=offset)
            nb = shape[1] * DTSZ.get(dtype, 4)
            for s in shape[2:]:
                nb *= s
            self.sb_base[name] = (offset, offset + nb)
        self.st[name] = t
        return t

    def ps(self, name, shape, dtype=F32):
        if name in self.st:
            return self.st[name]
        t = self.nc.alloc_psum_tensor(name, list(shape), dtype)
        self.st[name] = t
        return t

    def dump(self, name, ap, shape, dtype=F32):
        if name not in self.debug_names:
            return
        for s in ("pe", "dv", "ac", "pl", "xsp0", "xsp1", "x1sp"):
            if s in self.sems:
                self.nc.sync.wait_ge(self.sems[s], self.semv[s])
        o = self.nc.dram_tensor("dbg_" + name, list(shape), dtype,
                                kind="ExternalOutput").ap()
        self.dbg_out[name] = o
        self.inc(self.nc.sync.dma_start(out=o, in_=ap), "dbg", DMA)
        for eng in (self.nc.tensor, self.nc.vector, self.nc.scalar,
                    self.nc.gpsimd, self.nc.sync):
            eng.wait_ge(self.sems["dbg"], self.semv["dbg"])

    def finish(self):
        for s in ("dbg", "yst0", "yst1", "xsp0", "xsp1", "x1sp"):
            if s in self.sems:
                self.nc.sync.wait_ge(self.sems[s], self.semv[s])

    def barrier(self, name, engines=("tensor", "vector", "scalar", "gpsimd", "sync")):
        s = self.sem(name)
        for e in engines:
            getattr(self.nc, e).sem_inc(s, 1)
        self.semv[name] += len(engines)
        for e in engines:
            getattr(self.nc, e).wait_ge(s, self.semv[name])


def ceil_div(a, b):
    return (a + b - 1) // b


class PsumPool:
    def __init__(self, p):
        self.p = p
        self.banks = [p.ps(f"bank{i}", (128, 512), F32) for i in range(8)]
        self.free_tok = [None] * 8
        self.held = [False] * 8
        self.rr = 0

    def acquire(self, cols=512):
        for _ in range(8):
            i = self.rr
            self.rr = (self.rr + 1) % 8
            if not self.held[i]:
                self.held[i] = True
                tok = self.free_tok[i]
                self.free_tok[i] = None
                return i, self.banks[i][:, 0:cols], tok
        raise RuntimeError("psum pool exhausted")

    def release(self, i, tok):
        self.held[i] = False
        self.free_tok[i] = tok


def build(p: Prog, stop_after=None):
    nc = p.nc
    din = p.din
    pp = PsumPool(p)
    PE, DV, AC, PL = "pe", "dv", "ac", "pl"

    def tok(name):
        return (name, p.val(name))

    own_sem = {id(nc.vector): "dv", id(nc.scalar): "ac", id(nc.gpsimd): "pl"}

    def w8(eng, *toks):
        seen = {}
        own = own_sem.get(id(eng))
        for t in toks:
            if t is None:
                continue
            seen[t[0]] = max(seen.get(t[0], 0), t[1])
        for name, v in seen.items():
            if name == own:
                continue
            p.wait_val(eng, name, v)

    TOP = nc.sbuf_top
    R_W = TOP - 49152
    R_DR = R_W - 95392
    R_A = R_DR - 33792
    R_C = R_A - 32768
    ident_b = p.sb("ident_b", (128, 128), BF16)
    m_stril = p.sb("m_stril_s", (128, 128), BF16)
    m_triu = p.sb("m_triu_s", (128, 128), BF16)
    onorm_b = p.sb("onorm_b", (128, 8), BF16)
    gate_c = p.sb("gate_c", (128, 1), F32)
    ltemp_c = p.sb("ltemp_c", (128, 1), F32)
    ress_c = p.sb("ress_c", (128, 1), F32)
    resl_c = p.sb("resl_c", (128, 1), F32)
    ones_row = p.sb("ones_row", (1, 128), BF16)
    WqT_b = p.sb("WqT_b", (128, 8 * D), BF16, offset=R_W)
    WkT_b = p.sb("WkT_b", (128, 8 * D), BF16, offset=R_W + 16384)
    WvT_b = p.sb("WvT_b", (128, 8 * D), BF16, offset=R_W + 32768)
    bprojT_b = p.sb("bprojT_b", (128, 8 * 4), BF16)
    diag_q = p.sb("diag_q", (128, 32 * 128), BF16, offset=R_DR)
    diag_k = p.sb("diag_k", (128, 32 * 128), BF16, offset=R_DR + 8192)
    diag_v = p.sb("diag_v", (128, 32 * 128), BF16, offset=R_DR + 16384)
    epscol = p.sb("epscol", (128, 1), F32)
    beta_scr = p.sb("beta_scr", (128, 8), F32)


    assert nc.sbuf_base <= R_C, (nc.sbuf_base, R_C)
    v_ch = p.sb("v_ch", (128, 8 * (T + 64)), BF16, offset=R_A)
    do_tok = p.sb("do_tok", (128, NT * D), BF16, offset=R_C)

    ro = R_DR + 24576
    xr = p.sb("xr", (128, 2 * D), BF16, offset=ro); ro += 4096
    xpre_r = p.sb("xpre_r", (128, 2 * 8 * CO), BF16, offset=ro); ro += 4096
    lin = {}
    for t in ("q", "k", "v"):
        lin[t] = p.sb("lin" + t, (128, 8 * (CO + 4)), BF16, offset=ro); ro += 2112
    qch_r = p.sb("qch_r", (128, 2 * 8 * CO), BF16, offset=ro); ro += 4096
    kch_r = p.sb("kch_r", (128, 2 * 8 * CO), BF16, offset=ro); ro += 4096
    vch_r = p.sb("vch_r", (128, 2 * 8 * CO), BF16, offset=ro); ro += 4096
    ktok_r = p.sb("ktok_r", (128, 2 * D), BF16, offset=ro); ro += 4096
    vtok_r = p.sb("vtok_r", (128, 2 * D), BF16, offset=ro); ro += 4096
    tp_r = p.sb("tp_r", (128, 2 * H * 128), BF16, offset=ro); ro += 2048
    att2_r = p.sb("att2_r", (128, 2 * H * 128), BF16, offset=ro); ro += 2048
    chBB = p.sb("chBB", (128, 4 * 256), BF16, offset=ro); ro += 2048
    chP4 = p.sb("chP4", (128, 4 * 128), BF16, offset=ro); ro += 1024
    zu4 = p.sb("zu4", (128, 4 * 256), BF16, offset=ro); ro += 2048
    S_sb = []
    Sb_sb = []
    ro += 64
    s_off = ro
    for h in range(H):
        S_sb.append(p.sb(f"S{h}", (128, 512), F32, offset=ro)); ro += 2048
        Sb_sb.append(p.sb(f"Sb{h}", (128, 512), BF16, offset=ro)); ro += 1024
    cw_s = {}
    for t in ("q", "k", "v"):
        cw_s[t] = p.sb(t + "cw_s", (128, 8 * 4), F32, offset=ro); ro += 128
    fir_s_s = p.sb("fir_s_s", (128, 8 * 5), F32, offset=ro); ro += 160
    beta_c = p.sb("beta_c", (128, 2 * H), F32, offset=ro); ro += 32
    rk_c = p.sb("rk_c", (128, 2 * H), F32, offset=ro); ro += 32
    nrk_c = p.sb("nrk_c", (128, 2 * H), F32, offset=ro); ro += 32
    ns_c = p.sb("ns_c", (128, 2 * H), F32, offset=ro); ro += 32
    rkb_c = p.sb("rkb_c", (128, 2 * H), F32, offset=ro); ro += 32
    rq_c = p.sb("rq_c", (128, 2 * H), F32, offset=ro); ro += 32
    scr1 = p.sb("scr1", (128, 160), F32, offset=ro); ro += 640
    scr2 = p.sb("scr2", (128, 128), F32, offset=ro); ro += 512
    assert ro <= R_DR + 83104, (ro, R_DR + 83104)

    def loads():
        g_ = lambda dst, src: p.inc(nc.gpsimd.dma_start(out=dst, in_=src), "ldg", DMA)
        s_ = lambda dst, src: p.inc(nc.sync.dma_start(out=dst, in_=src), "lds", DMA)
        v3 = lambda t, n: t[:].rearrange("p (k co) -> p k co", k=n)
        g_(v3(WkT_b, 8), din["WkT"].rearrange("(k p) co -> p k co", p=128))
        g_(v3(WvT_b, 8), din["WvT"].rearrange("(k p) co -> p k co", p=128))
        g_(v3(WqT_b, 8), din["WqT"].rearrange("(k p) co -> p k co", p=128))
        g_(v3(bprojT_b, 8), din["bprojT"].rearrange("(k p) co -> p k co", p=128))
        g_(ident_b[:], din["ident"])
        g_(m_stril[:], din["m_stril"])
        g_(m_triu[:], din["m_triu"])
        g_(onorm_b[:], din["onorm"])
        for t in ("q", "k", "v"):
            s_(cw_s[t][:].rearrange("p (j d) -> p j d", j=8),
               din[t + "cw"].rearrange("(j p) d -> p j d", p=128))
        s_(fir_s_s[:].rearrange("p (j d) -> p j d", j=8),
           din["fir_s"].rearrange("(j p) d -> p j d", p=128))
        s_(gate_c[:], din["gate"])
        s_(ltemp_c[:], din["ltemp"])
        s_(ress_c[:], din["ress"])
        s_(resl_c[:], din["resl"])
        ldg, lds = tok("ldg"), tok("lds")
        w8(nc.vector, ldg, lds)
        i = nc.vector.memset(ones_row[:], 1.0)
        i = nc.vector.memset(epscol[:], 1e-5)
        for t, dg in (("q", diag_q), ("k", diag_k), ("v", diag_v)):
            for j in range(8):
                for d in range(4):
                    i = nc.vector.tensor_scalar(
                        dg[:, (j * 4 + d) * 128:(j * 4 + d + 1) * 128], ident_b[:],
                        cw_s[t][:, j * 4 + d:j * 4 + d + 1], None, AL.mult)
        p.inc(i, DV)
        for t in ("q", "k", "v"):
            for j in range(8):
                i = nc.vector.memset(lin[t][:, j * (CO + 4):j * (CO + 4) + 4], 0.0)
        for h in range(H):
            i = nc.vector.memset(S_sb[h][:], 0.0)
            i = nc.vector.memset(Sb_sb[h][:], 0.0)
        p.inc(i, DV)
        return tok(DV), ldg, lds

    def cast_x_chunk(o, which):
        src = din["x_pre"] if which == "pre" else din["x_own"]
        dst = xr[:, (o % 2) * D:(o % 2 + 1) * D]
        for t in st["xr_free"][o % 2]:
            w8(nc.gpsimd, t)
        p.inc(nc.gpsimd.dma_start(out=dst, in_=src[o * CO:(o + 1) * CO, :]),
              f"xdma{o % 2}", DMA)
        return tok(f"xdma{o % 2}")

    def xT_chunk(o, which, dep):
        res = None
        for jp in (0, 4):
            bi, bank, btok = pp.acquire()
            w8(nc.tensor, btok, dep)
            for j in range(jp, jp + 4):
                mm = nc.tensor.matmul(bank[:, (j - jp) * 128:(j - jp + 1) * 128],
                                      xr[:, (o % 2) * D + j * 128:(o % 2) * D + (j + 1) * 128],
                                      ident_b[:], start=True, stop=True)
            p.inc(mm, PE)
            pet = tok(PE)
            w8(nc.scalar, pet, *st["xpre_free"][o % 2])
            for j in range(jp, jp + 4):
                dst = xpre_r[:, ((o % 2) * 8 + j) * CO:((o % 2) * 8 + j + 1) * CO]
                i = nc.scalar.copy(dst, bank[:, (j - jp) * 128:(j - jp + 1) * 128])
            p.inc(i, AC)
            res = tok(AC)
            pp.release(bi, res)
        if which == "main":
            w8(nc.sync, res)
            dst = p.d_xch.rearrange("p (k t) -> p k t", k=8)[:, :, o * CO:(o + 1) * CO]
            src = xpre_r[:, (o % 2) * 8 * CO:((o % 2) * 8 + 8) * CO].rearrange(
                "p (k t) -> p k t", k=8)
            p.inc(nc.sync.dma_start(out=dst, in_=src), f"xsp{o % 2}", DMA)
        return res

    def xsrc(o, which):
        return lambda k: xpre_r[:, ((o % 2) * 8 + k) * CO:((o % 2) * 8 + k + 1) * CO]

    def proj_chunk(o, which, tensors, xdep, lindeps=()):
        xs = xsrc(o, which)
        res = None
        for tname in tensors:
            W = {"q": WqT_b, "k": WkT_b, "v": WvT_b}[tname]
            ln = lin[tname]
            for m in range(8):
                bi, bank, btok = pp.acquire(cols=128)
                w8(nc.tensor, btok, xdep)
                for k in range(8):
                    mm = nc.tensor.matmul(bank, W[:, (k * 8 + m) * 128:(k * 8 + m + 1) * 128],
                                          xs(k), start=(k == 0), stop=(k == 7))
                p.inc(mm, PE)
                pet = tok(PE)
                w8(nc.vector, pet, *lindeps)
                dst = ln[:, m * (CO + 4) + 4: m * (CO + 4) + 4 + CO]
                i = nc.vector.tensor_copy(dst, bank)
                p.inc(i, DV)
                res = tok(DV)
                pp.release(bi, res)
        return res

    def conv_silu_chunk(o, which, tensors, lindep, halodep):
        out_toks = {}
        for tname in tensors:
            dg = {"q": diag_q, "k": diag_k, "v": diag_v}[tname]
            ln = lin[tname]
            for j in range(8):
                bi, bank, btok = pp.acquire(cols=128)
                w8(nc.tensor, btok, lindep, halodep)
                base = j * (CO + 4) + 4
                for d in range(4):
                    mm = nc.tensor.matmul(bank, dg[:, (j * 4 + d) * 128:(j * 4 + d + 1) * 128],
                                          ln[:, base + d - 3: base + d - 3 + CO],
                                          start=(d == 0), stop=(d == 3))
                p.inc(mm, PE)
                pet = tok(PE)
                w8(nc.scalar, pet)
                if which == "main" and tname == "v":
                    dst = v_ch[:, j * (T + 64) + 64 + o * CO:j * (T + 64) + 64 + (o + 1) * CO]
                else:
                    dst = {"q": qch_r, "k": kch_r, "v": vch_r}[tname][
                        :, ((o % 2) * 8 + j) * CO:((o % 2) * 8 + j + 1) * CO]
                i = nc.scalar.activation(dst, bank, AF.Silu)
                p.inc(i, AC)
                pp.release(bi, tok(AC))
            out_toks[tname] = tok(AC)
        w8(nc.vector, lindep, out_toks[tensors[-1]])
        i = None
        for tname in tensors:
            ln = lin[tname]
            for j in range(8):
                b0 = j * (CO + 4)
                i = nc.vector.tensor_copy(ln[:, b0 + 1:b0 + 4],
                                          ln[:, b0 + 4 + CO - 3: b0 + 4 + CO])
        if i is not None:
            p.inc(i, DV)
        return out_toks, tok(DV)

    def beta_chunk(o, which, xdep):
        xs = xsrc(o, which)
        bi, bank, btok = pp.acquire(cols=4)
        w8(nc.tensor, btok, xdep)
        for k in range(8):
            mm = nc.tensor.matmul(bank, xs(k), bprojT_b[:, k * 4:(k + 1) * 4],
                                  start=(k == 0), stop=(k == 7))
        p.inc(mm, PE)
        pet = tok(PE)
        w8(nc.scalar, pet)
        i = nc.scalar.activation(scr1[:, 140:144], bank, AF.Tanh, scale=0.5)
        p.inc(i, AC)
        w8(nc.vector, tok(AC))
        i = nc.vector.tensor_scalar(beta_c[:, (o % 2) * H:(o % 2) * H + H], scr1[:, 140:144],
                                    0.5, 0.5, AL.mult, AL.add)
        p.inc(i, DV)
        res = tok(DV)
        pp.release(bi, res)
        return res

    def kv_tok_chunk(o, which, convtoks, tnames=("k", "v")):
        srcs = {}
        srcs["k"] = lambda j: kch_r[:, ((o % 2) * 8 + j) * CO:((o % 2) * 8 + j + 1) * CO]
        if which == "pre":
            srcs["v"] = lambda j: vch_r[:, ((o % 2) * 8 + j) * CO:((o % 2) * 8 + j + 1) * CO]
        else:
            srcs["v"] = lambda j: v_ch[:, j * (T + 64) + 64 + o * CO:j * (T + 64) + 64 + (o + 1) * CO]
        dsts = {"k": ktok_r, "v": vtok_r}
        res = None
        for tname in tnames:
            for jp in (0, 4):
                bi, bank, btok = pp.acquire()
                w8(nc.tensor, btok, convtoks.get(tname))
                for j in range(jp, jp + 4):
                    mm = nc.tensor.matmul(bank[:, (j - jp) * 128:(j - jp + 1) * 128],
                                          srcs[tname](j), ident_b[:], start=True, stop=True)
                p.inc(mm, PE)
                w8(nc.scalar, tok(PE))
                for j in range(jp, jp + 4):
                    i = nc.scalar.copy(
                        dsts[tname][:, (o % 2) * D + j * 128:(o % 2) * D + (j + 1) * 128],
                        bank[:, (j - jp) * 128:(j - jp + 1) * 128])
                p.inc(i, AC)
                res = tok(AC)
                pp.release(bi, res)
        return res

    def gram_into(lhs_fn, rhs_fn, dep):
        bi, bank, btok = pp.acquire(cols=128)
        w8(nc.tensor, btok, dep)
        for hh in range(2):
            mm = nc.tensor.matmul(bank, lhs_fn(hh), rhs_fn(hh),
                                  start=(hh == 0), stop=(hh == 1))
        p.inc(mm, PE)
        return bi, bank, tok(PE)

    def cols_chunk(o, which, kvt, bt):
        c0 = (o % 2) * H
        w8(nc.scalar, kvt)
        for h in range(H):
            i = nc.scalar.activation(do_tok[:, o * D + h * DH:o * D + (h + 1) * DH],
                                     ktok_r[:, (o % 2) * D + h * DH:(o % 2) * D + (h + 1) * DH],
                                     AF.Square, accum_out=beta_scr[:, h:h + 1])
        p.inc(i, AC)
        w8(nc.scalar, tok(AC))
        i = nc.scalar.activation(scr1[:, 128:132], beta_scr[:, 0:4], AF.Ln)
        i = nc.scalar.activation(rk_c[:, c0:c0 + 4], scr1[:, 128:132], AF.Exp, scale=-0.5)
        p.inc(i, AC)
        w8(nc.vector, tok(AC), bt)
        i = nc.vector.reciprocal(scr1[:, 132:136], beta_scr[:, 0:4])
        i = nc.vector.tensor_tensor(scr1[:, 136:140], beta_c[:, c0:c0 + 4],
                                    scr1[:, 132:136], AL.mult)
        i = nc.vector.tensor_scalar(ns_c[:, c0:c0 + 4], scr1[:, 136:140],
                                    -1.0, None, AL.mult)
        i = nc.vector.tensor_scalar(nrk_c[:, c0:c0 + 4], rk_c[:, c0:c0 + 4],
                                    -1.0, None, AL.mult)
        i = nc.vector.tensor_tensor(rkb_c[:, c0:c0 + 4], rk_c[:, c0:c0 + 4],
                                    beta_c[:, c0:c0 + 4], AL.mult)
        p.inc(i, DV)
        return tok(DV)

    def delta_prep_chunk(o, which, convtoks, kvtok_tok, colstok):
        c0 = (o % 2) * H
        kslice = lambda h, hh: kch_r[:, ((o % 2) * 8 + 2 * h + hh) * CO:
                                     ((o % 2) * 8 + 2 * h + hh + 1) * CO]
        qslice = lambda h, hh: qch_r[:, ((o % 2) * 8 + 2 * h + hh) * CO:
                                     ((o % 2) * 8 + 2 * h + hh + 1) * CO]
        chB = lambda h: chBB[:, h * 256:h * 256 + 128]
        chBt = lambda h: chBB[:, h * 256 + 128:h * 256 + 256]
        chP = lambda h: chP4[:, h * 128:(h + 1) * 128]
        cols_t = colstok
        g0b = []
        for h in range(H):
            bi, g0ps, pet = gram_into(lambda hh: kslice(h, hh),
                                      lambda hh: kslice(h, hh), convtoks.get("k"))
            g0b.append((bi, g0ps, pet))
        fill()
        bb_t = [None] * H
        for h, (bi, g0ps, pet) in enumerate(g0b):
            w8(nc.vector, pet, cols_t)
            i = nc.vector.scalar_tensor_tensor(chB(h), g0ps, ns_c[:, c0 + h:c0 + h + 1],
                                               m_stril[:], AL.mult, AL.mult)
            p.inc(i, DV)
            bb_t[h] = tok(DV)
            pp.release(bi, bb_t[h])
        fill(2)
        trs = []
        for h in range(H):
            bi, bank, btok = pp.acquire(cols=128)
            w8(nc.tensor, btok, bb_t[h])
            mm = nc.tensor.matmul(bank, chB(h), ident_b[:], start=True, stop=True)
            p.inc(mm, PE)
            trs.append((bi, bank, tok(PE)))
        bt_t = [None] * H
        for h, (bi, bank, pet) in enumerate(trs):
            w8(nc.scalar, pet)
            i = nc.scalar.copy(chBt(h), bank)
            p.inc(i, AC)
            bt_t[h] = tok(AC)
            pp.release(bi, bt_t[h])
        pP_t = [None] * H
        for h in range(H):
            w8(nc.vector, bb_t[h])
            i = nc.vector.tensor_tensor(chP(h), chB(h), ident_b[:], AL.add)
            p.inc(i, DV)
            pP_t[h] = tok(DV)
        for lvl in range(6):
            sq = []
            for h in range(H):
                bi1, b1, bt1 = pp.acquire(cols=256)
                w8(nc.tensor, bt1, bb_t[h], bt_t[h])
                mm = nc.tensor.matmul(b1[:, 0:128], chBt(h), chB(h), start=True, stop=True)
                mm = nc.tensor.matmul(b1[:, 128:256], chB(h), chBt(h), start=True, stop=True)
                p.inc(mm, PE)
                sq.append((bi1, b1, tok(PE)))
            for h, (bi1, b1, pet) in enumerate(sq):
                p0dep = pP_t[h] if lvl == 0 else None
                if h % 2 == 0:
                    w8(nc.vector, pet, p0dep)
                    i = nc.vector.tensor_copy(chBB[:, h * 256:(h + 1) * 256], b1)
                    p.inc(i, DV)
                    bb_t[h] = bt_t[h] = tok(DV)
                else:
                    w8(nc.scalar, pet, p0dep)
                    i = nc.scalar.copy(chBB[:, h * 256:(h + 1) * 256], b1)
                    p.inc(i, AC)
                    bb_t[h] = bt_t[h] = tok(AC)
                pp.release(bi1, bb_t[h])
            pr = []
            for h in range(H):
                bi3, b3, bt3 = pp.acquire(cols=128)
                w8(nc.tensor, bt3, bb_t[h], pP_t[h])
                mm = nc.tensor.matmul(b3, ident_b[:], chP(h), start=True, stop=False)
                mm = nc.tensor.matmul(b3, chBt(h), chP(h), start=False, stop=True)
                p.inc(mm, PE)
                pr.append((bi3, b3, tok(PE)))
            for h, (bi3, b3, pet) in enumerate(pr):
                if h % 2 == 1:
                    w8(nc.vector, pet)
                    i = nc.vector.tensor_copy(chP(h), b3)
                    p.inc(i, DV)
                    pP_t[h] = tok(DV)
                else:
                    w8(nc.scalar, pet)
                    i = nc.scalar.copy(chP(h), b3)
                    p.inc(i, AC)
                    pP_t[h] = tok(AC)
                pp.release(bi3, pP_t[h])
            fill()
        tps = []
        for h in range(H):
            bi, bank, btok = pp.acquire(cols=128)
            w8(nc.tensor, btok, pP_t[h])
            mm = nc.tensor.matmul(bank, chP(h), ident_b[:], start=True, stop=True)
            p.inc(mm, PE)
            tps.append((bi, bank, tok(PE)))
        tp_toks = []
        for h, (bi, bank, pet) in enumerate(tps):
            w8(nc.vector, pet)
            i = nc.vector.tensor_scalar(
                tp_r[:, ((o % 2) * H + h) * 128:((o % 2) * H + h + 1) * 128],
                bank, rkb_c[:, c0 + h:c0 + h + 1], None, AL.mult)
            p.inc(i, DV)
            tp_toks.append(tok(DV))
            pp.release(bi, tok(DV))
        fill()
        att2_tok = None
        rq_tok = None
        if which == "main":
            a2b = []
            for h in range(H):
                bi, a2ps, pet = gram_into(lambda hh: kslice(h, hh),
                                          lambda hh: qslice(h, hh), convtoks.get("q"))
                a2b.append((bi, a2ps, pet))
            fill()
            for h, (bi, a2ps, pet) in enumerate(a2b):
                w8(nc.vector, pet)
                i = nc.vector.tensor_tensor(
                    att2_r[:, ((o % 2) * H + h) * 128:((o % 2) * H + h + 1) * 128],
                    a2ps, m_triu[:], AL.mult)
                p.inc(i, DV)
                pp.release(bi, tok(DV))
            att2_tok = tok(DV)
            qb = []
            for h in range(H):
                bi, qps, pet = gram_into(lambda hh: qslice(h, hh),
                                         lambda hh: qslice(h, hh), convtoks.get("q"))
                qb.append((bi, qps, pet))
            exs = [scr1[:, 0:128], scr2[:, 0:128]]
            for h, (bi, qps, pet) in enumerate(qb):
                w8(nc.vector, pet)
                i = nc.vector.tensor_tensor(exs[h % 2], qps, ident_b[:], AL.mult)
                i = nc.vector.reduce_sum(beta_scr[:, 4 + h:5 + h], exs[h % 2], axis=AX.X)
                p.inc(i, DV)
                pp.release(bi, tok(DV))
            w8(nc.scalar, tok(DV))
            i = nc.scalar.activation(scr1[:, 128:132], beta_scr[:, 4:8], AF.Ln)
            i = nc.scalar.activation(rq_c[:, c0:c0 + 4], scr1[:, 128:132], AF.Exp, scale=-0.5)
            p.inc(i, AC)
            rq_tok = tok(AC)
        return {"tp": tp_toks[-1], "att2": att2_tok, "rq": rq_tok, "cols": cols_t}

    def scan_stages(o, which, preptoks, kvt):
        c0 = (o % 2) * H
        kslice = lambda h, hh: kch_r[:, ((o % 2) * 8 + 2 * h + hh) * CO:
                                      ((o % 2) * 8 + 2 * h + hh + 1) * CO]
        qslice = lambda h, hh: qch_r[:, ((o % 2) * 8 + 2 * h + hh) * CO:
                                      ((o % 2) * 8 + 2 * h + hh + 1) * CO]
        zu = lambda h: zu4[:, h * 256:(h + 1) * 256]
        sst = {"ks": {}, "zt": {}, "ub": {}, "ut": {}, "ob": {}, "db": {},
               "out": [None] * H}

        def ks_pair(hs):
            def f():
                for h in hs:
                    biA, psA, btokA = pp.acquire(cols=256)
                    w8(nc.tensor, btokA, st["s_toks"][h], preptoks["cols"], kvt)
                    for hh in range(2):
                        mm = nc.tensor.matmul(psA, kslice(h, hh),
                                              Sb_sb[h][:, hh * 256:(hh + 1) * 256],
                                              start=(hh == 0), stop=(hh == 1))
                    p.inc(mm, PE)
                    sst["ks"][h] = (biA, psA, tok(PE))
            return f

        def out_pair(hs):
            def f():
                for h in hs:
                    biC, psC, btokC = pp.acquire(cols=256)
                    w8(nc.tensor, btokC, sst["ut"][h], preptoks["att2"], preptoks["rq"])
                    for hh in range(2):
                        mm = nc.tensor.matmul(psC, qslice(h, hh),
                                              Sb_sb[h][:, hh * 256:(hh + 1) * 256],
                                              start=(hh == 0), stop=False)
                    mm = nc.tensor.matmul(
                        psC, att2_r[:, ((o % 2) * H + h) * 128:((o % 2) * H + h + 1) * 128],
                        zu(h), start=False, stop=True)
                    p.inc(mm, PE)
                    sst["ob"][h] = (biC, psC, tok(PE))
            return f

        def ds_pair(hs):
            def f():
                for h in hs:
                    biD, psD, btokD = pp.acquire(cols=512)
                    w8(nc.tensor, btokD, sst["ut"][h], kvt)
                    for hh in range(2):
                        mm = nc.tensor.matmul(
                            psD[:, hh * 256:(hh + 1) * 256],
                            ktok_r[:, (o % 2) * D + h * 256 + hh * 128:
                                   (o % 2) * D + h * 256 + hh * 128 + 128],
                            zu(h), start=True, stop=True)
                    p.inc(mm, PE)
                    sst["db"][h] = (biD, psD, tok(PE))
            return f

        def z_pair(hs):
            def f():
                for h in hs:
                    biA, psA, pet = sst["ks"][h]
                    w8(nc.vector, pet, kvt)
                    i = nc.vector.scalar_tensor_tensor(
                        zu(h), psA, nrk_c[:, c0 + h:c0 + h + 1],
                        vtok_r[:, (o % 2) * D + h * 256:(o % 2) * D + (h + 1) * 256],
                        AL.mult, AL.add)
                    p.inc(i, DV)
                    sst["zt"][h] = tok(DV)
                    pp.release(biA, sst["zt"][h])
            return f

        def u_pair(hs):
            def f():
                for h in hs:
                    biB, psB, btokB = pp.acquire(cols=256)
                    w8(nc.tensor, btokB, sst["zt"][h], preptoks["tp"])
                    mm = nc.tensor.matmul(
                        psB, tp_r[:, ((o % 2) * H + h) * 128:((o % 2) * H + h + 1) * 128],
                        zu(h), start=True, stop=True)
                    p.inc(mm, PE)
                    w8(nc.scalar, tok(PE))
                    i = nc.scalar.copy(zu(h), psB)
                    p.inc(i, AC)
                    sst["ut"][h] = tok(AC)
                    pp.release(biB, sst["ut"][h])
            return f

        def oev_pair(hs):
            def f():
                for h in hs:
                    biC, psC, pet = sst["ob"][h]
                    w8(nc.scalar, pet)
                    i = nc.scalar.activation(
                        do_tok[:, o * D + h * 256:o * D + (h + 1) * 256], psC,
                        AF.Copy, scale=rq_c[:, c0 + h:c0 + h + 1])
                    p.inc(i, AC)
                    pp.release(biC, tok(AC))
            return f

        def su_pair(hs):
            def f():
                for h in hs:
                    biD, psD, pet = sst["db"][h]
                    w8(nc.vector, pet)
                    i = nc.vector.tensor_tensor(S_sb[h][:], S_sb[h][:], psD, AL.add)
                    p.inc(i, DV)
                    w8(nc.scalar, tok(DV))
                    i = nc.scalar.copy(Sb_sb[h][:], S_sb[h][:])
                    p.inc(i, AC)
                    st["s_toks"][h] = tok(AC)
                    pp.release(biD, st["s_toks"][h])
            return f

        def seq(*fs):
            def f():
                for g in fs:
                    g()
            return f

        stages = [seq(ks_pair((0, 1)), z_pair((0, 1))),
                  seq(ks_pair((2, 3)), z_pair((2, 3))),
                  u_pair((0, 1)), u_pair((2, 3))]
        if which == "main":
            stages += [out_pair((0, 1)), seq(out_pair((2, 3)), oev_pair((0, 1))),
                       seq(oev_pair((2, 3)), ds_pair((0, 1))),
                       seq(ds_pair((2, 3)), su_pair((0, 1))), su_pair((2, 3))]
        else:
            stages += [ds_pair((0, 1)), seq(ds_pair((2, 3)), su_pair((0, 1))),
                       su_pair((2, 3))]
        return stages

    st = {}
    ld_dv, ldg, lds = loads()
    st["halo"] = ld_dv
    st["conv"] = ld_dv
    st["xr_free"] = {0: [ld_dv], 1: [ld_dv]}
    st["x1t_free"] = {0: ld_dv, 1: ld_dv}
    st["xpre_free"] = {0: [ld_dv], 1: [ld_dv]}
    s_toks = [ld_dv] * H
    st["s_toks"] = s_toks
    st["filler"] = []

    def fill(k=1):
        for _ in range(k):
            if st["filler"]:
                st["filler"].pop(0)()

    def fill_all():
        while st["filler"]:
            st["filler"].pop(0)()

    def front(o, which, tensors, xd):
        prevconv, prevhalo = st["conv"], st["halo"]
        xt = xT_chunk(o, which, xd)
        st["xr_free"][o % 2] = [tok(PE)]
        if o + 1 < NCO:
            st["xd_next"] = cast_x_chunk(o + 1, which)
        fill(2)
        pj = proj_chunk(o, which, tensors, xt, (prevconv, prevhalo))
        bt = beta_chunk(o, which, xt)
        cv1, halo1 = conv_silu_chunk(o, which, ["k"], pj, st["halo"])
        kvt_k = kv_tok_chunk(o, which, cv1, ("k",))
        colst = cols_chunk(o, which, kvt_k, bt)
        fill(2)
        rest = [t for t in tensors if t != "k" and (t != "q" or which == "main")]
        cv2, halo2 = conv_silu_chunk(o, which, rest, pj, halo1)
        cv = dict(cv1)
        cv.update(cv2)
        st["halo"] = halo2
        st["conv"] = cv2[rest[-1]] if rest else cv1["k"]
        frees = [tok(PE)]
        if f"xsp{o % 2}" in p.sems:
            frees.append((f"xsp{o % 2}", p.semv[f"xsp{o % 2}"]))
        st["xpre_free"][o % 2] = frees
        kvt = kv_tok_chunk(o, which, cv, ("v",))
        return cv, bt, kvt, colst

    xd = cast_x_chunk(0, "pre")
    for o in range(NCO):
        tensors = ("k", "v") if o < NCO - 1 else ("k", "v", "q")
        cv, bt, kvt, colst = front(o, "pre", tensors, xd)
        xd = st.get("xd_next")
        if o == NCO - 1:
            w8(nc.vector, tok(DV), tok(AC))
            iq = None
            for j in range(8):
                b0 = j * (CO + 4)
                iq = nc.vector.tensor_copy(lin["q"][:, b0 + 1:b0 + 4],
                                           lin["q"][:, b0 + 4 + CO - 3:b0 + 4 + CO])
            p.inc(iq, DV)
        prep = delta_prep_chunk(o, "pre", cv, kvt, colst)
        fill_all()
        if o == 1:
            p.dump("S0_c0", S_sb[0][:], (128, 512))
        st["filler"] = scan_stages(o, "pre", prep, kvt)
        if o == 0:
            p.dump("kch0", kch_r[:, 0:8 * CO], (128, 8 * CO), BF16)
            p.dump("vch0", vch_r[:, 0:8 * CO], (128, 8 * CO), BF16)
            p.dump("ktok0", ktok_r[:, 0:D], (128, D), BF16)
            p.dump("vtok0", vtok_r[:, 0:D], (128, D), BF16)
            p.dump("beta0", beta_c[:, 0:4], (128, 4))
            p.dump("rk0", rk_c[:, 0:4], (128, 4))
            p.dump("ns0", ns_c[:, 0:4], (128, 4))
            p.dump("rkb0", rkb_c[:, 0:4], (128, 4))
            p.dump("tp0", tp_r[:, 0:H * 128], (128, H * 128), BF16)
        if o == NCO - 1:
            w8(nc.vector, cv["v"])
            iv = None
            for j in range(8):
                iv = nc.vector.tensor_copy(
                    v_ch[:, j * (T + 64):j * (T + 64) + 64],
                    vch_r[:, ((o % 2) * 8 + j) * CO + CO - 64:((o % 2) * 8 + j + 1) * CO])
            p.inc(iv, DV)

    fill_all()
    w8(nc.vector, *st["s_toks"])
    w8(nc.vector, tok(DV), lds)
    i = None
    for t in ("q", "k", "v"):
        for j in range(8):
            b0 = j * (CO + 4)
            i = nc.vector.tensor_scalar(lin[t][:, b0 + 1:b0 + 4],
                                        lin[t][:, b0 + 1:b0 + 4],
                                        gate_c[:, 0:1], None, AL.mult)
    for j in range(8):
        i = nc.vector.tensor_scalar(v_ch[:, j * (T + 64):j * (T + 64) + 64],
                                    v_ch[:, j * (T + 64):j * (T + 64) + 64],
                                    gate_c[:, 0:1], None, AL.mult)
    for h in range(H):
        i = nc.vector.tensor_scalar(S_sb[h][:], S_sb[h][:], gate_c[:, 0:1],
                                    None, AL.mult)
    p.inc(i, DV)
    st["halo"] = tok(DV)
    w8(nc.scalar, tok(DV))
    for h in range(H):
        i = nc.scalar.copy(Sb_sb[h][:], S_sb[h][:])
    p.inc(i, AC)
    st["s_toks"] = [tok(AC) for _ in range(H)]
    p.dump("pre_S0", S_sb[0][:], (128, 512))
    p.dump("pre_S3", S_sb[3][:], (128, 512))

    if stop_after == "prefix":
        p.finish()
        return

    xd = cast_x_chunk(0, "main")
    for o in range(NCO):
        cv, bt, kvt, colst = front(o, "main", ("k", "v", "q"), xd)
        xd = st.get("xd_next")
        prep = delta_prep_chunk(o, "main", cv, kvt, colst)
        fill_all()
        st["filler"] = scan_stages(o, "main", prep, kvt)

    fill_all()
    p.dump("do_tok", do_tok[:], (128, NT * D), BF16)
    p.dump("vch", v_ch[:], (128, 8 * (T + 64)), BF16)
    p.dump("main_S0", S_sb[0][:], (128, 512))
    if stop_after == "scan":
        p.finish()
        return

    W1xT_b = p.sb("W1xT_b", (128, 8 * D), BF16, offset=R_W)
    oprojT_b = p.sb("oprojT_b", (128, 8 * D), BF16, offset=R_W + 16384)
    gx_b = p.sb("gx_b", (128, 8 * 512), BF16, offset=R_W + 32768)
    hdn_b = p.sb("hdn_b", (128, 8 * 512), BF16, offset=R_W + 40960)
    pr = R_DR
    do_blk = p.sb("do_blk", (128, 8 * 512), BF16, offset=pr); pr += 8192
    ls_blk = p.sb("ls_blk", (128, 8 * 512), BF16, offset=pr); pr += 8192
    ll_blk = p.sb("ll_blk", (128, 8 * 512), BF16, offset=pr); pr += 8192
    oroll = p.sb("oroll", (128, 8 * 512), BF16, offset=pr); pr += 8192
    absq = p.sb("absq", (128, 8 * 512), BF16, offset=pr)
    sqscr = p.sb("sqscr", (128, D), F32, offset=pr); pr += 8192
    x1t = p.sb("x1t", (128, D), F32, offset=pr); pr += 4096
    WA_b = p.sb("WA_b", (128, D), BF16, offset=pr); pr += 2048
    WB_b = p.sb("WB_b", (128, D), BF16, offset=pr); pr += 2048
    wpad = p.sb("wpad", (128, D), BF16, offset=pr); pr += 2048
    fdiag = p.sb("fdiag", (128, 5 * 128), BF16, offset=pr - 2048)
    win_b = p.sb("win_b", (128, D), BF16, offset=pr)
    statsp = p.sb("statsp", (128, 3 * 512), F32, offset=pr)
    xg_b = p.sb("xg_b", (128, 8 * 512), BF16, offset=pr)
    Pa_b = p.sb("Pa_b", (128, D), BF16, offset=pr + 2048)
    Pb_b = p.sb("Pb_b", (128, D), BF16, offset=pr + 4096)
    statstok = p.sb("statstok", (128, 4 * 80), F32, offset=pr + 6400); pr += 8192
    cmbb4 = p.sb("cmbb4", (128, 4 * 512), BF16, offset=pr); pr += 4096
    selbank = p.sb("selbank", (128, 256), BF16, offset=pr); pr += 512
    lgt = p.sb("lgt", (128, 64), F32, offset=pr); pr += 256
    fwc = p.sb("fwc", (128, 48), F32, offset=pr); pr += 192
    cmbc = p.sb("cmbc", (128, 32), BF16, offset=pr); pr += 64
    lncol = p.sb("lncol", (128, 8), F32, offset=pr); pr += 32
    tempc = p.sb("tempc", (128, 4), F32, offset=pr); pr += 32
    ident_f = p.sb("ident_f", (128, 128), F32, offset=pr); pr += 512
    fftF = p.sb("fftF", (128, 128), BF16, offset=pr); pr += 256
    fftG1 = p.sb("fftG1", (128, 64), BF16, offset=pr); pr += 128
    fftG2 = p.sb("fftG2", (128, 64), BF16, offset=pr); pr += 128
    fftFA = p.sb("fftFA", (128, 128), BF16, offset=pr); pr += 256
    fftFB = p.sb("fftFB", (128, 128), BF16, offset=pr); pr += 256
    b1_b = p.sb("b1_b", (128, 8), F32, offset=pr); pr += 64
    W2T_b = p.sb("W2T_b", (128, 8 * 4), BF16, offset=pr); pr += 64
    assert pr <= s_off, (pr, s_off)
    cl = R_DR + 83104
    selrows_b = p.sb("selrows_b", (16, 16 * 128), BF16, offset=cl); cl += 4096
    CL_W1S4 = cl
    W1s4A = p.sb("W1s4A", (64, D), BF16, offset=cl); cl += 2048
    W1s4B = p.sb("W1s4B", (64, D), BF16, offset=cl); cl += 2048
    cmbrow = p.sb("cmbrow", (16, 512), BF16, offset=cl); cl += 1024
    rstdrow = p.sb("rstdrow", (16, 512), BF16, offset=cl); cl += 1024
    stats_allA = p.sb("stats_allA", (64, 512), BF16, offset=cl); cl += 1024
    stats_allB = p.sb("stats_allB", (64, 512), BF16, offset=cl); cl += 1024
    b2_b = p.sb("b2_b", (1, 4), BF16)
    assert cl <= R_DR + 95392, cl

    s_toks = st["s_toks"]
    s_tok = s_toks[-1]
    w8(nc.gpsimd, *s_toks)
    w8(nc.gpsimd, tok(PE), tok(DV), tok(AC))
    w8(nc.sync, *s_toks)
    w8(nc.sync, tok(PE), tok(DV), tok(AC))
    g2_ = lambda dst, src, sem: p.inc(nc.gpsimd.dma_start(out=dst, in_=src), sem, DMA)
    g2_(W1xT_b[:].rearrange("p (k co) -> p k co", k=8),
        din["W1xT"].rearrange("(k p) co -> p k co", p=128), "ldp")
    g2_(oprojT_b[:].rearrange("p (k co) -> p k co", k=8),
        din["oprojT"].rearrange("(k p) co -> p k co", p=128), "ldp")
    g2_(selrows_b[:], din["selrows"], "ldp")
    g2_2 = lambda dst, src, sem: p.inc(nc.sync.dma_start(out=dst, in_=src), sem, DMA)
    g2_2(ident_f[:], din["ident"], "ldps")
    g2_(W1s4A[:], din["W1sT4"][0:64, :], "ldp")
    g2_(W1s4B[:], din["W1sT4"][64:128, :], "ldp")
    g2_(b1_b[:], din["b1"], "ldp")
    g2_(W2T_b[:].rearrange("p (k j) -> p k j", k=8),
        din["W2T"].rearrange("(k p) j -> p k j", p=128), "ldp")
    g2_(b2_b[:], din["b2row"], "ldp")
    g2_(wpad[0:64, :], din["fir_lT"], "ldp")
    p.sem("ldps")
    ldp = [tok("ldp"), tok("ldps")]

    w8(nc.vector, lds)
    i = nc.vector.tensor_scalar(tempc[:, 2:3], ltemp_c[:, 0:1], 0.0, None, AL.max)
    p.inc(i, DV)
    w8(nc.vector, tok(DV))
    i = nc.vector.scalar_tensor_tensor(tempc[:, 1:2], tempc[:, 2:3], 2.0,
                                       ltemp_c[:, 0:1], AL.mult, AL.subtract)
    p.inc(i, DV)
    w8(nc.scalar, tok(DV))
    i = nc.scalar.activation(tempc[:, 3:4], tempc[:, 1:2], AF.Exp, scale=-1.0)
    p.inc(i, AC)
    w8(nc.scalar, tok(AC))
    i = nc.scalar.activation(tempc[:, 1:2], tempc[:, 3:4], AF.Ln, bias=1.0)
    p.inc(i, AC)
    w8(nc.vector, tok(AC))
    i = nc.vector.tensor_tensor(tempc[:, 2:3], tempc[:, 2:3], tempc[:, 1:2], AL.add)
    i = nc.vector.tensor_scalar(tempc[:, 2:3], tempc[:, 2:3], 1e-4, None, AL.add)
    i = nc.vector.reciprocal(tempc[:, 0:1], tempc[:, 2:3])
    p.inc(i, DV)
    temp_tok = tok(DV)

    g2_(fftFA[:], din["fft_faT"], "ldp")
    g2_(fftFB[:], din["fft_fbT"], "ldp")
    g2_(fftF[:], din["fft_fT"], "ldp")
    g2_(fftG1[:], din["fft_g1T"], "ldp")
    g2_(fftG2[:], din["fft_g2T"], "ldp")
    ldp = [tok("ldp"), tok("ldps")]
    w8(nc.vector, *ldp)
    i = nc.vector.memset(wpad[64:128, :], 0.0)
    p.inc(i, DV)
    wp_tok = tok(DV)
    for half in range(2):
        for mat, dst in ((fftFA, WA_b), (fftFB, WB_b)):
            bi, bank, btok = pp.acquire()
            w8(nc.tensor, btok, wp_tok, *ldp)
            mm = nc.tensor.matmul(bank, mat[:], wpad[:, half * 512:(half + 1) * 512],
                                  start=True, stop=True)
            p.inc(mm, PE)
            w8(nc.vector, tok(PE))
            i = nc.vector.tensor_copy(dst[:, half * 512:(half + 1) * 512], bank)
            p.inc(i, DV)
            pp.release(bi, tok(DV))
    wab_tok = tok(DV)

    w8(nc.vector, tok(DV))
    i = nc.vector.memset(selbank[:], 0.0)
    p.inc(i, DV)
    w8(nc.vector, tok(DV))
    i = nc.vector.memset(selbank[:, 128:129], 1.0 / 256.0)
    p.inc(i, DV)
    sel_tok = tok(DV)

    x1_store_toks = []
    st["absq_free"] = tok(DV)
    st["blk_free"] = ()

    def post_front(n):
        t0 = n * 512
        res = {}
        winS = [win_b[:, 0:1024], absq[:, 2048:3072]]
        PaS = [Pa_b[:, 0:1024], absq[:, 3072:4096]]
        PbS = [Pb_b[:, 0:1024], absq[:, 2048:3072]]
        wtok = {}
        ptok = {}
        dotoks = []
        lltoks = []
        lstoks = []

        def do_tr(ti):
            for jp in (0, 4):
                bi, bank, btok = pp.acquire()
                w8(nc.tensor, btok, s_tok)
                for j in range(jp, jp + 4):
                    mm = nc.tensor.matmul(
                        bank[:, (j - jp) * 128:(j - jp + 1) * 128],
                        do_tok[:, (4 * n + ti) * D + j * 128:(4 * n + ti) * D + (j + 1) * 128],
                        ident_b[:], start=True, stop=True)
                p.inc(mm, PE)
                w8(nc.scalar, tok(PE), *st["blk_free"])
                for j in range(jp, jp + 4):
                    i = nc.scalar.copy(do_blk[:, j * 512 + ti * 128:j * 512 + (ti + 1) * 128],
                                       bank[:, (j - jp) * 128:(j - jp + 1) * 128])
                p.inc(i, AC)
                pp.release(bi, tok(AC))
            dotoks.append(tok(AC))
            if len(dotoks) == 4:
                res["doT"] = dotoks[-1]

        def win_tr(s):
            par = s % 2
            b = 8 * n + s
            for jp in (0, 4):
                bi, bank, btok = pp.acquire()
                w8(nc.tensor, btok, wab_tok)
                for j in range(jp, jp + 4):
                    mm = nc.tensor.matmul(
                        bank[:, (j - jp) * 128:(j - jp + 1) * 128],
                        v_ch[:, j * (T + 64) + 64 * b:j * (T + 64) + 64 * b + 128],
                        ident_b[:], start=True, stop=True)
                p.inc(mm, PE)
                w8(nc.scalar, tok(PE), st.get("absq_free") if par else None,
                   *st["blk_free"])
                for j in range(jp, jp + 4):
                    i = nc.scalar.copy(winS[par][:, j * 128:(j + 1) * 128],
                                       bank[:, (j - jp) * 128:(j - jp + 1) * 128])
                p.inc(i, AC)
                pp.release(bi, tok(AC))
            wtok[s] = tok(AC)

        def fwd(s):
            par = s % 2
            for half in range(2):
                bi, bank, btok = pp.acquire()
                w8(nc.tensor, btok, wtok[s])
                mm = nc.tensor.matmul(bank, fftF[:], winS[par][:, half * 512:(half + 1) * 512],
                                      start=True, stop=True)
                p.inc(mm, PE)
                w8(nc.vector, tok(PE), wab_tok,
                   st.get("absq_free") if par else None, *st["blk_free"])
                i = nc.vector.tensor_tensor(PaS[par][:, half * 512:(half + 1) * 512],
                                            bank, WA_b[:, half * 512:(half + 1) * 512], AL.mult)
                i = nc.vector.tensor_tensor(PbS[par][:, half * 512:(half + 1) * 512],
                                            bank, WB_b[:, half * 512:(half + 1) * 512], AL.mult)
                p.inc(i, DV)
                pp.release(bi, tok(DV))
            ptok[s] = tok(DV)

        def inv(s):
            par = s % 2
            bi, bank, btok = pp.acquire()
            w8(nc.tensor, btok, ptok[s])
            for j in range(8):
                mm = nc.tensor.matmul(bank[:, j * 64:(j + 1) * 64],
                                      PaS[par][:, j * 128:(j + 1) * 128], fftG1[:],
                                      start=True, stop=False)
                mm = nc.tensor.matmul(bank[:, j * 64:(j + 1) * 64],
                                      PbS[par][:, j * 128:(j + 1) * 128], fftG2[:],
                                      start=False, stop=True)
            p.inc(mm, PE)
            w8(nc.vector, tok(PE), *st["blk_free"])
            i = nc.vector.tensor_copy(
                ll_blk[:].rearrange("p (j t) -> p j t", j=8)[:, :, s * 64:(s + 1) * 64],
                bank[:].rearrange("p (j t) -> p j t", j=8))
            p.inc(i, DV)
            lltoks.append(tok(DV))
            if len(lltoks) == 8:
                res["ll"] = lltoks[-1]
            pp.release(bi, tok(DV))

        def fs(j):
            w8(nc.vector, lds, tok(PE))
            for d in range(5):
                i = nc.vector.tensor_scalar(
                    fdiag[:, d * 128:(d + 1) * 128], ident_b[:],
                    fir_s_s[:, j * 5 + d:j * 5 + d + 1], None, AL.mult)
            p.inc(i, DV)
            fd_tok = tok(DV)
            bi, bank, btok = pp.acquire()
            w8(nc.tensor, btok, fd_tok)
            vb = j * (T + 64) + 64 + t0
            for d in range(5):
                mm = nc.tensor.matmul(bank, fdiag[:, d * 128:(d + 1) * 128],
                                      v_ch[:, vb + d - 4:vb + d - 4 + 512],
                                      start=(d == 0), stop=(d == 4))
            p.inc(mm, PE)
            w8(nc.scalar, tok(PE), *st["blk_free"])
            i = nc.scalar.copy(ls_blk[:, j * 512:(j + 1) * 512], bank)
            p.inc(i, AC)
            lstoks.append(tok(AC))
            if len(lstoks) == 8:
                res["ls"] = lstoks[-1]
            pp.release(bi, tok(AC))

        def xg_gx():
            xg_v = do_tok[:, 4 * n * D:4 * n * D + 8 * 512]
            w8(nc.sync, res["doT"])
            for sp in ("xsp0", "xsp1"):
                if sp in p.sems:
                    p.wait_val(nc.sync, sp, p.semv[sp])
            p.inc(nc.sync.dma_start(
                out=xg_v.rearrange("p (k t) -> p k t", k=8),
                in_=p.d_xch.rearrange("p (k t) -> p k t", k=8)[:, :, t0:t0 + 512]),
                f"xg{n % 2}", DMA)
            xg_tok = (f"xg{n % 2}", p.semv[f"xg{n % 2}"])
            for m in range(8):
                bi, bank, btok = pp.acquire()
                w8(nc.tensor, btok, xg_tok, *ldp)
                for k in range(8):
                    mm = nc.tensor.matmul(bank, W1xT_b[:, (k * 8 + m) * 128:(k * 8 + m + 1) * 128],
                                          xg_v[:, k * 512:(k + 1) * 512],
                                          start=(k == 0), stop=(k == 7))
                p.inc(mm, PE)
                w8(nc.vector, tok(PE))
                i = nc.vector.tensor_copy(gx_b[:, m * 512:(m + 1) * 512], bank)
                p.inc(i, DV)
                pp.release(bi, tok(DV))
            res["gx"] = tok(DV)

        stages = [
            lambda: (do_tr(0), do_tr(1)),
            lambda: (do_tr(2), do_tr(3), win_tr(0), win_tr(1)),
            lambda: (xg_gx(), fwd(0), fs(0), fs(1)),
            lambda: (inv(0), win_tr(2), fwd(1), fs(2), fs(3)),
            lambda: (fwd(2), win_tr(4), inv(1), win_tr(3), fs(4), fs(5)),
            lambda: (inv(2), fwd(4), win_tr(6), fwd(3), inv(3), win_tr(5),
                     fs(6), fs(7)),
            lambda: (inv(4), fwd(6), fwd(5), inv(5), win_tr(7)),
            lambda: (inv(6), fwd(7), inv(7)),
        ]
        return stages, res

    st["pfill"] = []

    def pfill(k=1):
        for _ in range(k):
            if st["pfill"]:
                st["pfill"].pop(0)()

    pf, pf_res = post_front(0)
    for f_ in pf:
        f_()
    for n in range(4):
        t0 = n * 512
        doT_tok = pf_res["doT"]
        ll_tok = pf_res["ll"]
        ls_tok = pf_res["ls"]
        gx_tok = pf_res["gx"]
        if n < 4 - 1:
            st["pfill"], pf_res = post_front(n + 1)
        else:
            st["pfill"], pf_res = [], None
        if n == 0:
            p.dump("ls_b0", ls_blk[:], (128, 8 * 512), BF16)
            p.dump("ll_b0", ll_blk[:], (128, 8 * 512), BF16)
            p.dump("do_b0", do_blk[:], (128, 8 * 512), BF16)
        srcs = [("ls", ls_blk, ls_tok), ("ll", ll_blk, ll_tok),
                ("do", do_blk, doT_tok), ("v", None, None)]
        vblk = v_ch[:].rearrange("p (j t) -> p j t", j=8)[:, :, 64 + t0:64 + t0 + 512]
        kindmm = []
        slot_mm = [None] * 4
        ent = [tok(PE), tok(AC), tok(DV), tok(PL)]
        for kind in range(3):
            bi, bank, btok = pp.acquire()
            bview = bank
            first = True
            for si, (sname, stens, stok_) in enumerate(srcs):
                for j in range(8):
                    h = j // 2
                    c = h * 4 + si
                    if kind == 0:
                        rhs = (vblk[:, j, :] if stens is None
                               else stens[:, j * 512:(j + 1) * 512])
                        dep = stok_ if stok_ else s_tok
                        w8(nc.tensor, btok, dep, sel_tok)
                    else:
                        sl_ = (si * 8 + j) % 4
                        slv = absq[:, sl_ * 512:(sl_ + 1) * 512]
                        srcv = (vblk[:, j, :] if stens is None
                                else stens[:, j * 512:(j + 1) * 512])
                        ew = [stok_ if stok_ else s_tok, slot_mm[sl_]]
                        if slot_mm[sl_] is None:
                            ew += ent
                        if kind == 1:
                            w8(nc.scalar, *ew)
                            i = nc.scalar.activation(slv, srcv, AF.Abs)
                            p.inc(i, AC)
                            bdep = tok(AC)
                        else:
                            w8(nc.vector, *ew)
                            i = nc.vector.tensor_tensor(slv, srcv, srcv, AL.mult)
                            p.inc(i, DV)
                            bdep = tok(DV)
                        rhs = slv
                        w8(nc.tensor, btok, bdep)
                    mm = nc.tensor.matmul(bview, selbank[:, 128 - c:256 - c], rhs,
                                          start=first, stop=(si == 3 and j == 7))
                    first = False
                    if kind != 0:
                        p.inc(mm, PE)
                        slot_mm[(si * 8 + j) % 4] = tok(PE)
            if kind == 0:
                p.inc(mm, PE)
            w8(nc.vector, tok(PE))
            i = nc.vector.tensor_copy(statsp[:, kind * 512:(kind + 1) * 512], bview)
            p.inc(i, DV)
            pp.release(bi, tok(DV))
            kindmm.append(tok(DV))
        for ti in range(4):
            bi, bank, btok = pp.acquire(cols=192)
            w8(nc.tensor, btok, kindmm[-1])
            for kind in range(3):
                mm = nc.tensor.matmul(bank[:, kind * 64:kind * 64 + 16],
                                      statsp[0:16, kind * 512 + ti * 128:kind * 512 + (ti + 1) * 128],
                                      ident_f[0:16, 0:16], start=True, stop=True)
            p.inc(mm, PE)
            w8(nc.vector, tok(PE))
            i = nc.vector.tensor_copy(statstok[:, 0 + ti * 16:16 + ti * 16], bank[:, 0:16])
            i = nc.vector.tensor_copy(statstok[:, 128 + ti * 16:144 + ti * 16], bank[:, 64:80])
            i = nc.vector.tensor_copy(statstok[:, 256 + ti * 16:272 + ti * 16], bank[:, 128:144])
            p.inc(i, DV)
            pp.release(bi, tok(DV))
        w8(nc.vector, tok(DV))
        i = nc.vector.tensor_tensor(statstok[:, 64:128], statstok[:, 0:64],
                                    statstok[:, 0:64], AL.mult)
        p.inc(i, DV)
        w8(nc.vector, tok(DV))
        i = nc.vector.tensor_tensor(statstok[:, 64:128], statstok[:, 256:320],
                                    statstok[:, 64:128], AL.subtract)
        p.inc(i, DV)
        w8(nc.scalar, tok(DV))
        i = nc.scalar.activation(sqscr[:, 0:64], statstok[:, 256:320],
                                 AF.Ln, scale=256.0)
        p.inc(i, AC)
        w8(nc.scalar, tok(AC))
        i = nc.scalar.activation(statstok[:, 192:256], sqscr[:, 0:64],
                                 AF.Exp, scale=0.5)
        p.inc(i, AC)
        der_tok = tok(AC)
        w8(nc.vector, der_tok)
        for ti in range(4):
            i = nc.vector.tensor_copy(
                sqscr[:, 64 + 64 * ti:128 + 64 * ti].rearrange(
                    "p (h k s) -> p h k s", h=4, k=4, s=4),
                statstok[:].rearrange("p (k t h s) -> p t h k s",
                                      k=5, t=4, h=4, s=4)[:, ti, :, 0:4, :])
            p.inc(i, DV)
        der_tok = tok(DV)
        for ti in range(4):
            for h in range(H):
                bi, bank, btok = pp.acquire(cols=128)
                view = sqscr[:, 64 + 64 * ti + 16 * h:80 + 64 * ti + 16 * h]
                w8(nc.tensor, btok, der_tok)
                mm = nc.tensor.matmul(bank[0:16, :], view, ident_f[:],
                                      start=True, stop=True)
                p.inc(mm, PE)
                w8(nc.vector, tok(PE))
                sa = stats_allA if h < 2 else stats_allB
                i = nc.vector.tensor_copy(sa[32 * (h % 2):32 * (h % 2) + 16,
                                             ti * 128:(ti + 1) * 128], bank[0:16, :])
                p.inc(i, DV)
                pp.release(bi, tok(DV))
        stats_tok = tok(DV)
        if n == 0:
            p.dump("statsA_b0", stats_allA[0:16, :], (16, 512), BF16)
        lgmm_tok = None
        for h in range(H):
            Wsa = W1s4A if h < 2 else W1s4B
            sa = stats_allA if h < 2 else stats_allB
            mbanks = []
            for m in range(8):
                bi, bank, btok = pp.acquire()
                w8(nc.tensor, btok, stats_tok, *ldp)
                mm = nc.tensor.matmul(bank, Wsa[32 * (h % 2):32 * (h % 2) + 16,
                                              m * 128:(m + 1) * 128],
                                      sa[32 * (h % 2):32 * (h % 2) + 16, :],
                                      start=True, stop=True)
                p.inc(mm, PE)
                mbanks.append((bi, bank, tok(PE)))
            adds = []
            for m, (bi, bank, pet) in enumerate(mbanks):
                w8(nc.vector, pet, gx_tok, lgmm_tok)
                i = nc.vector.tensor_tensor(hdn_b[:, m * 512:(m + 1) * 512], bank,
                                            gx_b[:, m * 512:(m + 1) * 512], AL.add)
                p.inc(i, DV)
                adds.append(tok(DV))
                pp.release(bi, tok(DV))
            for m in range(8):
                w8(nc.scalar, adds[m])
                i = nc.scalar.activation(hdn_b[:, m * 512:(m + 1) * 512],
                                         hdn_b[:, m * 512:(m + 1) * 512],
                                         AF.Gelu, bias=b1_b[:, m:m + 1])
                p.inc(i, AC)
            hdn_tok = tok(AC)
            for ti in range(4):
                bi, bank, btok = pp.acquire(cols=4)
                w8(nc.tensor, btok, hdn_tok, *ldp)
                for k in range(8):
                    mm = nc.tensor.matmul(bank, hdn_b[:, k * 512 + ti * 128:k * 512 + (ti + 1) * 128],
                                          W2T_b[:, k * 4:(k + 1) * 4],
                                          start=(k == 0), stop=False)
                mm = nc.tensor.matmul(bank, ones_row[:, 0:128], b2_b[:],
                                      start=False, stop=True)
                p.inc(mm, PE)
                w8(nc.vector, tok(PE))
                i = nc.vector.tensor_copy(lgt[:, ti * 16 + h * 4:ti * 16 + (h + 1) * 4], bank)
                p.inc(i, DV)
                pp.release(bi, tok(DV))
            lgmm_tok = tok(PE)
        lg_tok = tok(DV)
        tr_prev = {}
        for ti in range(4):
            par = ti % 2
            for h in range(H):
                c4 = ti * 16 + h * 4
                s3 = (ti * 4 + h) * 3
                dst = cmbc[:, par * 16 + h * 4:par * 16 + h * 4 + 4]
                w8(nc.vector, lg_tok, temp_tok, tr_prev.get(par))
                i = nc.vector.reduce_max(fwc[:, s3:s3 + 1], lgt[:, c4:c4 + 4],
                                         axis=AX.X, negate=True)
                i = nc.vector.tensor_scalar(lgt[:, c4:c4 + 4], lgt[:, c4:c4 + 4],
                                            fwc[:, s3:s3 + 1], tempc[:, 0:1],
                                            AL.add, AL.mult)
                p.inc(i, DV)
                w8(nc.scalar, tok(DV))
                i = nc.scalar.activation(lgt[:, c4:c4 + 4], lgt[:, c4:c4 + 4], AF.Exp,
                                         accum_out=fwc[:, s3 + 1:s3 + 2])
                p.inc(i, AC)
                w8(nc.vector, tok(AC))
                i = nc.vector.reciprocal(fwc[:, s3 + 2:s3 + 3], fwc[:, s3 + 1:s3 + 2])
                i = nc.vector.tensor_scalar(dst, lgt[:, c4:c4 + 4],
                                            fwc[:, s3 + 2:s3 + 3], None, AL.mult)
                i = nc.vector.tensor_tensor(dst[:, 0:1], dst[:, 0:1],
                                            ress_c[:, 0:1], AL.add)
                i = nc.vector.tensor_tensor(dst[:, 1:2], dst[:, 1:2],
                                            resl_c[:, 0:1], AL.add)
                p.inc(i, DV)
            cmb_dv = tok(DV)
            bi, bank, btok = pp.acquire(cols=128)
            w8(nc.tensor, btok, cmb_dv)
            mm = nc.tensor.matmul(bank[0:16, :], cmbc[:, par * 16:par * 16 + 16],
                                  ident_b[:], start=True, stop=True)
            p.inc(mm, PE)
            tr_prev[par] = tok(PE)
            w8(nc.vector, tok(PE))
            i = nc.vector.tensor_copy(cmbrow[0:16, ti * 128:(ti + 1) * 128], bank[0:16, :])
            p.inc(i, DV)
            pp.release(bi, tok(DV))
        cmb_tok = tok(DV)
        if n == 0:
            p.dump("cmbrow_b0", cmbrow[:], (16, 512), BF16)
        oproj_prev = st.get("oproj_pe", ld_dv)
        mix_pl = None
        for h in range(H):
            for src_ in range(4):
                bi, bank, btok = pp.acquire()
                w8(nc.tensor, btok, cmb_tok)
                r = h * 4 + src_
                mm = nc.tensor.matmul(bank, selrows_b[0:16, r * 128:(r + 1) * 128],
                                      cmbrow[0:16, :], start=True, stop=True)
                p.inc(mm, PE)
                w8(nc.vector, tok(PE), tok(PL))
                i = nc.vector.tensor_copy(cmbb4[:, src_ * 512:(src_ + 1) * 512], bank)
                p.inc(i, DV)
                pp.release(bi, tok(DV))
            bc_tok = tok(DV)
            w8(nc.gpsimd, bc_tok, ls_tok, ll_tok, doT_tok, oproj_prev, tok(AC))
            w8(nc.vector, bc_tok, ls_tok, ll_tok, doT_tok, oproj_prev, tok(AC))
            je, jo = 2 * h, 2 * h + 1
            jse = slice(je * 512, (je + 1) * 512)
            jso = slice(jo * 512, (jo + 1) * 512)
            steps = [
                lambda e, js, j: e.tensor_tensor(oroll[:, js], ls_blk[:, js],
                                                 cmbb4[:, 0:512], AL.mult),
                lambda e, js, j: e.tensor_tensor(absq[:, js], ll_blk[:, js],
                                                 cmbb4[:, 512:1024], AL.mult),
                lambda e, js, j: e.tensor_tensor(oroll[:, js], oroll[:, js],
                                                 absq[:, js], AL.add),
                lambda e, js, j: e.tensor_tensor(absq[:, js], do_blk[:, js],
                                                 cmbb4[:, 1024:1536], AL.mult),
                lambda e, js, j: e.tensor_tensor(oroll[:, js], oroll[:, js],
                                                 absq[:, js], AL.add),
                lambda e, js, j: e.tensor_tensor(absq[:, js], vblk[:, j, :],
                                                 cmbb4[:, 1536:2048], AL.mult),
                lambda e, js, j: e.tensor_tensor(oroll[:, js], oroll[:, js],
                                                 absq[:, js], AL.add),
            ]
            for f_ in steps:
                i = f_(nc.vector, jse, je)
                p.inc(i, DV)
                i = f_(nc.gpsimd, jso, jo)
                p.inc(i, PL)
            mix_pl = tok(PL)
        mix_tok = tok(DV)
        w8(nc.vector, mix_pl)
        st["blk_free"] = (mix_tok, mix_pl)
        if n == 0:
            p.dump("omix_b0", oroll[:], (128, 8 * 512), BF16)
        w8(nc.vector, mix_tok)
        i = nc.vector.tensor_tensor(absq[:], oroll[:], oroll[:], AL.mult)
        p.inc(i, DV)
        osq_tok = tok(DV)
        bi, bank, btok = pp.acquire()
        w8(nc.tensor, btok, osq_tok, sel_tok)
        for j in range(8):
            c = j // 2
            mm = nc.tensor.matmul(bank[0:16, :], selbank[:, 128 - c:144 - c],
                                  absq[:, j * 512:(j + 1) * 512],
                                  start=(j == 0), stop=(j == 7))
        p.inc(mm, PE)
        st["absq_free"] = tok(PE)
        pfill(2)
        w8(nc.scalar, tok(PE))
        i = nc.scalar.activation(sqscr[0:16, 0:512],
                                 bank[0:16, :], AF.Ln, bias=epscol[0:16, 0:1])
        p.inc(i, AC)
        w8(nc.scalar, tok(AC))
        i = nc.scalar.activation(rstdrow[:], sqscr[0:16, 0:512], AF.Exp, scale=-0.5)
        p.inc(i, AC)
        rstd_tok = tok(AC)
        pp.release(bi, rstd_tok)
        if n == 0:
            p.dump("rstd_b0", rstdrow[:], (16, 512), BF16)
        for h in range(H):
            bi, bank, btok = pp.acquire()
            w8(nc.tensor, btok, rstd_tok)
            mm = nc.tensor.matmul(bank, selrows_b[0:16, h * 128:(h + 1) * 128],
                                  rstdrow[0:16, :], start=True, stop=True)
            p.inc(mm, PE)
            w8(nc.vector, tok(PE), tok(PL))
            i = nc.vector.tensor_copy(cmbb4[:, 0:512], bank)
            p.inc(i, DV)
            rb_tok = tok(DV)
            pp.release(bi, rb_tok)
            for j in (2 * h, 2 * h + 1):
                w8(nc.vector, rb_tok)
                i = nc.vector.scalar_tensor_tensor(oroll[:, j * 512:(j + 1) * 512],
                                                   oroll[:, j * 512:(j + 1) * 512],
                                                   onorm_b[:, j:j + 1],
                                                   cmbb4[:, 0:512], AL.mult, AL.mult)
                p.inc(i, DV)
                o_tok = tok(DV)
            pfill(1)
        p.dump(f"o_blk{n}", oroll[:], (128, 8 * 512), BF16)
        for ti in range(4):
            pfill(1)
            w8(nc.sync, st["x1t_free"][0])
            p.inc(nc.sync.dma_start(out=x1t[:],
                                    in_=din["x_own"][t0 + ti * 128:t0 + (ti + 1) * 128, :]),
                  "xf0", DMA)
            xf_tok = ("xf0", p.semv["xf0"])
            banks = []
            for half in range(2):
                bi, bank, btok = pp.acquire()
                w8(nc.tensor, btok, o_tok, *ldp)
                for k in range(8):
                    mm = nc.tensor.matmul(bank,
                                          oroll[:, k * 512 + ti * 128:k * 512 + (ti + 1) * 128],
                                          oprojT_b[:, k * D + half * 512:k * D + (half + 1) * 512],
                                          start=(k == 0), stop=(k == 7))
                p.inc(mm, PE)
                banks.append((bi, bank, tok(PE)))
            st["oproj_pe"] = tok(PE)
            x1v = x1t[:]
            for half, (bi, bank, pet) in enumerate(banks):
                w8(nc.vector, pet, xf_tok)
                i = nc.vector.tensor_tensor(x1v[:, half * 512:(half + 1) * 512],
                                            x1v[:, half * 512:(half + 1) * 512],
                                            bank, AL.add)
                p.inc(i, DV)
                pp.release(bi, tok(DV))
            w8(nc.vector, tok(DV))
            i = nc.vector.reduce_sum(lncol[:, 0:1], x1v, axis=AX.X)
            p.inc(i, DV)
            w8(nc.scalar, tok(DV))
            i = nc.scalar.activation(sqscr[:], x1v, AF.Square, accum_out=lncol[:, 1:2])
            p.inc(i, AC)
            w8(nc.vector, tok(AC))
            i = nc.vector.tensor_scalar(lncol[:, 2:3], lncol[:, 0:1], 1.0 / D, None, AL.mult)
            p.inc(i, DV)
            w8(nc.vector, tok(DV))
            i = nc.vector.tensor_tensor(lncol[:, 3:4], lncol[:, 2:3], lncol[:, 2:3], AL.mult)
            i = nc.vector.tensor_scalar(lncol[:, 4:5], lncol[:, 1:2], 1.0 / D, None, AL.mult)
            p.inc(i, DV)
            w8(nc.vector, tok(DV))
            i = nc.vector.tensor_tensor(lncol[:, 4:5], lncol[:, 4:5], lncol[:, 3:4], AL.subtract)
            p.inc(i, DV)
            w8(nc.scalar, tok(DV))
            i = nc.scalar.activation(lncol[:, 5:6], lncol[:, 4:5], AF.Ln, bias=epscol[:, 0:1])
            p.inc(i, AC)
            w8(nc.scalar, tok(AC))
            i = nc.scalar.activation(lncol[:, 6:7], lncol[:, 5:6], AF.Exp, scale=-0.5)
            p.inc(i, AC)
            w8(nc.vector, tok(AC))
            i = nc.vector.tensor_scalar(x1v, x1v, lncol[:, 2:3], lncol[:, 6:7],
                                        AL.subtract, AL.mult)
            p.inc(i, DV)
            x1_tok = tok(DV)
            w8(nc.sync, x1_tok)
            p.inc(nc.sync.dma_start(out=p.d_x1[t0 + ti * 128:t0 + (ti + 1) * 128, :],
                                    in_=x1v), "x1sp", DMA)
            st["x1t_free"][0] = ("x1sp", p.semv["x1sp"])
        if n == 0:
            p.dump("x1_b0", x1t[:, 0:D], (128, D))
    p.dump("x1d", p.d_x1[:], (T, D))
    if stop_after == "post":
        p.finish()
        return

        while st["pfill"]:
            st["pfill"].pop(0)()

    x1_ch = p.sb("x1_ch", (128, 8 * T), BF16, offset=R_A)
    ffn_ch = p.sb("ffn_ch", (128, 8 * T), BF16, offset=R_C)
    Wg_b = p.sb("Wg_b", (128, 8 * D), BF16, offset=R_W)
    Web = [p.sb(f"We{i}", (128, 8 * D), BF16, offset=R_W + 16384 * (i + 1)) for i in range(2)]
    fr = R_DR
    iota_t = p.sb("iota_t", (128, D), F32, offset=fr); fr += 4096
    lgr = p.sb("lgr", (128, D), F32, offset=fr); fr += 4096
    escr = p.sb("escr", (128, D), F32, offset=fr); fr += 4096
    x1m = p.sb("x1m", (128, 2 * 8 * 512), BF16, offset=fr); fr += 16384
    mpe8 = p.sb("mpe8", (128, 8 * 512), BF16, offset=fr); fr += 8192
    rcol = p.sb("rcol", (128, 16), F32, offset=fr); fr += 64
    me8 = p.sb("me8", (128, 8), F32, offset=fr); fr += 32
    mp8 = p.sb("mp8", (128, 8), BF16, offset=fr); fr += 32
    xb_t = p.sb("xb_t", (128, D), BF16, offset=fr); fr += 2048
    ffnb2 = p.sb("ffnb2", (128, 2 * D), BF16, offset=fr); fr += 4096
    x1t2 = p.sb("x1t2", (128, 2 * D), F32, offset=fr); fr += 8192
    sq2 = p.sb("sq2", (128, 2 * D), F32, offset=fr); fr += 8192
    lncol2 = p.sb("lncol2", (128, 16), F32, offset=fr); fr += 64
    be_b = p.sb("be_b", (8, D), BF16, offset=CL_W1S4)
    assert fr <= s_off, (fr, s_off)
    mpbuf = cmbrow

    w8(nc.gpsimd, tok(PE), tok(DV), tok(AC), tok(PL))
    w8(nc.sync, tok(PE), tok(DV), tok(AC), tok(PL))
    for sp in ("x1sp",):
        p.wait_val(nc.gpsimd, sp, p.semv[sp])
        p.wait_val(nc.sync, sp, p.semv[sp])
    g3 = lambda dst, src, sem: p.inc(nc.gpsimd.dma_start(out=dst, in_=src), sem, DMA)
    g3(Wg_b[:].rearrange("p (k co) -> p k co", k=8),
       din["Wg"].rearrange("(k p) co -> p k co", p=128), "ldf")
    g3(iota_t[:], din["iota_row"], "ldf")
    g3(be_b[:], din["be"], "ldf")
    bgrow_b = p.sb("bgrow_b", (1, D), BF16, offset=CL_W1S4 + 2048)
    g3(bgrow_b[:], din["bgrow"], "ldf")
    ldf = tok("ldf")

    g3(Web[0][:].rearrange("p (k f) -> p k f", k=8),
       din["We"][0].rearrange("(k p) f -> p k f", p=128), "we0")
    we_tok = {0: ("we0", p.semv["we0"])}
    mpv = [cmbrow, rstdrow, stats_allA, stats_allB]
    mp_all = [None] * 4
    x1c_toks = [None] * NT
    st["ffnb_free"] = {0: None, 1: None}

    def router_block(n):
        t0 = n * 512
        mpdst = mpv[n]
        for ti in range(4):
            tt0 = t0 + ti * 128
            banks = []
            for half in range(2):
                bi, bank, btok = pp.acquire()
                w8(nc.tensor, btok, x1c_toks[4 * n + ti], ldf)
                for k in range(8):
                    mm = nc.tensor.matmul(bank,
                                          x1_ch[:, k * T + tt0:k * T + tt0 + 128],
                                          Wg_b[:, k * D + half * 512:k * D + (half + 1) * 512],
                                          start=(k == 0), stop=False)
                mm = nc.tensor.matmul(bank, ones_row[:, 0:128],
                                      bgrow_b[:, half * 512:(half + 1) * 512],
                                      start=False, stop=True)
                p.inc(mm, PE)
                banks.append((bi, bank, tok(PE)))
            for half, (bi, bank, pet) in enumerate(banks):
                w8(nc.vector, pet)
                i = nc.vector.tensor_copy(lgr[:, half * 512:(half + 1) * 512], bank)
                p.inc(i, DV)
                pp.release(bi, tok(DV))
            i = nc.vector.reduce_max(rcol[:, 0:1], lgr[:], axis=AX.X, negate=True)
            p.inc(i, DV)
            w8(nc.scalar, tok(DV))
            i = nc.scalar.activation(escr[:], lgr[:], AF.Exp, bias=rcol[:, 0:1],
                                     accum_out=rcol[:, 1:2])
            p.inc(i, AC)
            w8(nc.vector, tok(AC), ldf)
            i = nc.vector.reciprocal(rcol[:, 2:3], rcol[:, 1:2])
            i = nc.vector.tensor_scalar(lgr[:], lgr[:], rcol[:, 0:1], -1e12,
                                        AL.add, AL.mult)
            i = nc.vector.tensor_tensor(lgr[:], lgr[:], iota_t[:], AL.add)
            i = nc.vector.tensor_reduce(me8[:].rearrange("p (e a) -> p e a", a=1),
                                        lgr[:].rearrange("p (a e) -> p e a", e=E),
                                        axis=AX.X, op=AL.min)
            i = nc.vector.tensor_reduce(rcol[:, 3:4], me8[:], axis=AX.X, op=AL.min)
            i = nc.vector.tensor_scalar(me8[:], me8[:], rcol[:, 3:4], None, AL.subtract)
            i = nc.vector.tensor_scalar_min(me8[:], me8[:], 1.0)
            i = nc.vector.tensor_scalar(me8[:], me8[:], -1.0, 1.0, AL.mult, AL.add)
            i = nc.vector.tensor_scalar(mp8[:], me8[:], rcol[:, 2:3], None, AL.mult)
            p.inc(i, DV)
            mp_tok = tok(DV)
            bi, bank, btok = pp.acquire(cols=128)
            w8(nc.tensor, btok, mp_tok)
            mm = nc.tensor.matmul(bank[0:8, :], mp8[:], ident_b[:], start=True, stop=True)
            p.inc(mm, PE)
            w8(nc.vector, tok(PE))
            i = nc.vector.tensor_copy(mpdst[0:8, ti * 128:(ti + 1) * 128], bank[0:8, :])
            p.inc(i, DV)
            pp.release(bi, tok(DV))
        mp_all[n] = tok(DV)

    for ti in range(NT):
        sl = ti % 2
        w8(nc.sync, st["x1t_free"].get(sl))
        p.inc(nc.sync.dma_start(out=x1t2[:, sl * D:(sl + 1) * D],
                                in_=p.d_x1[ti * 128:(ti + 1) * 128, :]),
              f"xf{sl}", DMA)
        xf = (f"xf{sl}", p.semv[f"xf{sl}"])
        w8(nc.vector, xf, tok(PE))
        i = nc.vector.tensor_copy(xb_t[:], x1t2[:, sl * D:(sl + 1) * D])
        p.inc(i, DV)
        cast_tok = tok(DV)
        for jp in (0, 4):
            bi, bank, btok = pp.acquire()
            w8(nc.tensor, btok, cast_tok)
            for j in range(jp, jp + 4):
                mm = nc.tensor.matmul(bank[:, (j - jp) * 128:(j - jp + 1) * 128],
                                      xb_t[:, j * 128:(j + 1) * 128],
                                      ident_b[:], start=True, stop=True)
            p.inc(mm, PE)
            w8(nc.scalar, tok(PE))
            for j in range(jp, jp + 4):
                i = nc.scalar.copy(x1_ch[:, j * T + ti * 128:j * T + (ti + 1) * 128],
                                   bank[:, (j - jp) * 128:(j - jp + 1) * 128])
            p.inc(i, AC)
            pp.release(bi, tok(AC))
        x1c_toks[ti] = tok(AC)
        st["x1t_free"][sl] = cast_tok
        if ti % 4 == 3:
            router_block(ti // 4)

    def ln2_group(n, ffn_dep):
        x1s = lambda sl: x1t2[:, sl * D:(sl + 1) * D]
        for g in range(0, 4, 2):
            tis = (4 * n + g, 4 * n + g + 1)
            xfs = {}
            for t in tis:
                sl = t % 2
                w8(nc.sync, st["x1t_free"].get(sl))
                p.inc(nc.sync.dma_start(out=x1s(sl),
                                        in_=p.d_x1[t * 128:(t + 1) * 128, :]),
                      f"xf{sl}", DMA)
                xfs[sl] = (f"xf{sl}", p.semv[f"xf{sl}"])
            fbt = {}
            for t in tis:
                sl = t % 2
                for jp in (0, 4):
                    bi, bank, btok = pp.acquire()
                    w8(nc.tensor, btok, ffn_dep)
                    for j in range(jp, jp + 4):
                        mm = nc.tensor.matmul(bank[:, (j - jp) * 128:(j - jp + 1) * 128],
                                              ffn_ch[:, j * T + t * 128:j * T + (t + 1) * 128],
                                              ident_b[:], start=True, stop=True)
                    p.inc(mm, PE)
                    w8(nc.scalar, tok(PE), st["ffnb_free"].get(sl))
                    for j in range(jp, jp + 4):
                        i = nc.scalar.copy(ffnb2[:, sl * D + j * 128:sl * D + (j + 1) * 128],
                                           bank[:, (j - jp) * 128:(j - jp + 1) * 128])
                    p.inc(i, AC)
                    pp.release(bi, tok(AC))
                fbt[sl] = tok(AC)
            add_t = {}
            for t in tis:
                sl = t % 2
                w8(nc.vector, fbt[sl], xfs[sl])
                i = nc.vector.tensor_tensor(x1s(sl), x1s(sl),
                                            ffnb2[:, sl * D:(sl + 1) * D], AL.add)
                p.inc(i, DV)
                add_t[sl] = tok(DV)
                st["ffnb_free"][sl] = add_t[sl]
            for t in tis:
                sl = t % 2
                i = nc.vector.reduce_sum(lncol2[:, sl * 8:sl * 8 + 1], x1s(sl), axis=AX.X)
                p.inc(i, DV)
            sq_t = {}
            for t in tis:
                sl = t % 2
                w8(nc.scalar, add_t[sl])
                i = nc.scalar.activation(sq2[:, sl * D:(sl + 1) * D], x1s(sl),
                                         AF.Square, accum_out=lncol2[:, sl * 8 + 1:sl * 8 + 2])
                p.inc(i, AC)
                sq_t[sl] = tok(AC)
            for t in tis:
                sl = t % 2
                c = lncol2[:, sl * 8:sl * 8 + 8]
                w8(nc.vector, sq_t[sl])
                i = nc.vector.tensor_scalar(c[:, 2:3], c[:, 0:1], 1.0 / D, None, AL.mult)
                i = nc.vector.tensor_tensor(c[:, 3:4], c[:, 2:3], c[:, 2:3], AL.mult)
                i = nc.vector.tensor_scalar(c[:, 4:5], c[:, 1:2], 1.0 / D, None, AL.mult)
                i = nc.vector.tensor_tensor(c[:, 4:5], c[:, 4:5], c[:, 3:4], AL.subtract)
                p.inc(i, DV)
            ln_t = {}
            for t in tis:
                sl = t % 2
                c = lncol2[:, sl * 8:sl * 8 + 8]
                w8(nc.scalar, tok(DV))
                i = nc.scalar.activation(c[:, 5:6], c[:, 4:5], AF.Ln, bias=epscol[:, 0:1])
                i = nc.scalar.activation(c[:, 6:7], c[:, 5:6], AF.Exp, scale=-0.5)
                p.inc(i, AC)
                ln_t[sl] = tok(AC)
            for t in tis:
                sl = t % 2
                c = lncol2[:, sl * 8:sl * 8 + 8]
                w8(nc.vector, ln_t[sl])
                i = nc.vector.tensor_scalar(x1s(sl), x1s(sl), c[:, 2:3], c[:, 6:7],
                                            AL.subtract, AL.mult)
                p.inc(i, DV)
                y_tok = tok(DV)
                w8(nc.sync, y_tok)
                p.inc(nc.sync.dma_start(out=p.y[t * 128:(t + 1) * 128, :], in_=x1s(sl)),
                      f"yst{sl}", DMA)
                st["x1t_free"][sl] = (f"yst{sl}", p.semv[f"yst{sl}"])

    prev_mm = tok(PE)
    mm_tok2 = [prev_mm, prev_mm]
    for n in range(4):
        t0 = n * 512
        for e in range(E):
            bi, bank, btok = pp.acquire()
            w8(nc.tensor, btok, mp_all[n])
            mm = nc.tensor.matmul(bank, selrows_b[0:8, e * 128:(e + 1) * 128],
                                  mpv[n][0:8, :], start=True, stop=True)
            p.inc(mm, PE)
            w8(nc.vector, tok(PE))
            i = nc.vector.tensor_copy(mpe8[:, e * 512:(e + 1) * 512], bank)
            p.inc(i, DV)
            pp.release(bi, tok(DV))
        mpe_tok = tok(DV)
        ebanks = []
        for f in range(8):
            bi, bank, btok = pp.acquire()
            ebanks.append([bi, bank, btok])
        for e in range(E):
            nxt = (e + 1) % E
            w8(nc.gpsimd, prev_mm)
            g3(Web[(e + 1) % 2][:].rearrange("p (k f) -> p k f", k=8),
               din["We"][nxt].rearrange("(k p) f -> p k f", p=128),
               f"we{(e + 1) % 2}")
            we_tok[(e + 1) % 2] = (f"we{(e + 1) % 2}", p.semv[f"we{(e + 1) % 2}"])
            xs = (e % 2) * 8 * 512
            w8(nc.vector, mpe_tok, x1c_toks[-1], mm_tok2[e % 2])
            w8(nc.gpsimd, mpe_tok, x1c_toks[-1], mm_tok2[e % 2])
            for k in range(5):
                i = nc.vector.tensor_tensor(x1m[:, xs + k * 512:xs + (k + 1) * 512],
                                            x1_ch[:, k * T + t0:k * T + t0 + 512],
                                            mpe8[:, e * 512:(e + 1) * 512], AL.mult)
            p.inc(i, DV)
            for k in range(5, 8):
                i = nc.gpsimd.tensor_tensor(x1m[:, xs + k * 512:xs + (k + 1) * 512],
                                            x1_ch[:, k * T + t0:k * T + t0 + 512],
                                            mpe8[:, e * 512:(e + 1) * 512], AL.mult)
            p.inc(i, PL)
            xm_tok = tok(DV)
            xm_tok2 = tok(PL)
            for f in range(8):
                bi, bank, btok = ebanks[f]
                w8(nc.tensor, btok, xm_tok, xm_tok2, we_tok[e % 2])
                for k in range(8):
                    mm = nc.tensor.matmul(bank,
                                          Web[e % 2][:, (k * 8 + f) * 128:(k * 8 + f + 1) * 128],
                                          x1m[:, xs + k * 512:xs + (k + 1) * 512],
                                          start=(e == 0 and k == 0), stop=False)
                ebanks[f] = [bi, bank, None]
            p.inc(mm, PE)
            prev_mm = tok(PE)
            mm_tok2[e % 2] = prev_mm
        for f in range(8):
            bi, bank, _ = ebanks[f]
            w8(nc.tensor, mp_all[n], ldf)
            mm = nc.tensor.matmul(bank, be_b[0:8, f * 128:(f + 1) * 128],
                                  mpv[n][0:8, :], start=False, stop=True)
            p.inc(mm, PE)
            w8(nc.scalar, tok(PE))
            i = nc.scalar.activation(ffn_ch[:, f * T + t0:f * T + t0 + 512], bank,
                                     AF.Relu)
            p.inc(i, AC)
            pp.release(bi, tok(AC))
        ln2_group(n, tok(AC))
    p.dump("ffn0", ffn_ch[:, 0:512], (128, 512), BF16)
    for sl in (0, 1):
        if f"yst{sl}" in p.sems:
            p.wait_val(nc.sync, f"yst{sl}", p.semv[f"yst{sl}"])
    p.finish()


def build_program(debug_names=(), stop_after=None):
    p = Prog(debug_names)
    build(p, stop_after=stop_after)
    return p


def kernel(**inputs):
    in_maps = host_prep(inputs)
    p = build_program()
    res = run_bass_kernel_spmd(p.nc, in_maps, core_ids=list(range(8)))
    out = np.zeros((4, 4096, 1024), np.float32)
    for c in range(8):
        out[c // 2, (c % 2) * T:(c % 2 + 1) * T] = res.results[c]["y"]
    return out


# version 16
# speedup vs baseline: 1.0428x; 1.0428x over previous
import os

import numpy as np
import concourse.bass as bass
import concourse.mybir as mybir
from concourse.bass_utils import run_bass_kernel_spmd

F32 = mybir.dt.float32
BF16 = mybir.dt.bfloat16
U32 = mybir.dt.uint32
AF = mybir.ActivationFunctionType
AL = mybir.AluOpType
AX = mybir.AxisListType

DTSZ = {mybir.dt.float32: 4, mybir.dt.bfloat16: 2, mybir.dt.uint32: 4,
        mybir.dt.int32: 4, mybir.dt.uint8: 1, mybir.dt.float16: 2}
PAGES_SBUF = (224 * 1024) // 256
NPAGES = PAGES_SBUF + 2048

T = 2048
NT = T // 128
D = 1024
H = 4
DH = 256
CO = 128
CI = 32
NCO = T // CO
E = 8
KL = 64
FW = 128
HOP = 64
NB = T // HOP

DMA = 16



def _fft_mats():
    t = np.arange(FW)
    k_re = np.arange(65)
    k_im = np.arange(1, 64)
    Fre = np.cos(2 * np.pi * np.outer(k_re, t) / FW)
    Fim = -np.sin(2 * np.pi * np.outer(k_im, t) / FW)
    Fp = np.concatenate([Fre, Fim], 0)
    tt = np.arange(HOP, FW)
    c = np.where((k_re == 0) | (k_re == 64), 1.0, 2.0) / FW
    Gre = c[None, :] * np.cos(2 * np.pi * np.outer(tt, k_re) / FW)
    Gim = (-2.0 / FW) * np.sin(2 * np.pi * np.outer(tt, k_im) / FW)
    G1 = np.concatenate([Gre, Gim], 1)
    G2 = np.concatenate([np.zeros((HOP, 1)), Gim, np.zeros((HOP, 1)), -Gre[:, 1:64]], 1)
    FA = np.concatenate([Fre, Fre[1:64]], 0)
    FB = np.concatenate([np.zeros((1, FW)), Fim, np.zeros((1, FW)), Fim], 0)
    return (np.ascontiguousarray(Fp.T, np.float32),
            np.ascontiguousarray(G1.T, np.float32),
            np.ascontiguousarray(G2.T, np.float32),
            np.ascontiguousarray(FA.T, np.float32),
            np.ascontiguousarray(FB.T, np.float32))


def _host_consts():
    c = {}
    c["ident"] = np.eye(128, dtype=np.float32)
    p = np.arange(128)
    f = np.arange(128)
    c["m_stril"] = (p[:, None] > f[None, :]).astype(np.float32)
    c["m_triu"] = (p[:, None] <= f[None, :]).astype(np.float32)
    fpT, g1T, g2T, faT, fbT = _fft_mats()
    c["fft_fT"] = fpT
    c["fft_g1T"] = g1T
    c["fft_g2T"] = g2T
    c["fft_faT"] = faT
    c["fft_fbT"] = fbT
    c["iota_row"] = np.broadcast_to(np.arange(D, dtype=np.float32), (128, D)).copy()
    sr = np.zeros((16, 16 * 128), np.float32)
    for r in range(16):
        sr[r, r * 128:(r + 1) * 128] = 1.0
    c["selrows"] = sr
    return c


def _w1s4(w1s):
    out = np.zeros((128, D), np.float32)
    for h in range(4):
        for r in range(16):
            kind, src = divmod(r, 4)
            out[32 * h + r] = w1s[:, src * 4 + kind]
    return np.ascontiguousarray(out)


def host_prep(inputs):
    g = lambda k: np.asarray(inputs[k])
    x = np.ascontiguousarray(g("L0_attn_hidden_states"), np.float32)
    tr = lambda a: np.ascontiguousarray(np.asarray(a, np.float32).T)
    shared = {
        "WqT": tr(g("L0_attn_q_proj_w")),
        "WkT": tr(g("L0_attn_k_proj_w")),
        "WvT": tr(g("L0_attn_v_proj_w")),
        "bprojT": tr(g("L0_attn_b_proj_w")),
        "qcw": np.ascontiguousarray(g("L0_attn_q_conv_w"), np.float32),
        "kcw": np.ascontiguousarray(g("L0_attn_k_conv_w"), np.float32),
        "vcw": np.ascontiguousarray(g("L0_attn_v_conv_w"), np.float32),
        "fir_s": np.ascontiguousarray(
            g("L0_attn_fir_short_w").reshape(D, 5), np.float32),
        "fir_lT": np.ascontiguousarray(
            g("L0_attn_fir_long_w").reshape(D, KL)[:, ::-1].T, np.float32),
        "fir_l": np.ascontiguousarray(g("L0_attn_fir_long_w").reshape(D, KL), np.float32),
        "oprojT": tr(g("L0_attn_o_proj_w")),
        "W1xT": np.ascontiguousarray(np.asarray(g("L0_attn_fuse_w1"), np.float32)[:, :D].T),
        "W1sT4": _w1s4(np.asarray(g("L0_attn_fuse_w1"), np.float32)[:, D:]),
        "b1": np.ascontiguousarray(np.asarray(g("L0_attn_fuse_b1"), np.float32).reshape(8, 128).T),
        "b2row": np.ascontiguousarray(g("L0_attn_fuse_b2").reshape(1, 4), np.float32),
        "W2T": tr(g("L0_attn_fuse_w2")),
        "ltemp": np.broadcast_to(np.asarray(g("L0_attn_logit_temp"), np.float32).reshape(1, 1), (128, 1)).copy(),
        "ress": np.broadcast_to(np.asarray(g("L0_attn_res_scale_short"), np.float32).reshape(1, 1), (128, 1)).copy(),
        "resl": np.broadcast_to(np.asarray(g("L0_attn_res_scale_long"), np.float32).reshape(1, 1), (128, 1)).copy(),
        "onorm": np.ascontiguousarray(
            np.tile(np.asarray(g("L0_attn_o_norm_w"), np.float32), H).reshape(8, 128).T),
        "Wg": np.ascontiguousarray(g("L0_ffn_Wg"), np.float32),
        "bgrow": np.ascontiguousarray(g("L0_ffn_bg").reshape(1, D), np.float32),
        "We": np.ascontiguousarray(g("L0_ffn_We"), np.float32),
        "be": np.ascontiguousarray(g("L0_ffn_be"), np.float32),
    }
    shared.update(_host_consts())
    in_maps = []
    for c_ in range(8):
        b, tau = c_ // 2, c_ % 2
        m = dict(shared)
        m["x_own"] = np.ascontiguousarray(x[b, tau * T:(tau + 1) * T])
        m["x_pre"] = np.ascontiguousarray(x[b, 0:T])
        m["gate"] = np.full((128, 1), float(tau), np.float32)
        in_maps.append(m)
    return in_maps



class Prog:
    def __init__(self, debug_names=()):
        self.nc = bass.Bass("TRN2", target_bir_lowering=False, detect_race_conditions=bool(os.environ.get("RACECHK")))
        nc = self.nc
        self.debug_names = list(debug_names)
        self.dbg_out = {}
        self.sems = {}
        self.semv = {}
        self.sbuf_off = {}
        self.din = {}
        for name, shape in [
            ("x_own", (T, D)), ("x_pre", (T, D)), ("gate", (128, 1)),
            ("WqT", (D, D)), ("WkT", (D, D)), ("WvT", (D, D)), ("bprojT", (D, 4)),
            ("qcw", (D, 4)), ("kcw", (D, 4)), ("vcw", (D, 4)),
            ("fir_s", (D, 5)), ("fir_lT", (KL, D)), ("fir_l", (D, KL)),
            ("oprojT", (D, D)), ("W1xT", (D, D)), ("W1sT4", (128, D)),
            ("b1", (128, 8)), ("b2row", (1, 4)), ("W2T", (D, 4)),
            ("ltemp", (128, 1)), ("ress", (128, 1)), ("resl", (128, 1)),
            ("onorm", (128, 8)),
            ("Wg", (D, D)), ("bgrow", (1, D)), ("We", (E, D, D)), ("be", (E, D)),
            ("ident", (128, 128)), ("m_stril", (128, 128)), ("m_triu", (128, 128)),
            ("fft_fT", (128, 128)), ("fft_g1T", (128, 64)), ("fft_g2T", (128, 64)),
            ("fft_faT", (128, 128)), ("fft_fbT", (128, 128)),
            ("iota_row", (128, D)), ("selrows", (16, 16 * 128)),
        ]:
            self.din[name] = nc.dram_tensor(name, list(shape), F32, kind="ExternalInput").ap()
        self.y = nc.dram_tensor("y", [T, D], F32, kind="ExternalOutput").ap()
        self.d_xch = nc.dram_tensor("d_xch", [128, 8 * T], BF16, kind="Internal").ap()
        self.d_x1 = nc.dram_tensor("d_x1", [T, D], F32, kind="Internal").ap()
        self.st = {}
        self.wait_hw = {}
        self.sb_base = {}
        self._pseudo = {}
        self._pseudo_next = PAGES_SBUF
        self._wrap_engine(nc.vector, "dv")
        self._wrap_engine(nc.scalar, "ac")
        self._wrap_engine(nc.gpsimd, "pl")

    _WRAP_OPS = ("tensor_copy", "tensor_tensor", "tensor_add", "tensor_sub",
                 "tensor_mul", "tensor_max", "tensor_scalar", "tensor_single_scalar",
                 "tensor_scalar_add", "tensor_scalar_sub", "tensor_scalar_mul",
                 "tensor_scalar_max", "tensor_scalar_min", "tensor_relu",
                 "tensor_reduce", "reduce_max", "reduce_sum", "scalar_tensor_tensor",
                 "tensor_tensor_scan", "memset", "memzero", "reciprocal", "select",
                 "copy_predicated", "transpose", "max", "max_index", "pool",
                 "pool_avg", "pool_max", "bn_stats", "bn_aggr", "activation",
                 "copy", "add", "mul", "square", "sqrt", "sign", "affine_select",
                 "iota")

    def _ap_pages(self, x):
        try:
            pat = x.ap
            off = x.offset
            sp = str(x.space)
            name = x.name
        except (AttributeError, TypeError):
            return None
        if "SBUF" not in sp and "PSUM" not in sp:
            return None
        esz = DTSZ.get(x.dtype, 4)
        ext = 0
        neg = False
        for stq, n in pat[1:]:
            if stq < 0:
                neg = True
            ext += (n - 1) * abs(stq)
        lo = off * esz
        hi = (off + ext + 1) * esz
        if "PSUM" in sp or name not in self.sb_base:
            if name not in self._pseudo:
                self._pseudo[name] = self._pseudo_next
                self._pseudo_next += 16
            b = self._pseudo[name]
            return range(b, b + 16)
        b0, b1 = self.sb_base[name]
        if neg:
            lo, hi = 0, b1 - b0
        alo = b0 + lo
        ahi = min(b0 + hi, b1 + 256)
        return range(alo >> 8, ((ahi - 1) >> 8) + 1)

    def _wrap_engine(self, eng, sem_name):
        semh = self.sem(sem_name)
        lastw = [-1] * NPAGES
        lastr = [-1] * NPAGES
        state = {"waited": 0}

        def dep_sync(a, k):
            reads = []
            writes = []
            for i, x in enumerate(a):
                pg = self._ap_pages(x)
                if pg is not None:
                    (writes if i == 0 else reads).append(pg)
            for kw, x in k.items():
                pg = self._ap_pages(x)
                if pg is not None:
                    (writes if kw in ("accum_out", "out") else reads).append(pg)
            need = -1
            for pgs in reads:
                for pg in pgs:
                    if lastw[pg] > need:
                        need = lastw[pg]
            for pgs in writes:
                for pg in pgs:
                    if lastw[pg] > need:
                        need = lastw[pg]
                    if lastr[pg] > need:
                        need = lastr[pg]
            if need >= state["waited"]:
                eng.wait_ge(semh, need + 1)
                state["waited"] = need + 1
            seq = self.semv[sem_name]
            for pgs in writes:
                for pg in pgs:
                    lastw[pg] = seq
            for pgs in reads:
                for pg in pgs:
                    if lastr[pg] < seq:
                        lastr[pg] = seq

        for name in self._WRAP_OPS:
            if not hasattr(eng, name):
                continue
            orig = getattr(eng, name)

            def mk(orig):
                def f(*a, **k):
                    dep_sync(a, k)
                    instr = orig(*a, **k)
                    instr.then_inc(semh, 1)
                    self.semv[sem_name] += 1
                    return instr
                return f
            setattr(eng, name, mk(orig))

    def sem(self, name):
        if name not in self.sems:
            self.sems[name] = self.nc.alloc_semaphore(name=name)
            self.semv[name] = 0
        return self.sems[name]

    def inc(self, instr, name, n=1):
        if name in ("dv", "ac", "pl"):
            return instr
        instr.then_inc(self.sem(name), n)
        self.semv[name] += n
        return instr

    def wait(self, eng, name):
        self.wait_val(eng, name, self.semv[name])

    def wait_val(self, eng, name, val):
        hw = self.wait_hw.setdefault(id(eng), {})
        if hw.get(name, -1) >= val:
            return
        hw[name] = val
        eng.wait_ge(self.sem(name), val)

    def val(self, name):
        self.sem(name)
        return self.semv[name]

    def sb(self, name, shape, dtype, offset=None):
        if name in self.st:
            return self.st[name]
        if offset is None:
            b0 = self.nc.sbuf_base
            t = self.nc.alloc_sbuf_tensor(name, list(shape), dtype)
            self.sb_base[name] = (b0, self.nc.sbuf_base)
        else:
            t = self.nc.alloc_sbuf_tensor_at(name, list(shape), dtype, offset=offset)
            nb = shape[1] * DTSZ.get(dtype, 4)
            for s in shape[2:]:
                nb *= s
            self.sb_base[name] = (offset, offset + nb)
        self.st[name] = t
        return t

    def ps(self, name, shape, dtype=F32):
        if name in self.st:
            return self.st[name]
        t = self.nc.alloc_psum_tensor(name, list(shape), dtype)
        self.st[name] = t
        return t

    def dump(self, name, ap, shape, dtype=F32):
        if name not in self.debug_names:
            return
        for s in ("pe", "dv", "ac", "pl", "xsp0", "xsp1", "x1sp"):
            if s in self.sems:
                self.nc.sync.wait_ge(self.sems[s], self.semv[s])
        o = self.nc.dram_tensor("dbg_" + name, list(shape), dtype,
                                kind="ExternalOutput").ap()
        self.dbg_out[name] = o
        self.inc(self.nc.sync.dma_start(out=o, in_=ap), "dbg", DMA)
        for eng in (self.nc.tensor, self.nc.vector, self.nc.scalar,
                    self.nc.gpsimd, self.nc.sync):
            eng.wait_ge(self.sems["dbg"], self.semv["dbg"])

    def finish(self):
        for s in ("dbg", "yst0", "yst1", "xsp0", "xsp1", "x1sp"):
            if s in self.sems:
                self.nc.sync.wait_ge(self.sems[s], self.semv[s])

    def barrier(self, name, engines=("tensor", "vector", "scalar", "gpsimd", "sync")):
        s = self.sem(name)
        for e in engines:
            getattr(self.nc, e).sem_inc(s, 1)
        self.semv[name] += len(engines)
        for e in engines:
            getattr(self.nc, e).wait_ge(s, self.semv[name])


def ceil_div(a, b):
    return (a + b - 1) // b


class PsumPool:
    def __init__(self, p):
        self.p = p
        self.banks = [p.ps(f"bank{i}", (128, 512), F32) for i in range(8)]
        self.free_tok = [None] * 8
        self.held = [False] * 8
        self.rr = 0

    def acquire(self, cols=512):
        for _ in range(8):
            i = self.rr
            self.rr = (self.rr + 1) % 8
            if not self.held[i]:
                self.held[i] = True
                tok = self.free_tok[i]
                self.free_tok[i] = None
                return i, self.banks[i][:, 0:cols], tok
        raise RuntimeError("psum pool exhausted")

    def release(self, i, tok):
        self.held[i] = False
        self.free_tok[i] = tok


def build(p: Prog, stop_after=None):
    nc = p.nc
    din = p.din
    pp = PsumPool(p)
    PE, DV, AC, PL = "pe", "dv", "ac", "pl"

    def tok(name):
        return (name, p.val(name))

    own_sem = {id(nc.vector): "dv", id(nc.scalar): "ac", id(nc.gpsimd): "pl"}

    def w8(eng, *toks):
        seen = {}
        own = own_sem.get(id(eng))
        for t in toks:
            if t is None:
                continue
            seen[t[0]] = max(seen.get(t[0], 0), t[1])
        for name, v in seen.items():
            if name == own:
                continue
            p.wait_val(eng, name, v)

    TOP = nc.sbuf_top
    R_W = TOP - 49152
    R_DR = R_W - 95392
    R_A = R_DR - 33792
    R_C = R_A - 32768
    ident_b = p.sb("ident_b", (128, 128), BF16)
    m_stril = p.sb("m_stril_s", (128, 128), BF16)
    m_triu = p.sb("m_triu_s", (128, 128), BF16)
    onorm_b = p.sb("onorm_b", (128, 8), BF16)
    gate_c = p.sb("gate_c", (128, 1), F32)
    ltemp_c = p.sb("ltemp_c", (128, 1), F32)
    ress_c = p.sb("ress_c", (128, 1), F32)
    resl_c = p.sb("resl_c", (128, 1), F32)
    ones_row = p.sb("ones_row", (1, 128), BF16)
    WqT_b = p.sb("WqT_b", (128, 8 * D), BF16, offset=R_W)
    WkT_b = p.sb("WkT_b", (128, 8 * D), BF16, offset=R_W + 16384)
    WvT_b = p.sb("WvT_b", (128, 8 * D), BF16, offset=R_W + 32768)
    bprojT_b = p.sb("bprojT_b", (128, 8 * 4), BF16)
    diag_q = p.sb("diag_q", (128, 32 * 128), BF16, offset=R_DR)
    diag_k = p.sb("diag_k", (128, 32 * 128), BF16, offset=R_DR + 8192)
    diag_v = p.sb("diag_v", (128, 32 * 128), BF16, offset=R_DR + 16384)
    epscol = p.sb("epscol", (128, 1), F32)
    beta_scr = p.sb("beta_scr", (128, 8), F32)


    assert nc.sbuf_base <= R_C, (nc.sbuf_base, R_C)
    v_ch = p.sb("v_ch", (128, 8 * (T + 64)), BF16, offset=R_A)
    do_tok = p.sb("do_tok", (128, NT * D), BF16, offset=R_C)

    ro = R_DR + 24576
    xr = p.sb("xr", (128, 2 * D), BF16, offset=ro); ro += 4096
    xpre_r = p.sb("xpre_r", (128, 2 * 8 * CO), BF16, offset=ro); ro += 4096
    lin = {}
    for t in ("q", "k", "v"):
        lin[t] = p.sb("lin" + t, (128, 8 * (CO + 4)), BF16, offset=ro); ro += 2112
    qch_r = p.sb("qch_r", (128, 2 * 8 * CO), BF16, offset=ro); ro += 4096
    kch_r = p.sb("kch_r", (128, 2 * 8 * CO), BF16, offset=ro); ro += 4096
    vch_r = p.sb("vch_r", (128, 2 * 8 * CO), BF16, offset=ro); ro += 4096
    ktok_r = p.sb("ktok_r", (128, 2 * D), BF16, offset=ro); ro += 4096
    vtok_r = p.sb("vtok_r", (128, 2 * D), BF16, offset=ro); ro += 4096
    tp_r = p.sb("tp_r", (128, 2 * H * 128), BF16, offset=ro); ro += 2048
    att2_r = p.sb("att2_r", (128, 2 * H * 128), BF16, offset=ro); ro += 2048
    chBB = p.sb("chBB", (128, 4 * 256), BF16, offset=ro); ro += 2048
    chP4 = p.sb("chP4", (128, 4 * 128), BF16, offset=ro); ro += 1024
    zu4 = p.sb("zu4", (128, 4 * 256), BF16, offset=ro); ro += 2048
    S_sb = []
    Sb_sb = []
    ro += 64
    s_off = ro
    for h in range(H):
        S_sb.append(p.sb(f"S{h}", (128, 512), F32, offset=ro)); ro += 2048
        Sb_sb.append(p.sb(f"Sb{h}", (128, 512), BF16, offset=ro)); ro += 1024
    cw_s = {}
    for t in ("q", "k", "v"):
        cw_s[t] = p.sb(t + "cw_s", (128, 8 * 4), F32, offset=ro); ro += 128
    fir_s_s = p.sb("fir_s_s", (128, 8 * 5), F32, offset=ro); ro += 160
    beta_c = p.sb("beta_c", (128, 2 * H), F32, offset=ro); ro += 32
    rk_c = p.sb("rk_c", (128, 2 * H), F32, offset=ro); ro += 32
    nrk_c = p.sb("nrk_c", (128, 2 * H), F32, offset=ro); ro += 32
    ns_c = p.sb("ns_c", (128, 2 * H), F32, offset=ro); ro += 32
    rkb_c = p.sb("rkb_c", (128, 2 * H), F32, offset=ro); ro += 32
    rq_c = p.sb("rq_c", (128, 2 * H), F32, offset=ro); ro += 32
    scr1 = p.sb("scr1", (128, 160), F32, offset=ro); ro += 640
    scr2 = p.sb("scr2", (128, 128), F32, offset=ro); ro += 512
    assert ro <= R_DR + 83104, (ro, R_DR + 83104)

    def loads():
        g_ = lambda dst, src: p.inc(nc.gpsimd.dma_start(out=dst, in_=src), "ldg", DMA)
        s_ = lambda dst, src: p.inc(nc.sync.dma_start(out=dst, in_=src), "lds", DMA)
        v3 = lambda t, n: t[:].rearrange("p (k co) -> p k co", k=n)
        g_(v3(WkT_b, 8), din["WkT"].rearrange("(k p) co -> p k co", p=128))
        g_(v3(WvT_b, 8), din["WvT"].rearrange("(k p) co -> p k co", p=128))
        g_(v3(WqT_b, 8), din["WqT"].rearrange("(k p) co -> p k co", p=128))
        g_(v3(bprojT_b, 8), din["bprojT"].rearrange("(k p) co -> p k co", p=128))
        g_(ident_b[:], din["ident"])
        g_(m_stril[:], din["m_stril"])
        g_(m_triu[:], din["m_triu"])
        g_(onorm_b[:], din["onorm"])
        for t in ("q", "k", "v"):
            s_(cw_s[t][:].rearrange("p (j d) -> p j d", j=8),
               din[t + "cw"].rearrange("(j p) d -> p j d", p=128))
        s_(fir_s_s[:].rearrange("p (j d) -> p j d", j=8),
           din["fir_s"].rearrange("(j p) d -> p j d", p=128))
        s_(gate_c[:], din["gate"])
        s_(ltemp_c[:], din["ltemp"])
        s_(ress_c[:], din["ress"])
        s_(resl_c[:], din["resl"])
        ldg, lds = tok("ldg"), tok("lds")
        w8(nc.vector, ldg, lds)
        i = nc.vector.memset(ones_row[:], 1.0)
        i = nc.vector.memset(epscol[:], 1e-5)
        for t, dg in (("q", diag_q), ("k", diag_k), ("v", diag_v)):
            for j in range(8):
                for d in range(4):
                    i = nc.vector.tensor_scalar(
                        dg[:, (j * 4 + d) * 128:(j * 4 + d + 1) * 128], ident_b[:],
                        cw_s[t][:, j * 4 + d:j * 4 + d + 1], None, AL.mult)
        p.inc(i, DV)
        for t in ("q", "k", "v"):
            for j in range(8):
                i = nc.vector.memset(lin[t][:, j * (CO + 4):j * (CO + 4) + 4], 0.0)
        for h in range(H):
            i = nc.vector.memset(S_sb[h][:], 0.0)
            i = nc.vector.memset(Sb_sb[h][:], 0.0)
        p.inc(i, DV)
        return tok(DV), ldg, lds

    def cast_x_chunk(o, which):
        src = din["x_pre"] if which == "pre" else din["x_own"]
        dst = xr[:, (o % 2) * D:(o % 2 + 1) * D]
        for t in st["xr_free"][o % 2]:
            w8(nc.gpsimd, t)
        p.inc(nc.gpsimd.dma_start(out=dst, in_=src[o * CO:(o + 1) * CO, :]),
              f"xdma{o % 2}", DMA)
        return tok(f"xdma{o % 2}")

    def xT_chunk(o, which, dep):
        res = None
        for jp in (0, 4):
            bi, bank, btok = pp.acquire()
            w8(nc.tensor, btok, dep)
            for j in range(jp, jp + 4):
                mm = nc.tensor.matmul(bank[:, (j - jp) * 128:(j - jp + 1) * 128],
                                      xr[:, (o % 2) * D + j * 128:(o % 2) * D + (j + 1) * 128],
                                      ident_b[:], start=True, stop=True)
            p.inc(mm, PE)
            pet = tok(PE)
            w8(nc.scalar, pet, *st["xpre_free"][o % 2])
            for j in range(jp, jp + 4):
                dst = xpre_r[:, ((o % 2) * 8 + j) * CO:((o % 2) * 8 + j + 1) * CO]
                i = nc.scalar.copy(dst, bank[:, (j - jp) * 128:(j - jp + 1) * 128])
            p.inc(i, AC)
            res = tok(AC)
            pp.release(bi, res)
        if which == "main":
            w8(nc.sync, res)
            dst = p.d_xch.rearrange("p (k t) -> p k t", k=8)[:, :, o * CO:(o + 1) * CO]
            src = xpre_r[:, (o % 2) * 8 * CO:((o % 2) * 8 + 8) * CO].rearrange(
                "p (k t) -> p k t", k=8)
            p.inc(nc.sync.dma_start(out=dst, in_=src), f"xsp{o % 2}", DMA)
        return res

    def xsrc(o, which):
        return lambda k: xpre_r[:, ((o % 2) * 8 + k) * CO:((o % 2) * 8 + k + 1) * CO]

    def proj_chunk(o, which, tensors, xdep, lindeps=()):
        xs = xsrc(o, which)
        res = {}
        for tname in tensors:
            W = {"q": WqT_b, "k": WkT_b, "v": WvT_b}[tname]
            ln = lin[tname]
            for m in range(8):
                bi, bank, btok = pp.acquire(cols=128)
                w8(nc.tensor, btok, xdep)
                for k in range(8):
                    mm = nc.tensor.matmul(bank, W[:, (k * 8 + m) * 128:(k * 8 + m + 1) * 128],
                                          xs(k), start=(k == 0), stop=(k == 7))
                p.inc(mm, PE)
                pet = tok(PE)
                w8(nc.vector, pet, *lindeps)
                dst = ln[:, m * (CO + 4) + 4: m * (CO + 4) + 4 + CO]
                i = nc.vector.tensor_copy(dst, bank)
                p.inc(i, DV)
                res[tname] = tok(DV)
                pp.release(bi, res[tname])
        return res

    def conv_silu_chunk(o, which, tensors, lindep, halodep):
        out_toks = {}
        for tname in tensors:
            dg = {"q": diag_q, "k": diag_k, "v": diag_v}[tname]
            ln = lin[tname]
            ldep = lindep[tname] if isinstance(lindep, dict) else lindep
            for j in range(8):
                bi, bank, btok = pp.acquire(cols=128)
                w8(nc.tensor, btok, ldep, halodep)
                base = j * (CO + 4) + 4
                for d in range(4):
                    mm = nc.tensor.matmul(bank, dg[:, (j * 4 + d) * 128:(j * 4 + d + 1) * 128],
                                          ln[:, base + d - 3: base + d - 3 + CO],
                                          start=(d == 0), stop=(d == 3))
                p.inc(mm, PE)
                pet = tok(PE)
                w8(nc.scalar, pet)
                if which == "main" and tname == "v":
                    dst = v_ch[:, j * (T + 64) + 64 + o * CO:j * (T + 64) + 64 + (o + 1) * CO]
                else:
                    dst = {"q": qch_r, "k": kch_r, "v": vch_r}[tname][
                        :, ((o % 2) * 8 + j) * CO:((o % 2) * 8 + j + 1) * CO]
                i = nc.scalar.activation(dst, bank, AF.Silu)
                p.inc(i, AC)
                pp.release(bi, tok(AC))
            out_toks[tname] = tok(AC)
        if isinstance(lindep, dict):
            w8(nc.vector, *[lindep[t] for t in tensors])
        else:
            w8(nc.vector, lindep)
        w8(nc.vector, out_toks[tensors[-1]])
        i = None
        for tname in tensors:
            ln = lin[tname]
            for j in range(8):
                b0 = j * (CO + 4)
                i = nc.vector.tensor_copy(ln[:, b0 + 1:b0 + 4],
                                          ln[:, b0 + 4 + CO - 3: b0 + 4 + CO])
        if i is not None:
            p.inc(i, DV)
        return out_toks, tok(DV)

    def beta_chunk(o, which, xdep):
        xs = xsrc(o, which)
        bi, bank, btok = pp.acquire(cols=4)
        w8(nc.tensor, btok, xdep)
        for k in range(8):
            mm = nc.tensor.matmul(bank, xs(k), bprojT_b[:, k * 4:(k + 1) * 4],
                                  start=(k == 0), stop=(k == 7))
        p.inc(mm, PE)
        pet = tok(PE)
        w8(nc.scalar, pet)
        i = nc.scalar.activation(scr1[:, 140:144], bank, AF.Tanh, scale=0.5)
        p.inc(i, AC)
        w8(nc.vector, tok(AC))
        i = nc.vector.tensor_scalar(beta_c[:, (o % 2) * H:(o % 2) * H + H], scr1[:, 140:144],
                                    0.5, 0.5, AL.mult, AL.add)
        p.inc(i, DV)
        res = tok(DV)
        pp.release(bi, res)
        return res

    def kv_tok_chunk(o, which, convtoks, tnames=("k", "v")):
        srcs = {}
        srcs["k"] = lambda j: kch_r[:, ((o % 2) * 8 + j) * CO:((o % 2) * 8 + j + 1) * CO]
        if which == "pre":
            srcs["v"] = lambda j: vch_r[:, ((o % 2) * 8 + j) * CO:((o % 2) * 8 + j + 1) * CO]
        else:
            srcs["v"] = lambda j: v_ch[:, j * (T + 64) + 64 + o * CO:j * (T + 64) + 64 + (o + 1) * CO]
        dsts = {"k": ktok_r, "v": vtok_r}
        res = None
        for tname in tnames:
            for jp in (0, 4):
                bi, bank, btok = pp.acquire()
                w8(nc.tensor, btok, convtoks.get(tname))
                for j in range(jp, jp + 4):
                    mm = nc.tensor.matmul(bank[:, (j - jp) * 128:(j - jp + 1) * 128],
                                          srcs[tname](j), ident_b[:], start=True, stop=True)
                p.inc(mm, PE)
                w8(nc.scalar, tok(PE))
                for j in range(jp, jp + 4):
                    i = nc.scalar.copy(
                        dsts[tname][:, (o % 2) * D + j * 128:(o % 2) * D + (j + 1) * 128],
                        bank[:, (j - jp) * 128:(j - jp + 1) * 128])
                p.inc(i, AC)
                res = tok(AC)
                pp.release(bi, res)
        return res

    def gram_into(lhs_fn, rhs_fn, dep):
        bi, bank, btok = pp.acquire(cols=128)
        w8(nc.tensor, btok, dep)
        for hh in range(2):
            mm = nc.tensor.matmul(bank, lhs_fn(hh), rhs_fn(hh),
                                  start=(hh == 0), stop=(hh == 1))
        p.inc(mm, PE)
        return bi, bank, tok(PE)

    def cols_chunk(o, which, kvt, bt):
        c0 = (o % 2) * H
        w8(nc.scalar, kvt)
        for h in range(H):
            i = nc.scalar.activation(do_tok[:, o * D + h * DH:o * D + (h + 1) * DH],
                                     ktok_r[:, (o % 2) * D + h * DH:(o % 2) * D + (h + 1) * DH],
                                     AF.Square, accum_out=beta_scr[:, h:h + 1])
        p.inc(i, AC)
        w8(nc.scalar, tok(AC))
        i = nc.scalar.activation(scr1[:, 128:132], beta_scr[:, 0:4], AF.Ln)
        i = nc.scalar.activation(rk_c[:, c0:c0 + 4], scr1[:, 128:132], AF.Exp, scale=-0.5)
        p.inc(i, AC)
        w8(nc.vector, tok(AC), bt)
        i = nc.vector.reciprocal(scr1[:, 132:136], beta_scr[:, 0:4])
        i = nc.vector.tensor_tensor(scr1[:, 136:140], beta_c[:, c0:c0 + 4],
                                    scr1[:, 132:136], AL.mult)
        i = nc.vector.tensor_scalar(ns_c[:, c0:c0 + 4], scr1[:, 136:140],
                                    -1.0, None, AL.mult)
        i = nc.vector.tensor_scalar(nrk_c[:, c0:c0 + 4], rk_c[:, c0:c0 + 4],
                                    -1.0, None, AL.mult)
        i = nc.vector.tensor_tensor(rkb_c[:, c0:c0 + 4], rk_c[:, c0:c0 + 4],
                                    beta_c[:, c0:c0 + 4], AL.mult)
        p.inc(i, DV)
        return tok(DV)

    def delta_prep_chunk(o, which, convtoks, kvtok_tok, colstok):
        c0 = (o % 2) * H
        kslice = lambda h, hh: kch_r[:, ((o % 2) * 8 + 2 * h + hh) * CO:
                                     ((o % 2) * 8 + 2 * h + hh + 1) * CO]
        qslice = lambda h, hh: qch_r[:, ((o % 2) * 8 + 2 * h + hh) * CO:
                                     ((o % 2) * 8 + 2 * h + hh + 1) * CO]
        chB = lambda h: chBB[:, h * 256:h * 256 + 128]
        chBt = lambda h: chBB[:, h * 256 + 128:h * 256 + 256]
        chP = lambda h: chP4[:, h * 128:(h + 1) * 128]
        cols_t = colstok
        g0b = []
        for h in range(H):
            bi, g0ps, pet = gram_into(lambda hh: kslice(h, hh),
                                      lambda hh: kslice(h, hh), convtoks.get("k"))
            g0b.append((bi, g0ps, pet))
        fill()
        bb_t = [None] * H
        for h, (bi, g0ps, pet) in enumerate(g0b):
            w8(nc.vector, pet, cols_t)
            i = nc.vector.scalar_tensor_tensor(chB(h), g0ps, ns_c[:, c0 + h:c0 + h + 1],
                                               m_stril[:], AL.mult, AL.mult)
            p.inc(i, DV)
            bb_t[h] = tok(DV)
            pp.release(bi, bb_t[h])
        fill(2)
        trs = []
        for h in range(H):
            bi, bank, btok = pp.acquire(cols=128)
            w8(nc.tensor, btok, bb_t[h])
            mm = nc.tensor.matmul(bank, chB(h), ident_b[:], start=True, stop=True)
            p.inc(mm, PE)
            trs.append((bi, bank, tok(PE)))
        bt_t = [None] * H
        for h, (bi, bank, pet) in enumerate(trs):
            w8(nc.scalar, pet)
            i = nc.scalar.copy(chBt(h), bank)
            p.inc(i, AC)
            bt_t[h] = tok(AC)
            pp.release(bi, bt_t[h])
        pP_t = [None] * H
        for h in range(H):
            w8(nc.vector, bb_t[h])
            i = nc.vector.tensor_tensor(chP(h), chB(h), ident_b[:], AL.add)
            p.inc(i, DV)
            pP_t[h] = tok(DV)
        for lvl in range(6):
            sq = []
            for h in range(H):
                bi1, b1, bt1 = pp.acquire(cols=256)
                w8(nc.tensor, bt1, bb_t[h], bt_t[h])
                mm = nc.tensor.matmul(b1[:, 0:128], chBt(h), chB(h), start=True, stop=True)
                mm = nc.tensor.matmul(b1[:, 128:256], chB(h), chBt(h), start=True, stop=True)
                p.inc(mm, PE)
                sq.append((bi1, b1, tok(PE)))
            for h, (bi1, b1, pet) in enumerate(sq):
                p0dep = pP_t[h] if lvl == 0 else None
                if h % 2 == 0:
                    w8(nc.vector, pet, p0dep)
                    i = nc.vector.tensor_copy(chBB[:, h * 256:(h + 1) * 256], b1)
                    p.inc(i, DV)
                    bb_t[h] = bt_t[h] = tok(DV)
                else:
                    w8(nc.scalar, pet, p0dep)
                    i = nc.scalar.copy(chBB[:, h * 256:(h + 1) * 256], b1)
                    p.inc(i, AC)
                    bb_t[h] = bt_t[h] = tok(AC)
                pp.release(bi1, bb_t[h])
            pr = []
            for h in range(H):
                bi3, b3, bt3 = pp.acquire(cols=128)
                w8(nc.tensor, bt3, bb_t[h], pP_t[h])
                mm = nc.tensor.matmul(b3, ident_b[:], chP(h), start=True, stop=False)
                mm = nc.tensor.matmul(b3, chBt(h), chP(h), start=False, stop=True)
                p.inc(mm, PE)
                pr.append((bi3, b3, tok(PE)))
            for h, (bi3, b3, pet) in enumerate(pr):
                if h % 2 == 1:
                    w8(nc.vector, pet)
                    i = nc.vector.tensor_copy(chP(h), b3)
                    p.inc(i, DV)
                    pP_t[h] = tok(DV)
                else:
                    w8(nc.scalar, pet)
                    i = nc.scalar.copy(chP(h), b3)
                    p.inc(i, AC)
                    pP_t[h] = tok(AC)
                pp.release(bi3, pP_t[h])
            fill()
        tps = []
        for h in range(H):
            bi, bank, btok = pp.acquire(cols=128)
            w8(nc.tensor, btok, pP_t[h])
            mm = nc.tensor.matmul(bank, chP(h), ident_b[:], start=True, stop=True)
            p.inc(mm, PE)
            tps.append((bi, bank, tok(PE)))
        tp_toks = []
        for h, (bi, bank, pet) in enumerate(tps):
            w8(nc.vector, pet)
            i = nc.vector.tensor_scalar(
                tp_r[:, ((o % 2) * H + h) * 128:((o % 2) * H + h + 1) * 128],
                bank, rkb_c[:, c0 + h:c0 + h + 1], None, AL.mult)
            p.inc(i, DV)
            tp_toks.append(tok(DV))
            pp.release(bi, tok(DV))
        fill()
        att2_tok = None
        rq_tok = None
        if which == "main":
            a2b = []
            for h in range(H):
                bi, a2ps, pet = gram_into(lambda hh: kslice(h, hh),
                                          lambda hh: qslice(h, hh), convtoks.get("q"))
                a2b.append((bi, a2ps, pet))
            fill()
            for h, (bi, a2ps, pet) in enumerate(a2b):
                w8(nc.vector, pet)
                i = nc.vector.tensor_tensor(
                    att2_r[:, ((o % 2) * H + h) * 128:((o % 2) * H + h + 1) * 128],
                    a2ps, m_triu[:], AL.mult)
                p.inc(i, DV)
                pp.release(bi, tok(DV))
            att2_tok = tok(DV)
            qb = []
            for h in range(H):
                bi, qps, pet = gram_into(lambda hh: qslice(h, hh),
                                         lambda hh: qslice(h, hh), convtoks.get("q"))
                qb.append((bi, qps, pet))
            exs = [scr1[:, 0:128], scr2[:, 0:128]]
            for h, (bi, qps, pet) in enumerate(qb):
                w8(nc.vector, pet)
                i = nc.vector.tensor_tensor(exs[h % 2], qps, ident_b[:], AL.mult)
                i = nc.vector.reduce_sum(beta_scr[:, 4 + h:5 + h], exs[h % 2], axis=AX.X)
                p.inc(i, DV)
                pp.release(bi, tok(DV))
            w8(nc.scalar, tok(DV))
            i = nc.scalar.activation(scr1[:, 128:132], beta_scr[:, 4:8], AF.Ln)
            i = nc.scalar.activation(rq_c[:, c0:c0 + 4], scr1[:, 128:132], AF.Exp, scale=-0.5)
            p.inc(i, AC)
            rq_tok = tok(AC)
        return {"tp": tp_toks[-1], "att2": att2_tok, "rq": rq_tok, "cols": cols_t}

    def scan_stages(o, which, preptoks, kvt):
        c0 = (o % 2) * H
        kslice = lambda h, hh: kch_r[:, ((o % 2) * 8 + 2 * h + hh) * CO:
                                      ((o % 2) * 8 + 2 * h + hh + 1) * CO]
        qslice = lambda h, hh: qch_r[:, ((o % 2) * 8 + 2 * h + hh) * CO:
                                      ((o % 2) * 8 + 2 * h + hh + 1) * CO]
        zu = lambda h: zu4[:, h * 256:(h + 1) * 256]
        sst = {"ks": {}, "zt": {}, "ub": {}, "ut": {}, "ob": {}, "db": {},
               "out": [None] * H}

        def ks_pair(hs):
            def f():
                for h in hs:
                    biA, psA, btokA = pp.acquire(cols=256)
                    w8(nc.tensor, btokA, st["s_toks"][h], preptoks["cols"], kvt)
                    for hh in range(2):
                        mm = nc.tensor.matmul(psA, kslice(h, hh),
                                              Sb_sb[h][:, hh * 256:(hh + 1) * 256],
                                              start=(hh == 0), stop=(hh == 1))
                    p.inc(mm, PE)
                    sst["ks"][h] = (biA, psA, tok(PE))
            return f

        def out_pair(hs):
            def f():
                for h in hs:
                    biC, psC, btokC = pp.acquire(cols=256)
                    w8(nc.tensor, btokC, sst["ut"][h], preptoks["att2"], preptoks["rq"])
                    for hh in range(2):
                        mm = nc.tensor.matmul(psC, qslice(h, hh),
                                              Sb_sb[h][:, hh * 256:(hh + 1) * 256],
                                              start=(hh == 0), stop=False)
                    mm = nc.tensor.matmul(
                        psC, att2_r[:, ((o % 2) * H + h) * 128:((o % 2) * H + h + 1) * 128],
                        zu(h), start=False, stop=True)
                    p.inc(mm, PE)
                    sst["ob"][h] = (biC, psC, tok(PE))
            return f

        def ds_pair(hs):
            def f():
                for h in hs:
                    biD, psD, btokD = pp.acquire(cols=512)
                    w8(nc.tensor, btokD, sst["ut"][h], kvt)
                    for hh in range(2):
                        mm = nc.tensor.matmul(
                            psD[:, hh * 256:(hh + 1) * 256],
                            ktok_r[:, (o % 2) * D + h * 256 + hh * 128:
                                   (o % 2) * D + h * 256 + hh * 128 + 128],
                            zu(h), start=True, stop=True)
                    p.inc(mm, PE)
                    sst["db"][h] = (biD, psD, tok(PE))
            return f

        def z_pair(hs):
            def f():
                for h in hs:
                    biA, psA, pet = sst["ks"][h]
                    w8(nc.vector, pet, kvt)
                    i = nc.vector.scalar_tensor_tensor(
                        zu(h), psA, nrk_c[:, c0 + h:c0 + h + 1],
                        vtok_r[:, (o % 2) * D + h * 256:(o % 2) * D + (h + 1) * 256],
                        AL.mult, AL.add)
                    p.inc(i, DV)
                    sst["zt"][h] = tok(DV)
                    pp.release(biA, sst["zt"][h])
            return f

        def u_pair(hs):
            def f():
                for h in hs:
                    biB, psB, btokB = pp.acquire(cols=256)
                    w8(nc.tensor, btokB, sst["zt"][h], preptoks["tp"])
                    mm = nc.tensor.matmul(
                        psB, tp_r[:, ((o % 2) * H + h) * 128:((o % 2) * H + h + 1) * 128],
                        zu(h), start=True, stop=True)
                    p.inc(mm, PE)
                    w8(nc.scalar, tok(PE))
                    i = nc.scalar.copy(zu(h), psB)
                    p.inc(i, AC)
                    sst["ut"][h] = tok(AC)
                    pp.release(biB, sst["ut"][h])
            return f

        def oev_pair(hs):
            def f():
                for h in hs:
                    biC, psC, pet = sst["ob"][h]
                    w8(nc.scalar, pet)
                    i = nc.scalar.activation(
                        do_tok[:, o * D + h * 256:o * D + (h + 1) * 256], psC,
                        AF.Copy, scale=rq_c[:, c0 + h:c0 + h + 1])
                    p.inc(i, AC)
                    pp.release(biC, tok(AC))
            return f

        def su_pair(hs):
            def f():
                for h in hs:
                    biD, psD, pet = sst["db"][h]
                    w8(nc.vector, pet)
                    i = nc.vector.tensor_tensor(S_sb[h][:], S_sb[h][:], psD, AL.add)
                    p.inc(i, DV)
                    w8(nc.scalar, tok(DV))
                    i = nc.scalar.copy(Sb_sb[h][:], S_sb[h][:])
                    p.inc(i, AC)
                    st["s_toks"][h] = tok(AC)
                    pp.release(biD, st["s_toks"][h])
            return f

        def seq(*fs):
            def f():
                for g in fs:
                    g()
            return f

        stages = [seq(ks_pair((0, 1)), z_pair((0, 1))),
                  seq(ks_pair((2, 3)), z_pair((2, 3))),
                  u_pair((0, 1)), u_pair((2, 3))]
        if which == "main":
            stages += [out_pair((0, 1)), seq(out_pair((2, 3)), oev_pair((0, 1))),
                       seq(oev_pair((2, 3)), ds_pair((0, 1))),
                       seq(ds_pair((2, 3)), su_pair((0, 1))), su_pair((2, 3))]
        else:
            stages += [ds_pair((0, 1)), seq(ds_pair((2, 3)), su_pair((0, 1))),
                       su_pair((2, 3))]
        return stages

    st = {}
    ld_dv, ldg, lds = loads()
    st["halo"] = ld_dv
    st["conv"] = ld_dv
    st["xr_free"] = {0: [ld_dv], 1: [ld_dv]}
    st["x1t_free"] = {0: ld_dv, 1: ld_dv}
    st["xpre_free"] = {0: [ld_dv], 1: [ld_dv]}
    s_toks = [ld_dv] * H
    st["s_toks"] = s_toks
    st["filler"] = []

    def fill(k=1):
        for _ in range(k):
            if st["filler"]:
                st["filler"].pop(0)()

    def fill_all():
        while st["filler"]:
            st["filler"].pop(0)()

    def front(o, which, tensors, xd):
        prevconv, prevhalo = st["conv"], st["halo"]
        xt = xT_chunk(o, which, xd)
        st["xr_free"][o % 2] = [tok(PE)]
        if o + 1 < NCO:
            st["xd_next"] = cast_x_chunk(o + 1, which)
        fill(2)
        pj = proj_chunk(o, which, tensors, xt, (prevconv, prevhalo))
        bt = beta_chunk(o, which, xt)
        cv1, halo1 = conv_silu_chunk(o, which, ["k"], pj, st["halo"])
        kvt_k = kv_tok_chunk(o, which, cv1, ("k",))
        colst = cols_chunk(o, which, kvt_k, bt)
        fill(2)
        rest = [t for t in tensors if t != "k" and (t != "q" or which == "main")]
        cv2, halo2 = conv_silu_chunk(o, which, rest, pj, halo1)
        cv = dict(cv1)
        cv.update(cv2)
        st["halo"] = halo2
        st["conv"] = cv2[rest[-1]] if rest else cv1["k"]
        frees = [tok(PE)]
        if f"xsp{o % 2}" in p.sems:
            frees.append((f"xsp{o % 2}", p.semv[f"xsp{o % 2}"]))
        st["xpre_free"][o % 2] = frees
        kvt = kv_tok_chunk(o, which, cv, ("v",))
        return cv, bt, kvt, colst

    xd = cast_x_chunk(0, "pre")
    for o in range(NCO):
        tensors = ("k", "v") if o < NCO - 1 else ("k", "v", "q")
        cv, bt, kvt, colst = front(o, "pre", tensors, xd)
        xd = st.get("xd_next")
        if o == NCO - 1:
            w8(nc.vector, tok(DV), tok(AC))
            iq = None
            for j in range(8):
                b0 = j * (CO + 4)
                iq = nc.vector.tensor_copy(lin["q"][:, b0 + 1:b0 + 4],
                                           lin["q"][:, b0 + 4 + CO - 3:b0 + 4 + CO])
            p.inc(iq, DV)
        prep = delta_prep_chunk(o, "pre", cv, kvt, colst)
        fill_all()
        if o == 1:
            p.dump("S0_c0", S_sb[0][:], (128, 512))
        st["filler"] = scan_stages(o, "pre", prep, kvt)
        if o == 0:
            p.dump("kch0", kch_r[:, 0:8 * CO], (128, 8 * CO), BF16)
            p.dump("vch0", vch_r[:, 0:8 * CO], (128, 8 * CO), BF16)
            p.dump("ktok0", ktok_r[:, 0:D], (128, D), BF16)
            p.dump("vtok0", vtok_r[:, 0:D], (128, D), BF16)
            p.dump("beta0", beta_c[:, 0:4], (128, 4))
            p.dump("rk0", rk_c[:, 0:4], (128, 4))
            p.dump("ns0", ns_c[:, 0:4], (128, 4))
            p.dump("rkb0", rkb_c[:, 0:4], (128, 4))
            p.dump("tp0", tp_r[:, 0:H * 128], (128, H * 128), BF16)
        if o == NCO - 1:
            w8(nc.vector, cv["v"])
            iv = None
            for j in range(8):
                iv = nc.vector.tensor_copy(
                    v_ch[:, j * (T + 64):j * (T + 64) + 64],
                    vch_r[:, ((o % 2) * 8 + j) * CO + CO - 64:((o % 2) * 8 + j + 1) * CO])
            p.inc(iv, DV)

    fill_all()
    w8(nc.vector, *st["s_toks"])
    w8(nc.vector, tok(DV), lds)
    i = None
    for t in ("q", "k", "v"):
        for j in range(8):
            b0 = j * (CO + 4)
            i = nc.vector.tensor_scalar(lin[t][:, b0 + 1:b0 + 4],
                                        lin[t][:, b0 + 1:b0 + 4],
                                        gate_c[:, 0:1], None, AL.mult)
    for j in range(8):
        i = nc.vector.tensor_scalar(v_ch[:, j * (T + 64):j * (T + 64) + 64],
                                    v_ch[:, j * (T + 64):j * (T + 64) + 64],
                                    gate_c[:, 0:1], None, AL.mult)
    for h in range(H):
        i = nc.vector.tensor_scalar(S_sb[h][:], S_sb[h][:], gate_c[:, 0:1],
                                    None, AL.mult)
    p.inc(i, DV)
    st["halo"] = tok(DV)
    w8(nc.scalar, tok(DV))
    for h in range(H):
        i = nc.scalar.copy(Sb_sb[h][:], S_sb[h][:])
    p.inc(i, AC)
    st["s_toks"] = [tok(AC) for _ in range(H)]
    p.dump("pre_S0", S_sb[0][:], (128, 512))
    p.dump("pre_S3", S_sb[3][:], (128, 512))

    if stop_after == "prefix":
        p.finish()
        return

    xd = cast_x_chunk(0, "main")
    for o in range(NCO):
        cv, bt, kvt, colst = front(o, "main", ("k", "v", "q"), xd)
        xd = st.get("xd_next")
        prep = delta_prep_chunk(o, "main", cv, kvt, colst)
        fill_all()
        st["filler"] = scan_stages(o, "main", prep, kvt)

    fill_all()
    p.dump("do_tok", do_tok[:], (128, NT * D), BF16)
    p.dump("vch", v_ch[:], (128, 8 * (T + 64)), BF16)
    p.dump("main_S0", S_sb[0][:], (128, 512))
    if stop_after == "scan":
        p.finish()
        return

    W1xT_b = p.sb("W1xT_b", (128, 8 * D), BF16, offset=R_W)
    oprojT_b = p.sb("oprojT_b", (128, 8 * D), BF16, offset=R_W + 16384)
    gx_b = p.sb("gx_b", (128, 8 * 512), BF16, offset=R_W + 32768)
    hdn_b = p.sb("hdn_b", (128, 8 * 512), BF16, offset=R_W + 40960)
    pr = R_DR
    do_blk = p.sb("do_blk", (128, 8 * 512), BF16, offset=pr); pr += 8192
    ls_blk = p.sb("ls_blk", (128, 8 * 512), BF16, offset=pr); pr += 8192
    ll_blk = p.sb("ll_blk", (128, 8 * 512), BF16, offset=pr); pr += 8192
    oroll = p.sb("oroll", (128, 8 * 512), BF16, offset=pr); pr += 8192
    absq = p.sb("absq", (128, 8 * 512), BF16, offset=pr)
    sqscr = p.sb("sqscr", (128, D), F32, offset=pr); pr += 8192
    x1t = p.sb("x1t", (128, D), F32, offset=pr); pr += 4096
    WA_b = p.sb("WA_b", (128, D), BF16, offset=pr); pr += 2048
    WB_b = p.sb("WB_b", (128, D), BF16, offset=pr); pr += 2048
    wpad = p.sb("wpad", (128, D), BF16, offset=pr); pr += 2048
    fdiag = p.sb("fdiag", (128, 5 * 128), BF16, offset=pr - 2048)
    win_b = p.sb("win_b", (128, D), BF16, offset=pr)
    statsp = p.sb("statsp", (128, 3 * 512), F32, offset=pr)
    xg_b = p.sb("xg_b", (128, 8 * 512), BF16, offset=pr)
    Pa_b = p.sb("Pa_b", (128, D), BF16, offset=pr + 2048)
    Pb_b = p.sb("Pb_b", (128, D), BF16, offset=pr + 4096)
    statstok = p.sb("statstok", (128, 4 * 80), F32, offset=pr + 6400); pr += 8192
    cmbb4 = p.sb("cmbb4", (128, 4 * 512), BF16, offset=pr); pr += 4096
    selbank = p.sb("selbank", (128, 256), BF16, offset=pr); pr += 512
    lgt = p.sb("lgt", (128, 64), F32, offset=pr); pr += 256
    fwc = p.sb("fwc", (128, 48), F32, offset=pr); pr += 192
    cmbc = p.sb("cmbc", (128, 32), BF16, offset=pr); pr += 64
    lncol = p.sb("lncol", (128, 8), F32, offset=pr); pr += 32
    tempc = p.sb("tempc", (128, 4), F32, offset=pr); pr += 32
    ident_f = p.sb("ident_f", (128, 128), F32, offset=pr); pr += 512
    fftF = p.sb("fftF", (128, 128), BF16, offset=pr); pr += 256
    fftG1 = p.sb("fftG1", (128, 64), BF16, offset=pr); pr += 128
    fftG2 = p.sb("fftG2", (128, 64), BF16, offset=pr); pr += 128
    fftFA = p.sb("fftFA", (128, 128), BF16, offset=pr); pr += 256
    fftFB = p.sb("fftFB", (128, 128), BF16, offset=pr); pr += 256
    b1_b = p.sb("b1_b", (128, 8), F32, offset=pr); pr += 64
    W2T_b = p.sb("W2T_b", (128, 8 * 4), BF16, offset=pr); pr += 64
    assert pr <= s_off, (pr, s_off)
    cl = R_DR + 83104
    selrows_b = p.sb("selrows_b", (16, 16 * 128), BF16, offset=cl); cl += 4096
    CL_W1S4 = cl
    W1s4A = p.sb("W1s4A", (64, D), BF16, offset=cl); cl += 2048
    W1s4B = p.sb("W1s4B", (64, D), BF16, offset=cl); cl += 2048
    cmbrow = p.sb("cmbrow", (16, 512), BF16, offset=cl); cl += 1024
    rstdrow = p.sb("rstdrow", (16, 512), BF16, offset=cl); cl += 1024
    stats_allA = p.sb("stats_allA", (64, 512), BF16, offset=cl); cl += 1024
    stats_allB = p.sb("stats_allB", (64, 512), BF16, offset=cl); cl += 1024
    b2_b = p.sb("b2_b", (1, 4), BF16)
    assert cl <= R_DR + 95392, cl

    s_toks = st["s_toks"]
    s_tok = s_toks[-1]
    w8(nc.gpsimd, *s_toks)
    w8(nc.gpsimd, tok(PE), tok(DV), tok(AC))
    w8(nc.sync, *s_toks)
    w8(nc.sync, tok(PE), tok(DV), tok(AC))
    g2_ = lambda dst, src, sem: p.inc(nc.gpsimd.dma_start(out=dst, in_=src), sem, DMA)
    g2_(W1xT_b[:].rearrange("p (k co) -> p k co", k=8),
        din["W1xT"].rearrange("(k p) co -> p k co", p=128), "ldp")
    g2_(oprojT_b[:].rearrange("p (k co) -> p k co", k=8),
        din["oprojT"].rearrange("(k p) co -> p k co", p=128), "ldp")
    g2_(selrows_b[:], din["selrows"], "ldp")
    g2_2 = lambda dst, src, sem: p.inc(nc.sync.dma_start(out=dst, in_=src), sem, DMA)
    g2_2(ident_f[:], din["ident"], "ldps")
    g2_(W1s4A[:], din["W1sT4"][0:64, :], "ldp")
    g2_(W1s4B[:], din["W1sT4"][64:128, :], "ldp")
    g2_(b1_b[:], din["b1"], "ldp")
    g2_(W2T_b[:].rearrange("p (k j) -> p k j", k=8),
        din["W2T"].rearrange("(k p) j -> p k j", p=128), "ldp")
    g2_(b2_b[:], din["b2row"], "ldp")
    g2_(wpad[0:64, :], din["fir_lT"], "ldp")
    p.sem("ldps")
    ldp = [tok("ldp"), tok("ldps")]

    w8(nc.vector, lds)
    i = nc.vector.tensor_scalar(tempc[:, 2:3], ltemp_c[:, 0:1], 0.0, None, AL.max)
    p.inc(i, DV)
    w8(nc.vector, tok(DV))
    i = nc.vector.scalar_tensor_tensor(tempc[:, 1:2], tempc[:, 2:3], 2.0,
                                       ltemp_c[:, 0:1], AL.mult, AL.subtract)
    p.inc(i, DV)
    w8(nc.scalar, tok(DV))
    i = nc.scalar.activation(tempc[:, 3:4], tempc[:, 1:2], AF.Exp, scale=-1.0)
    p.inc(i, AC)
    w8(nc.scalar, tok(AC))
    i = nc.scalar.activation(tempc[:, 1:2], tempc[:, 3:4], AF.Ln, bias=1.0)
    p.inc(i, AC)
    w8(nc.vector, tok(AC))
    i = nc.vector.tensor_tensor(tempc[:, 2:3], tempc[:, 2:3], tempc[:, 1:2], AL.add)
    i = nc.vector.tensor_scalar(tempc[:, 2:3], tempc[:, 2:3], 1e-4, None, AL.add)
    i = nc.vector.reciprocal(tempc[:, 0:1], tempc[:, 2:3])
    p.inc(i, DV)
    temp_tok = tok(DV)

    g2_(fftFA[:], din["fft_faT"], "ldp")
    g2_(fftFB[:], din["fft_fbT"], "ldp")
    g2_(fftF[:], din["fft_fT"], "ldp")
    g2_(fftG1[:], din["fft_g1T"], "ldp")
    g2_(fftG2[:], din["fft_g2T"], "ldp")
    ldp = [tok("ldp"), tok("ldps")]
    w8(nc.vector, *ldp)
    i = nc.vector.memset(wpad[64:128, :], 0.0)
    p.inc(i, DV)
    wp_tok = tok(DV)
    for half in range(2):
        for mat, dst in ((fftFA, WA_b), (fftFB, WB_b)):
            bi, bank, btok = pp.acquire()
            w8(nc.tensor, btok, wp_tok, *ldp)
            mm = nc.tensor.matmul(bank, mat[:], wpad[:, half * 512:(half + 1) * 512],
                                  start=True, stop=True)
            p.inc(mm, PE)
            w8(nc.vector, tok(PE))
            i = nc.vector.tensor_copy(dst[:, half * 512:(half + 1) * 512], bank)
            p.inc(i, DV)
            pp.release(bi, tok(DV))
    wab_tok = tok(DV)

    w8(nc.vector, tok(DV))
    i = nc.vector.memset(selbank[:], 0.0)
    p.inc(i, DV)
    w8(nc.vector, tok(DV))
    i = nc.vector.memset(selbank[:, 128:129], 1.0 / 256.0)
    p.inc(i, DV)
    sel_tok = tok(DV)

    x1_store_toks = []
    st["absq_free"] = tok(DV)
    st["blk_free"] = ()

    def post_front(n):
        t0 = n * 512
        res = {}
        winS = [win_b[:, 0:1024], absq[:, 2048:3072]]
        PaS = [Pa_b[:, 0:1024], absq[:, 3072:4096]]
        PbS = [Pb_b[:, 0:1024], absq[:, 2048:3072]]
        wtok = {}
        ptok = {}
        dotoks = []
        lltoks = []
        lstoks = []

        def do_tr(ti):
            for jp in (0, 4):
                bi, bank, btok = pp.acquire()
                w8(nc.tensor, btok, s_tok)
                for j in range(jp, jp + 4):
                    mm = nc.tensor.matmul(
                        bank[:, (j - jp) * 128:(j - jp + 1) * 128],
                        do_tok[:, (4 * n + ti) * D + j * 128:(4 * n + ti) * D + (j + 1) * 128],
                        ident_b[:], start=True, stop=True)
                p.inc(mm, PE)
                w8(nc.scalar, tok(PE), *st["blk_free"])
                for j in range(jp, jp + 4):
                    i = nc.scalar.copy(do_blk[:, j * 512 + ti * 128:j * 512 + (ti + 1) * 128],
                                       bank[:, (j - jp) * 128:(j - jp + 1) * 128])
                p.inc(i, AC)
                pp.release(bi, tok(AC))
            dotoks.append(tok(AC))
            if len(dotoks) == 4:
                res["doT"] = dotoks[-1]

        def win_tr(s):
            par = s % 2
            b = 8 * n + s
            for jp in (0, 4):
                bi, bank, btok = pp.acquire()
                w8(nc.tensor, btok, wab_tok)
                for j in range(jp, jp + 4):
                    mm = nc.tensor.matmul(
                        bank[:, (j - jp) * 128:(j - jp + 1) * 128],
                        v_ch[:, j * (T + 64) + 64 * b:j * (T + 64) + 64 * b + 128],
                        ident_b[:], start=True, stop=True)
                p.inc(mm, PE)
                w8(nc.scalar, tok(PE), st.get("absq_free") if par else None,
                   *st["blk_free"])
                for j in range(jp, jp + 4):
                    i = nc.scalar.copy(winS[par][:, j * 128:(j + 1) * 128],
                                       bank[:, (j - jp) * 128:(j - jp + 1) * 128])
                p.inc(i, AC)
                pp.release(bi, tok(AC))
            wtok[s] = tok(AC)

        def fwd(s):
            par = s % 2
            for half in range(2):
                bi, bank, btok = pp.acquire()
                w8(nc.tensor, btok, wtok[s])
                mm = nc.tensor.matmul(bank, fftF[:], winS[par][:, half * 512:(half + 1) * 512],
                                      start=True, stop=True)
                p.inc(mm, PE)
                w8(nc.vector, tok(PE), wab_tok,
                   st.get("absq_free") if par else None, *st["blk_free"])
                i = nc.vector.tensor_tensor(PaS[par][:, half * 512:(half + 1) * 512],
                                            bank, WA_b[:, half * 512:(half + 1) * 512], AL.mult)
                i = nc.vector.tensor_tensor(PbS[par][:, half * 512:(half + 1) * 512],
                                            bank, WB_b[:, half * 512:(half + 1) * 512], AL.mult)
                p.inc(i, DV)
                pp.release(bi, tok(DV))
            ptok[s] = tok(DV)

        def inv(s):
            par = s % 2
            bi, bank, btok = pp.acquire()
            w8(nc.tensor, btok, ptok[s])
            for j in range(8):
                mm = nc.tensor.matmul(bank[:, j * 64:(j + 1) * 64],
                                      PaS[par][:, j * 128:(j + 1) * 128], fftG1[:],
                                      start=True, stop=False)
                mm = nc.tensor.matmul(bank[:, j * 64:(j + 1) * 64],
                                      PbS[par][:, j * 128:(j + 1) * 128], fftG2[:],
                                      start=False, stop=True)
            p.inc(mm, PE)
            w8(nc.vector, tok(PE), *st["blk_free"])
            i = nc.vector.tensor_copy(
                ll_blk[:].rearrange("p (j t) -> p j t", j=8)[:, :, s * 64:(s + 1) * 64],
                bank[:].rearrange("p (j t) -> p j t", j=8))
            p.inc(i, DV)
            lltoks.append(tok(DV))
            if len(lltoks) == 8:
                res["ll"] = lltoks[-1]
            pp.release(bi, tok(DV))

        def fs(j):
            w8(nc.vector, lds, tok(PE))
            for d in range(5):
                i = nc.vector.tensor_scalar(
                    fdiag[:, d * 128:(d + 1) * 128], ident_b[:],
                    fir_s_s[:, j * 5 + d:j * 5 + d + 1], None, AL.mult)
            p.inc(i, DV)
            fd_tok = tok(DV)
            bi, bank, btok = pp.acquire()
            w8(nc.tensor, btok, fd_tok)
            vb = j * (T + 64) + 64 + t0
            for d in range(5):
                mm = nc.tensor.matmul(bank, fdiag[:, d * 128:(d + 1) * 128],
                                      v_ch[:, vb + d - 4:vb + d - 4 + 512],
                                      start=(d == 0), stop=(d == 4))
            p.inc(mm, PE)
            w8(nc.scalar, tok(PE), *st["blk_free"])
            i = nc.scalar.copy(ls_blk[:, j * 512:(j + 1) * 512], bank)
            p.inc(i, AC)
            lstoks.append(tok(AC))
            if len(lstoks) == 8:
                res["ls"] = lstoks[-1]
            pp.release(bi, tok(AC))

        def xg_gx():
            xg_v = do_tok[:, 4 * n * D:4 * n * D + 8 * 512]
            w8(nc.sync, res["doT"])
            for sp in ("xsp0", "xsp1"):
                if sp in p.sems:
                    p.wait_val(nc.sync, sp, p.semv[sp])
            p.inc(nc.sync.dma_start(
                out=xg_v.rearrange("p (k t) -> p k t", k=8),
                in_=p.d_xch.rearrange("p (k t) -> p k t", k=8)[:, :, t0:t0 + 512]),
                f"xg{n % 2}", DMA)
            xg_tok = (f"xg{n % 2}", p.semv[f"xg{n % 2}"])
            for m in range(8):
                bi, bank, btok = pp.acquire()
                w8(nc.tensor, btok, xg_tok, *ldp)
                for k in range(8):
                    mm = nc.tensor.matmul(bank, W1xT_b[:, (k * 8 + m) * 128:(k * 8 + m + 1) * 128],
                                          xg_v[:, k * 512:(k + 1) * 512],
                                          start=(k == 0), stop=(k == 7))
                p.inc(mm, PE)
                w8(nc.vector, tok(PE))
                i = nc.vector.tensor_copy(gx_b[:, m * 512:(m + 1) * 512], bank)
                p.inc(i, DV)
                pp.release(bi, tok(DV))
            res["gx"] = tok(DV)

        stages = [
            lambda: (do_tr(0), do_tr(1)),
            lambda: (do_tr(2), do_tr(3), win_tr(0), win_tr(1)),
            lambda: (xg_gx(), fwd(0), fs(0), fs(1)),
            lambda: (inv(0), win_tr(2), fwd(1), fs(2), fs(3)),
            lambda: (fwd(2), win_tr(4), inv(1), win_tr(3), fs(4), fs(5)),
            lambda: (inv(2), fwd(4), win_tr(6), fwd(3), inv(3), win_tr(5),
                     fs(6), fs(7)),
            lambda: (inv(4), fwd(6), fwd(5), inv(5), win_tr(7)),
            lambda: (inv(6), fwd(7), inv(7)),
        ]
        return stages, res

    st["pfill"] = []

    def pfill(k=1):
        for _ in range(k):
            if st["pfill"]:
                st["pfill"].pop(0)()

    pf, pf_res = post_front(0)
    for f_ in pf:
        f_()
    for n in range(4):
        t0 = n * 512
        doT_tok = pf_res["doT"]
        ll_tok = pf_res["ll"]
        ls_tok = pf_res["ls"]
        gx_tok = pf_res["gx"]
        if n < 4 - 1:
            st["pfill"], pf_res = post_front(n + 1)
        else:
            st["pfill"], pf_res = [], None
        if n == 0:
            p.dump("ls_b0", ls_blk[:], (128, 8 * 512), BF16)
            p.dump("ll_b0", ll_blk[:], (128, 8 * 512), BF16)
            p.dump("do_b0", do_blk[:], (128, 8 * 512), BF16)
        srcs = [("ls", ls_blk, ls_tok), ("ll", ll_blk, ll_tok),
                ("do", do_blk, doT_tok), ("v", None, None)]
        vblk = v_ch[:].rearrange("p (j t) -> p j t", j=8)[:, :, 64 + t0:64 + t0 + 512]
        kindmm = []
        slot_mm = [None] * 4
        ent = [tok(PE), tok(AC), tok(DV), tok(PL)]
        for kind in range(3):
            bi, bank, btok = pp.acquire()
            bview = bank
            first = True
            for si, (sname, stens, stok_) in enumerate(srcs):
                for j in range(8):
                    h = j // 2
                    c = h * 4 + si
                    if kind == 0:
                        rhs = (vblk[:, j, :] if stens is None
                               else stens[:, j * 512:(j + 1) * 512])
                        dep = stok_ if stok_ else s_tok
                        w8(nc.tensor, btok, dep, sel_tok)
                    else:
                        sl_ = (si * 8 + j) % 4
                        slv = absq[:, sl_ * 512:(sl_ + 1) * 512]
                        srcv = (vblk[:, j, :] if stens is None
                                else stens[:, j * 512:(j + 1) * 512])
                        ew = [stok_ if stok_ else s_tok, slot_mm[sl_]]
                        if slot_mm[sl_] is None:
                            ew += ent
                        if kind == 1:
                            w8(nc.scalar, *ew)
                            i = nc.scalar.activation(slv, srcv, AF.Abs)
                            p.inc(i, AC)
                            bdep = tok(AC)
                        else:
                            w8(nc.vector, *ew)
                            i = nc.vector.tensor_tensor(slv, srcv, srcv, AL.mult)
                            p.inc(i, DV)
                            bdep = tok(DV)
                        rhs = slv
                        w8(nc.tensor, btok, bdep)
                    mm = nc.tensor.matmul(bview, selbank[:, 128 - c:256 - c], rhs,
                                          start=first, stop=(si == 3 and j == 7))
                    first = False
                    if kind != 0:
                        p.inc(mm, PE)
                        slot_mm[(si * 8 + j) % 4] = tok(PE)
            if kind == 0:
                p.inc(mm, PE)
            w8(nc.vector, tok(PE))
            i = nc.vector.tensor_copy(statsp[:, kind * 512:(kind + 1) * 512], bview)
            p.inc(i, DV)
            pp.release(bi, tok(DV))
            kindmm.append(tok(DV))
        for ti in range(4):
            bi, bank, btok = pp.acquire(cols=192)
            w8(nc.tensor, btok, kindmm[-1])
            for kind in range(3):
                mm = nc.tensor.matmul(bank[:, kind * 64:kind * 64 + 16],
                                      statsp[0:16, kind * 512 + ti * 128:kind * 512 + (ti + 1) * 128],
                                      ident_f[0:16, 0:16], start=True, stop=True)
            p.inc(mm, PE)
            w8(nc.vector, tok(PE))
            i = nc.vector.tensor_copy(statstok[:, 0 + ti * 16:16 + ti * 16], bank[:, 0:16])
            i = nc.vector.tensor_copy(statstok[:, 128 + ti * 16:144 + ti * 16], bank[:, 64:80])
            i = nc.vector.tensor_copy(statstok[:, 256 + ti * 16:272 + ti * 16], bank[:, 128:144])
            p.inc(i, DV)
            pp.release(bi, tok(DV))
        w8(nc.vector, tok(DV))
        i = nc.vector.tensor_tensor(statstok[:, 64:128], statstok[:, 0:64],
                                    statstok[:, 0:64], AL.mult)
        p.inc(i, DV)
        w8(nc.vector, tok(DV))
        i = nc.vector.tensor_tensor(statstok[:, 64:128], statstok[:, 256:320],
                                    statstok[:, 64:128], AL.subtract)
        p.inc(i, DV)
        w8(nc.scalar, tok(DV))
        i = nc.scalar.activation(sqscr[:, 0:64], statstok[:, 256:320],
                                 AF.Ln, scale=256.0)
        p.inc(i, AC)
        w8(nc.scalar, tok(AC))
        i = nc.scalar.activation(statstok[:, 192:256], sqscr[:, 0:64],
                                 AF.Exp, scale=0.5)
        p.inc(i, AC)
        der_tok = tok(AC)
        w8(nc.vector, der_tok)
        for ti in range(4):
            i = nc.vector.tensor_copy(
                sqscr[:, 64 + 64 * ti:128 + 64 * ti].rearrange(
                    "p (h k s) -> p h k s", h=4, k=4, s=4),
                statstok[:].rearrange("p (k t h s) -> p t h k s",
                                      k=5, t=4, h=4, s=4)[:, ti, :, 0:4, :])
            p.inc(i, DV)
        der_tok = tok(DV)
        for ti in range(4):
            for h in range(H):
                bi, bank, btok = pp.acquire(cols=128)
                view = sqscr[:, 64 + 64 * ti + 16 * h:80 + 64 * ti + 16 * h]
                w8(nc.tensor, btok, der_tok)
                mm = nc.tensor.matmul(bank[0:16, :], view, ident_f[:],
                                      start=True, stop=True)
                p.inc(mm, PE)
                w8(nc.vector, tok(PE))
                sa = stats_allA if h < 2 else stats_allB
                i = nc.vector.tensor_copy(sa[32 * (h % 2):32 * (h % 2) + 16,
                                             ti * 128:(ti + 1) * 128], bank[0:16, :])
                p.inc(i, DV)
                pp.release(bi, tok(DV))
        stats_tok = tok(DV)
        if n == 0:
            p.dump("statsA_b0", stats_allA[0:16, :], (16, 512), BF16)
        lgmm_tok = None
        for h in range(H):
            Wsa = W1s4A if h < 2 else W1s4B
            sa = stats_allA if h < 2 else stats_allB
            mbanks = []
            for m in range(8):
                bi, bank, btok = pp.acquire()
                w8(nc.tensor, btok, stats_tok, *ldp)
                mm = nc.tensor.matmul(bank, Wsa[32 * (h % 2):32 * (h % 2) + 16,
                                              m * 128:(m + 1) * 128],
                                      sa[32 * (h % 2):32 * (h % 2) + 16, :],
                                      start=True, stop=True)
                p.inc(mm, PE)
                mbanks.append((bi, bank, tok(PE)))
            adds = []
            for m, (bi, bank, pet) in enumerate(mbanks):
                w8(nc.vector, pet, gx_tok, lgmm_tok)
                i = nc.vector.tensor_tensor(hdn_b[:, m * 512:(m + 1) * 512], bank,
                                            gx_b[:, m * 512:(m + 1) * 512], AL.add)
                p.inc(i, DV)
                adds.append(tok(DV))
                pp.release(bi, tok(DV))
            for m in range(8):
                w8(nc.scalar, adds[m])
                i = nc.scalar.activation(hdn_b[:, m * 512:(m + 1) * 512],
                                         hdn_b[:, m * 512:(m + 1) * 512],
                                         AF.Gelu, bias=b1_b[:, m:m + 1])
                p.inc(i, AC)
            hdn_tok = tok(AC)
            for ti in range(4):
                bi, bank, btok = pp.acquire(cols=4)
                w8(nc.tensor, btok, hdn_tok, *ldp)
                for k in range(8):
                    mm = nc.tensor.matmul(bank, hdn_b[:, k * 512 + ti * 128:k * 512 + (ti + 1) * 128],
                                          W2T_b[:, k * 4:(k + 1) * 4],
                                          start=(k == 0), stop=False)
                mm = nc.tensor.matmul(bank, ones_row[:, 0:128], b2_b[:],
                                      start=False, stop=True)
                p.inc(mm, PE)
                w8(nc.vector, tok(PE))
                i = nc.vector.tensor_copy(lgt[:, ti * 16 + h * 4:ti * 16 + (h + 1) * 4], bank)
                p.inc(i, DV)
                pp.release(bi, tok(DV))
            lgmm_tok = tok(PE)
        lg_tok = tok(DV)
        tr_prev = {}
        for ti in range(4):
            par = ti % 2
            for h in range(H):
                c4 = ti * 16 + h * 4
                s3 = (ti * 4 + h) * 3
                dst = cmbc[:, par * 16 + h * 4:par * 16 + h * 4 + 4]
                w8(nc.vector, lg_tok, temp_tok, tr_prev.get(par))
                i = nc.vector.reduce_max(fwc[:, s3:s3 + 1], lgt[:, c4:c4 + 4],
                                         axis=AX.X, negate=True)
                i = nc.vector.tensor_scalar(lgt[:, c4:c4 + 4], lgt[:, c4:c4 + 4],
                                            fwc[:, s3:s3 + 1], tempc[:, 0:1],
                                            AL.add, AL.mult)
                p.inc(i, DV)
                w8(nc.scalar, tok(DV))
                i = nc.scalar.activation(lgt[:, c4:c4 + 4], lgt[:, c4:c4 + 4], AF.Exp,
                                         accum_out=fwc[:, s3 + 1:s3 + 2])
                p.inc(i, AC)
                w8(nc.vector, tok(AC))
                i = nc.vector.reciprocal(fwc[:, s3 + 2:s3 + 3], fwc[:, s3 + 1:s3 + 2])
                i = nc.vector.tensor_scalar(dst, lgt[:, c4:c4 + 4],
                                            fwc[:, s3 + 2:s3 + 3], None, AL.mult)
                i = nc.vector.tensor_tensor(dst[:, 0:1], dst[:, 0:1],
                                            ress_c[:, 0:1], AL.add)
                i = nc.vector.tensor_tensor(dst[:, 1:2], dst[:, 1:2],
                                            resl_c[:, 0:1], AL.add)
                p.inc(i, DV)
            cmb_dv = tok(DV)
            bi, bank, btok = pp.acquire(cols=128)
            w8(nc.tensor, btok, cmb_dv)
            mm = nc.tensor.matmul(bank[0:16, :], cmbc[:, par * 16:par * 16 + 16],
                                  ident_b[:], start=True, stop=True)
            p.inc(mm, PE)
            tr_prev[par] = tok(PE)
            w8(nc.vector, tok(PE))
            i = nc.vector.tensor_copy(cmbrow[0:16, ti * 128:(ti + 1) * 128], bank[0:16, :])
            p.inc(i, DV)
            pp.release(bi, tok(DV))
        cmb_tok = tok(DV)
        if n == 0:
            p.dump("cmbrow_b0", cmbrow[:], (16, 512), BF16)
        oproj_prev = st.get("oproj_pe", ld_dv)
        mix_pl = None
        for h in range(H):
            for src_ in range(4):
                bi, bank, btok = pp.acquire()
                w8(nc.tensor, btok, cmb_tok)
                r = h * 4 + src_
                mm = nc.tensor.matmul(bank, selrows_b[0:16, r * 128:(r + 1) * 128],
                                      cmbrow[0:16, :], start=True, stop=True)
                p.inc(mm, PE)
                w8(nc.vector, tok(PE), tok(PL))
                i = nc.vector.tensor_copy(cmbb4[:, src_ * 512:(src_ + 1) * 512], bank)
                p.inc(i, DV)
                pp.release(bi, tok(DV))
            bc_tok = tok(DV)
            w8(nc.gpsimd, bc_tok, ls_tok, ll_tok, doT_tok, oproj_prev, tok(AC))
            w8(nc.vector, bc_tok, ls_tok, ll_tok, doT_tok, oproj_prev, tok(AC))
            je, jo = 2 * h, 2 * h + 1
            jse = slice(je * 512, (je + 1) * 512)
            jso = slice(jo * 512, (jo + 1) * 512)
            steps = [
                lambda e, js, j: e.tensor_tensor(oroll[:, js], ls_blk[:, js],
                                                 cmbb4[:, 0:512], AL.mult),
                lambda e, js, j: e.tensor_tensor(absq[:, js], ll_blk[:, js],
                                                 cmbb4[:, 512:1024], AL.mult),
                lambda e, js, j: e.tensor_tensor(oroll[:, js], oroll[:, js],
                                                 absq[:, js], AL.add),
                lambda e, js, j: e.tensor_tensor(absq[:, js], do_blk[:, js],
                                                 cmbb4[:, 1024:1536], AL.mult),
                lambda e, js, j: e.tensor_tensor(oroll[:, js], oroll[:, js],
                                                 absq[:, js], AL.add),
                lambda e, js, j: e.tensor_tensor(absq[:, js], vblk[:, j, :],
                                                 cmbb4[:, 1536:2048], AL.mult),
                lambda e, js, j: e.tensor_tensor(oroll[:, js], oroll[:, js],
                                                 absq[:, js], AL.add),
            ]
            for f_ in steps:
                i = f_(nc.vector, jse, je)
                p.inc(i, DV)
                i = f_(nc.gpsimd, jso, jo)
                p.inc(i, PL)
            mix_pl = tok(PL)
        mix_tok = tok(DV)
        w8(nc.vector, mix_pl)
        st["blk_free"] = (mix_tok, mix_pl)
        if n == 0:
            p.dump("omix_b0", oroll[:], (128, 8 * 512), BF16)
        w8(nc.vector, mix_tok)
        i = nc.vector.tensor_tensor(absq[:], oroll[:], oroll[:], AL.mult)
        p.inc(i, DV)
        osq_tok = tok(DV)
        bi, bank, btok = pp.acquire()
        w8(nc.tensor, btok, osq_tok, sel_tok)
        for j in range(8):
            c = j // 2
            mm = nc.tensor.matmul(bank[0:16, :], selbank[:, 128 - c:144 - c],
                                  absq[:, j * 512:(j + 1) * 512],
                                  start=(j == 0), stop=(j == 7))
        p.inc(mm, PE)
        st["absq_free"] = tok(PE)
        pfill(2)
        w8(nc.scalar, tok(PE))
        i = nc.scalar.activation(sqscr[0:16, 0:512],
                                 bank[0:16, :], AF.Ln, bias=epscol[0:16, 0:1])
        p.inc(i, AC)
        w8(nc.scalar, tok(AC))
        i = nc.scalar.activation(rstdrow[:], sqscr[0:16, 0:512], AF.Exp, scale=-0.5)
        p.inc(i, AC)
        rstd_tok = tok(AC)
        pp.release(bi, rstd_tok)
        if n == 0:
            p.dump("rstd_b0", rstdrow[:], (16, 512), BF16)
        for h in range(H):
            bi, bank, btok = pp.acquire()
            w8(nc.tensor, btok, rstd_tok)
            mm = nc.tensor.matmul(bank, selrows_b[0:16, h * 128:(h + 1) * 128],
                                  rstdrow[0:16, :], start=True, stop=True)
            p.inc(mm, PE)
            w8(nc.vector, tok(PE), tok(PL))
            i = nc.vector.tensor_copy(cmbb4[:, 0:512], bank)
            p.inc(i, DV)
            rb_tok = tok(DV)
            pp.release(bi, rb_tok)
            for j in (2 * h, 2 * h + 1):
                w8(nc.vector, rb_tok)
                i = nc.vector.scalar_tensor_tensor(oroll[:, j * 512:(j + 1) * 512],
                                                   oroll[:, j * 512:(j + 1) * 512],
                                                   onorm_b[:, j:j + 1],
                                                   cmbb4[:, 0:512], AL.mult, AL.mult)
                p.inc(i, DV)
                o_tok = tok(DV)
            pfill(1)
        p.dump(f"o_blk{n}", oroll[:], (128, 8 * 512), BF16)
        for ti in range(4):
            pfill(1)
            w8(nc.sync, st["x1t_free"][0])
            p.inc(nc.sync.dma_start(out=x1t[:],
                                    in_=din["x_own"][t0 + ti * 128:t0 + (ti + 1) * 128, :]),
                  "xf0", DMA)
            xf_tok = ("xf0", p.semv["xf0"])
            banks = []
            for half in range(2):
                bi, bank, btok = pp.acquire()
                w8(nc.tensor, btok, o_tok, *ldp)
                for k in range(8):
                    mm = nc.tensor.matmul(bank,
                                          oroll[:, k * 512 + ti * 128:k * 512 + (ti + 1) * 128],
                                          oprojT_b[:, k * D + half * 512:k * D + (half + 1) * 512],
                                          start=(k == 0), stop=(k == 7))
                p.inc(mm, PE)
                banks.append((bi, bank, tok(PE)))
            st["oproj_pe"] = tok(PE)
            x1v = x1t[:]
            for half, (bi, bank, pet) in enumerate(banks):
                w8(nc.vector, pet, xf_tok)
                i = nc.vector.tensor_tensor(x1v[:, half * 512:(half + 1) * 512],
                                            x1v[:, half * 512:(half + 1) * 512],
                                            bank, AL.add)
                p.inc(i, DV)
                pp.release(bi, tok(DV))
            w8(nc.vector, tok(DV))
            i = nc.vector.reduce_sum(lncol[:, 0:1], x1v, axis=AX.X)
            p.inc(i, DV)
            w8(nc.scalar, tok(DV))
            i = nc.scalar.activation(sqscr[:], x1v, AF.Square, accum_out=lncol[:, 1:2])
            p.inc(i, AC)
            w8(nc.vector, tok(AC))
            i = nc.vector.tensor_scalar(lncol[:, 2:3], lncol[:, 0:1], 1.0 / D, None, AL.mult)
            p.inc(i, DV)
            w8(nc.vector, tok(DV))
            i = nc.vector.tensor_tensor(lncol[:, 3:4], lncol[:, 2:3], lncol[:, 2:3], AL.mult)
            i = nc.vector.tensor_scalar(lncol[:, 4:5], lncol[:, 1:2], 1.0 / D, None, AL.mult)
            p.inc(i, DV)
            w8(nc.vector, tok(DV))
            i = nc.vector.tensor_tensor(lncol[:, 4:5], lncol[:, 4:5], lncol[:, 3:4], AL.subtract)
            p.inc(i, DV)
            w8(nc.scalar, tok(DV))
            i = nc.scalar.activation(lncol[:, 5:6], lncol[:, 4:5], AF.Ln, bias=epscol[:, 0:1])
            p.inc(i, AC)
            w8(nc.scalar, tok(AC))
            i = nc.scalar.activation(lncol[:, 6:7], lncol[:, 5:6], AF.Exp, scale=-0.5)
            p.inc(i, AC)
            w8(nc.vector, tok(AC))
            i = nc.vector.tensor_scalar(x1v, x1v, lncol[:, 2:3], lncol[:, 6:7],
                                        AL.subtract, AL.mult)
            p.inc(i, DV)
            x1_tok = tok(DV)
            w8(nc.sync, x1_tok)
            p.inc(nc.sync.dma_start(out=p.d_x1[t0 + ti * 128:t0 + (ti + 1) * 128, :],
                                    in_=x1v), "x1sp", DMA)
            st["x1t_free"][0] = ("x1sp", p.semv["x1sp"])
        if n == 0:
            p.dump("x1_b0", x1t[:, 0:D], (128, D))
    p.dump("x1d", p.d_x1[:], (T, D))
    if stop_after == "post":
        p.finish()
        return

        while st["pfill"]:
            st["pfill"].pop(0)()

    x1_ch = p.sb("x1_ch", (128, 8 * T), BF16, offset=R_A)
    ffn_ch = p.sb("ffn_ch", (128, 8 * T), BF16, offset=R_C)
    Wg_b = p.sb("Wg_b", (128, 8 * D), BF16, offset=R_W)
    Web = [p.sb(f"We{i}", (128, 8 * D), BF16, offset=R_W + 16384 * (i + 1)) for i in range(2)]
    fr = R_DR
    iota_t = p.sb("iota_t", (128, D), F32, offset=fr); fr += 4096
    lgr = p.sb("lgr", (128, D), F32, offset=fr); fr += 4096
    escr = p.sb("escr", (128, D), F32, offset=fr); fr += 4096
    x1m = p.sb("x1m", (128, 2 * 8 * 512), BF16, offset=fr); fr += 16384
    mpe8 = p.sb("mpe8", (128, 8 * 512), BF16, offset=fr); fr += 8192
    rcol = p.sb("rcol", (128, 16), F32, offset=fr); fr += 64
    me8 = p.sb("me8", (128, 8), F32, offset=fr); fr += 32
    mp8 = p.sb("mp8", (128, 8), BF16, offset=fr); fr += 32
    xb_t = p.sb("xb_t", (128, D), BF16, offset=fr); fr += 2048
    ffnb2 = p.sb("ffnb2", (128, 2 * D), BF16, offset=fr); fr += 4096
    x1t2 = p.sb("x1t2", (128, 2 * D), F32, offset=fr); fr += 8192
    sq2 = p.sb("sq2", (128, 2 * D), F32, offset=fr); fr += 8192
    lncol2 = p.sb("lncol2", (128, 16), F32, offset=fr); fr += 64
    be_b = p.sb("be_b", (8, D), BF16, offset=CL_W1S4)
    assert fr <= s_off, (fr, s_off)
    mpbuf = cmbrow

    w8(nc.gpsimd, tok(PE), tok(DV), tok(AC), tok(PL))
    w8(nc.sync, tok(PE), tok(DV), tok(AC), tok(PL))
    for sp in ("x1sp",):
        p.wait_val(nc.gpsimd, sp, p.semv[sp])
        p.wait_val(nc.sync, sp, p.semv[sp])
    g3 = lambda dst, src, sem: p.inc(nc.gpsimd.dma_start(out=dst, in_=src), sem, DMA)
    g3(Wg_b[:].rearrange("p (k co) -> p k co", k=8),
       din["Wg"].rearrange("(k p) co -> p k co", p=128), "ldf")
    g3(iota_t[:], din["iota_row"], "ldf")
    g3(be_b[:], din["be"], "ldf")
    bgrow_b = p.sb("bgrow_b", (1, D), BF16, offset=CL_W1S4 + 2048)
    g3(bgrow_b[:], din["bgrow"], "ldf")
    ldf = tok("ldf")

    g3(Web[0][:].rearrange("p (k f) -> p k f", k=8),
       din["We"][0].rearrange("(k p) f -> p k f", p=128), "we0")
    we_tok = {0: ("we0", p.semv["we0"])}
    mpv = [cmbrow, rstdrow, stats_allA, stats_allB]
    mp_all = [None] * 4
    x1c_toks = [None] * NT
    st["ffnb_free"] = {0: None, 1: None}

    def router_block(n):
        t0 = n * 512
        mpdst = mpv[n]
        for ti in range(4):
            tt0 = t0 + ti * 128
            banks = []
            for half in range(2):
                bi, bank, btok = pp.acquire()
                w8(nc.tensor, btok, x1c_toks[4 * n + ti], ldf)
                for k in range(8):
                    mm = nc.tensor.matmul(bank,
                                          x1_ch[:, k * T + tt0:k * T + tt0 + 128],
                                          Wg_b[:, k * D + half * 512:k * D + (half + 1) * 512],
                                          start=(k == 0), stop=False)
                mm = nc.tensor.matmul(bank, ones_row[:, 0:128],
                                      bgrow_b[:, half * 512:(half + 1) * 512],
                                      start=False, stop=True)
                p.inc(mm, PE)
                banks.append((bi, bank, tok(PE)))
            for half, (bi, bank, pet) in enumerate(banks):
                w8(nc.vector, pet)
                i = nc.vector.tensor_copy(lgr[:, half * 512:(half + 1) * 512], bank)
                p.inc(i, DV)
                pp.release(bi, tok(DV))
            i = nc.vector.reduce_max(rcol[:, 0:1], lgr[:], axis=AX.X, negate=True)
            p.inc(i, DV)
            w8(nc.scalar, tok(DV))
            i = nc.scalar.activation(escr[:], lgr[:], AF.Exp, bias=rcol[:, 0:1],
                                     accum_out=rcol[:, 1:2])
            p.inc(i, AC)
            w8(nc.vector, tok(AC), ldf)
            i = nc.vector.reciprocal(rcol[:, 2:3], rcol[:, 1:2])
            i = nc.vector.tensor_scalar(lgr[:], lgr[:], rcol[:, 0:1], -1e12,
                                        AL.add, AL.mult)
            i = nc.vector.tensor_tensor(lgr[:], lgr[:], iota_t[:], AL.add)
            i = nc.vector.tensor_reduce(me8[:].rearrange("p (e a) -> p e a", a=1),
                                        lgr[:].rearrange("p (a e) -> p e a", e=E),
                                        axis=AX.X, op=AL.min)
            i = nc.vector.tensor_reduce(rcol[:, 3:4], me8[:], axis=AX.X, op=AL.min)
            i = nc.vector.tensor_scalar(me8[:], me8[:], rcol[:, 3:4], None, AL.subtract)
            i = nc.vector.tensor_scalar_min(me8[:], me8[:], 1.0)
            i = nc.vector.tensor_scalar(me8[:], me8[:], -1.0, 1.0, AL.mult, AL.add)
            i = nc.vector.tensor_scalar(mp8[:], me8[:], rcol[:, 2:3], None, AL.mult)
            p.inc(i, DV)
            mp_tok = tok(DV)
            bi, bank, btok = pp.acquire(cols=128)
            w8(nc.tensor, btok, mp_tok)
            mm = nc.tensor.matmul(bank[0:8, :], mp8[:], ident_b[:], start=True, stop=True)
            p.inc(mm, PE)
            w8(nc.vector, tok(PE))
            i = nc.vector.tensor_copy(mpdst[0:8, ti * 128:(ti + 1) * 128], bank[0:8, :])
            p.inc(i, DV)
            pp.release(bi, tok(DV))
        mp_all[n] = tok(DV)

    for ti in range(NT):
        sl = ti % 2
        w8(nc.sync, st["x1t_free"].get(sl))
        p.inc(nc.sync.dma_start(out=x1t2[:, sl * D:(sl + 1) * D],
                                in_=p.d_x1[ti * 128:(ti + 1) * 128, :]),
              f"xf{sl}", DMA)
        xf = (f"xf{sl}", p.semv[f"xf{sl}"])
        w8(nc.vector, xf, tok(PE))
        i = nc.vector.tensor_copy(xb_t[:], x1t2[:, sl * D:(sl + 1) * D])
        p.inc(i, DV)
        cast_tok = tok(DV)
        for jp in (0, 4):
            bi, bank, btok = pp.acquire()
            w8(nc.tensor, btok, cast_tok)
            for j in range(jp, jp + 4):
                mm = nc.tensor.matmul(bank[:, (j - jp) * 128:(j - jp + 1) * 128],
                                      xb_t[:, j * 128:(j + 1) * 128],
                                      ident_b[:], start=True, stop=True)
            p.inc(mm, PE)
            w8(nc.scalar, tok(PE))
            for j in range(jp, jp + 4):
                i = nc.scalar.copy(x1_ch[:, j * T + ti * 128:j * T + (ti + 1) * 128],
                                   bank[:, (j - jp) * 128:(j - jp + 1) * 128])
            p.inc(i, AC)
            pp.release(bi, tok(AC))
        x1c_toks[ti] = tok(AC)
        st["x1t_free"][sl] = cast_tok
        if ti % 4 == 3:
            router_block(ti // 4)

    def ln2_group(n, ffn_dep):
        x1s = lambda sl: x1t2[:, sl * D:(sl + 1) * D]
        for g in range(0, 4, 2):
            tis = (4 * n + g, 4 * n + g + 1)
            xfs = {}
            for t in tis:
                sl = t % 2
                w8(nc.sync, st["x1t_free"].get(sl))
                p.inc(nc.sync.dma_start(out=x1s(sl),
                                        in_=p.d_x1[t * 128:(t + 1) * 128, :]),
                      f"xf{sl}", DMA)
                xfs[sl] = (f"xf{sl}", p.semv[f"xf{sl}"])
            fbt = {}
            for t in tis:
                sl = t % 2
                for jp in (0, 4):
                    bi, bank, btok = pp.acquire()
                    w8(nc.tensor, btok, ffn_dep)
                    for j in range(jp, jp + 4):
                        mm = nc.tensor.matmul(bank[:, (j - jp) * 128:(j - jp + 1) * 128],
                                              ffn_ch[:, j * T + t * 128:j * T + (t + 1) * 128],
                                              ident_b[:], start=True, stop=True)
                    p.inc(mm, PE)
                    w8(nc.scalar, tok(PE), st["ffnb_free"].get(sl))
                    for j in range(jp, jp + 4):
                        i = nc.scalar.copy(ffnb2[:, sl * D + j * 128:sl * D + (j + 1) * 128],
                                           bank[:, (j - jp) * 128:(j - jp + 1) * 128])
                    p.inc(i, AC)
                    pp.release(bi, tok(AC))
                fbt[sl] = tok(AC)
            add_t = {}
            for t in tis:
                sl = t % 2
                w8(nc.vector, fbt[sl], xfs[sl])
                i = nc.vector.tensor_tensor(x1s(sl), x1s(sl),
                                            ffnb2[:, sl * D:(sl + 1) * D], AL.add)
                p.inc(i, DV)
                add_t[sl] = tok(DV)
                st["ffnb_free"][sl] = add_t[sl]
            for t in tis:
                sl = t % 2
                i = nc.vector.reduce_sum(lncol2[:, sl * 8:sl * 8 + 1], x1s(sl), axis=AX.X)
                p.inc(i, DV)
            sq_t = {}
            for t in tis:
                sl = t % 2
                w8(nc.scalar, add_t[sl])
                i = nc.scalar.activation(sq2[:, sl * D:(sl + 1) * D], x1s(sl),
                                         AF.Square, accum_out=lncol2[:, sl * 8 + 1:sl * 8 + 2])
                p.inc(i, AC)
                sq_t[sl] = tok(AC)
            for t in tis:
                sl = t % 2
                c = lncol2[:, sl * 8:sl * 8 + 8]
                w8(nc.vector, sq_t[sl])
                i = nc.vector.tensor_scalar(c[:, 2:3], c[:, 0:1], 1.0 / D, None, AL.mult)
                i = nc.vector.tensor_tensor(c[:, 3:4], c[:, 2:3], c[:, 2:3], AL.mult)
                i = nc.vector.tensor_scalar(c[:, 4:5], c[:, 1:2], 1.0 / D, None, AL.mult)
                i = nc.vector.tensor_tensor(c[:, 4:5], c[:, 4:5], c[:, 3:4], AL.subtract)
                p.inc(i, DV)
            ln_t = {}
            for t in tis:
                sl = t % 2
                c = lncol2[:, sl * 8:sl * 8 + 8]
                w8(nc.scalar, tok(DV))
                i = nc.scalar.activation(c[:, 5:6], c[:, 4:5], AF.Ln, bias=epscol[:, 0:1])
                i = nc.scalar.activation(c[:, 6:7], c[:, 5:6], AF.Exp, scale=-0.5)
                p.inc(i, AC)
                ln_t[sl] = tok(AC)
            for t in tis:
                sl = t % 2
                c = lncol2[:, sl * 8:sl * 8 + 8]
                w8(nc.vector, ln_t[sl])
                i = nc.vector.tensor_scalar(x1s(sl), x1s(sl), c[:, 2:3], c[:, 6:7],
                                            AL.subtract, AL.mult)
                p.inc(i, DV)
                y_tok = tok(DV)
                w8(nc.sync, y_tok)
                p.inc(nc.sync.dma_start(out=p.y[t * 128:(t + 1) * 128, :], in_=x1s(sl)),
                      f"yst{sl}", DMA)
                st["x1t_free"][sl] = (f"yst{sl}", p.semv[f"yst{sl}"])

    prev_mm = tok(PE)
    mm_tok2 = [prev_mm, prev_mm]
    for n in range(4):
        t0 = n * 512
        for e in range(E):
            bi, bank, btok = pp.acquire()
            w8(nc.tensor, btok, mp_all[n])
            mm = nc.tensor.matmul(bank, selrows_b[0:8, e * 128:(e + 1) * 128],
                                  mpv[n][0:8, :], start=True, stop=True)
            p.inc(mm, PE)
            w8(nc.vector, tok(PE))
            i = nc.vector.tensor_copy(mpe8[:, e * 512:(e + 1) * 512], bank)
            p.inc(i, DV)
            pp.release(bi, tok(DV))
        mpe_tok = tok(DV)
        ebanks = []
        for f in range(8):
            bi, bank, btok = pp.acquire()
            ebanks.append([bi, bank, btok])
        for e in range(E):
            nxt = (e + 1) % E
            w8(nc.gpsimd, prev_mm)
            g3(Web[(e + 1) % 2][:].rearrange("p (k f) -> p k f", k=8),
               din["We"][nxt].rearrange("(k p) f -> p k f", p=128),
               f"we{(e + 1) % 2}")
            we_tok[(e + 1) % 2] = (f"we{(e + 1) % 2}", p.semv[f"we{(e + 1) % 2}"])
            xs = (e % 2) * 8 * 512
            w8(nc.vector, mpe_tok, x1c_toks[-1], mm_tok2[e % 2])
            for k in range(8):
                i = nc.vector.tensor_tensor(x1m[:, xs + k * 512:xs + (k + 1) * 512],
                                            x1_ch[:, k * T + t0:k * T + t0 + 512],
                                            mpe8[:, e * 512:(e + 1) * 512], AL.mult)
            p.inc(i, DV)
            xm_tok = tok(DV)
            for f in range(8):
                bi, bank, btok = ebanks[f]
                w8(nc.tensor, btok, xm_tok, we_tok[e % 2])
                for k in range(8):
                    mm = nc.tensor.matmul(bank,
                                          Web[e % 2][:, (k * 8 + f) * 128:(k * 8 + f + 1) * 128],
                                          x1m[:, xs + k * 512:xs + (k + 1) * 512],
                                          start=(e == 0 and k == 0), stop=False)
                ebanks[f] = [bi, bank, None]
            p.inc(mm, PE)
            prev_mm = tok(PE)
            mm_tok2[e % 2] = prev_mm
        for f in range(8):
            bi, bank, _ = ebanks[f]
            w8(nc.tensor, mp_all[n], ldf)
            mm = nc.tensor.matmul(bank, be_b[0:8, f * 128:(f + 1) * 128],
                                  mpv[n][0:8, :], start=False, stop=True)
            p.inc(mm, PE)
            w8(nc.scalar, tok(PE))
            i = nc.scalar.activation(ffn_ch[:, f * T + t0:f * T + t0 + 512], bank,
                                     AF.Relu)
            p.inc(i, AC)
            pp.release(bi, tok(AC))
        ln2_group(n, tok(AC))
    p.dump("ffn0", ffn_ch[:, 0:512], (128, 512), BF16)
    for sl in (0, 1):
        if f"yst{sl}" in p.sems:
            p.wait_val(nc.sync, f"yst{sl}", p.semv[f"yst{sl}"])
    p.finish()


def build_program(debug_names=(), stop_after=None):
    p = Prog(debug_names)
    build(p, stop_after=stop_after)
    return p


def kernel(**inputs):
    in_maps = host_prep(inputs)
    p = build_program()
    res = run_bass_kernel_spmd(p.nc, in_maps, core_ids=list(range(8)))
    out = np.zeros((4, 4096, 1024), np.float32)
    for c in range(8):
        out[c // 2, (c % 2) * T:(c % 2 + 1) * T] = res.results[c]["y"]
    return out
